# version 32
# baseline (speedup 1.0000x reference)
import os
import numpy as np
from contextlib import ExitStack

KLIM = float(os.environ.get("KLIM", "99"))
KSUB = float(os.environ.get("KSUB", "99"))
MMSAFE = int(os.environ.get("MMSAFE", "2"))
DR_S = int(os.environ.get("DR_S", "1"))
DR_V = int(os.environ.get("DR_V", "1"))

import concourse.bass as bass
import concourse.mybir as mybir
from concourse.bass_utils import run_bass_kernel_spmd

F32 = mybir.dt.float32
BF16 = mybir.dt.bfloat16
AF = mybir.ActivationFunctionType
ALU = mybir.AluOpType
AX = mybir.AxisListType

LN_EPS = 1e-5
NEG_BIG = -1.0e30


class Cfg:
    def __init__(self, DM=1024, H=16, DF=4096, TLOC=2048, BLK=512, NCORE=8):
        self.DM, self.H, self.HD, self.DF = DM, H, 64, DF
        assert H * 64 == DM
        self.TLOC, self.L, self.BLK = TLOC, 128, BLK
        self.NBLK = TLOC // BLK
        self.NC = TLOC // 128
        self.CPB = BLK // 128
        self.FC = DM // 128
        self.HC = DF // 128
        self.NT = TLOC // 128
        self.HP = H // 2
        self.NCORE = NCORE
        self.FBLK = min(512, TLOC)
        self.NFB = TLOC // self.FBLK


def build_nc(cfg: Cfg, groups):
    nc = bass.Bass("TRN2", target_bir_lowering=False)
    DM, H, DF, TLOC = cfg.DM, cfg.H, cfg.DF, cfg.TLOC
    L, NC, BLK, NBLK, CPB = cfg.L, cfg.NC, cfg.BLK, cfg.NBLK, cfg.CPB
    FC, HC, NT, HP = cfg.FC, cfg.HC, cfg.NT, cfg.HP
    FBLK, NFB = cfg.FBLK, cfg.NFB
    SL = 65
    NSL = CPB + 1
    NBRAW = max(1, HP // 4)
    NBDEN = max(1, H // 8)
    PPB = HP // 2

    dt = nc.dram_tensor
    x_in = dt("x", [TLOC, DM], F32, kind="ExternalInput").ap()
    Wq = dt("Wq", [DM, DM], F32, kind="ExternalInput").ap()
    Wk = dt("Wk", [DM, DM], F32, kind="ExternalInput").ap()
    Wv = dt("Wv", [DM, DM], F32, kind="ExternalInput").ap()
    Wo = dt("Wo", [DM, DM], F32, kind="ExternalInput").ap()
    Wif_in = dt("Wif", [DM, 32], F32, kind="ExternalInput").ap()
    b_if = dt("b_if", [16, 2], F32, kind="ExternalInput").ap()
    b_o_c = dt("b_o_c", [128, FC], F32, kind="ExternalInput").ap()
    gamma_r = dt("gamma_r", [128, DM], F32, kind="ExternalInput").ap()
    beta_r = dt("beta_r", [128, DM], F32, kind="ExternalInput").ap()
    w1T_in = dt("w1T", [DM, DF], F32, kind="ExternalInput").ap()
    w1h_in = dt("w1h", [DF, DM], F32, kind="ExternalInput").ap()
    w2T_in = dt("w2T", [DF, DM], F32, kind="ExternalInput").ap()
    w2h_in = dt("w2h", [DM, DF], F32, kind="ExternalInput").ap()
    b1_c = dt("b1_c", [128, HC], F32, kind="ExternalInput").ap()
    b2_c = dt("b2_c", [128, FC], F32, kind="ExternalInput").ap()
    ident = dt("ident", [128, 128], BF16, kind="ExternalInput").ap()
    maskT_in = dt("maskT", [128, 128], BF16, kind="ExternalInput").ap()
    isr_in = dt("isr", [128, 1], F32, kind="ExternalInput").ap()
    y_out = dt("y", [TLOC, DM], F32, kind="ExternalOutput").ap()

    NROW = 10 * NC + 2 * NBLK * NSL
    s1sc = dt("s1sc", [DM, DF], BF16, kind="Internal")
    s2sc = dt("s2sc", [DF, DM], BF16, kind="Internal")
    rowsc = dt("rowsc", [16, NROW], F32, kind="Internal")
    gsc = dt("gsc", [16, TLOC], BF16, kind="Internal")
    prow = dt("prow", [TLOC, 16], F32, kind="Internal")
    cm_in = dt("cm_in", [16, 1], F32, kind="Internal")
    cm_out = dt("cm_out", [16, 1], F32, kind="Internal")
    cs_in = dt("cs_in", [128, HP * SL], F32, kind="Internal")
    cs_out = dt("cs_out", [128, HP * SL], F32, kind="Internal")
    Qhbm = dt("Qhbm", [128, FC * TLOC], BF16, kind="Internal")
    Ohbm = dt("Ohbm", [128, FC * TLOC], BF16, kind="Internal")
    N0hbm = dt("N0hbm", [128, HP * TLOC], BF16, kind="Internal")

    with ExitStack() as ctx:
        def sbt(name, shape, dtyp):
            return ctx.enter_context(nc.sbuf_tensor(name, shape, dtyp))

        def pst(name, shape, dtyp):
            return ctx.enter_context(nc.psum_tensor(name, shape, dtyp))

        xres = sbt("xres", [128, NT * DM], BF16)
        xT = sbt("xT", [128, FC * BLK], BF16)
        SHR = sbt("SHR", [128, max(FC * BLK, 2 * H * L, CPB * H * 64)], BF16)
        Qwin = sbt("Qwin", [128, FC * BLK], BF16)
        N0stg = sbt("N0stg", [128, HP * 128], BF16)
        Kfm = sbt("Kfm", [128, FC * BLK], BF16)
        Vfm = sbt("Vfm", [128, FC * BLK], BF16)
        KPT = sbt("KPT", [128, max(CPB * DM, 4 * DM)], BF16)
        VTp = SHR
        BvB = SHR[:, 0:H * L]
        BkB = SHR[:, H * L:2 * H * L]
        Wwin = sbt("Wwin", [128, 2 * FC * 128], BF16)
        rawZ = sbt("rawZ", [128, HP * SL * NSL], BF16)
        scanZ = sbt("scanZ", [128, HP * SL * NSL], BF16)
        scanS = sbt("scanS", [128, HP * SL * CPB], BF16)
        sXcC = sbt("sXcC", [128, HP * SL], BF16)
        gI = sbt("gI", [16, TLOC], F32)
        gF = sbt("gF", [16, TLOC], F32)
        gB = sbt("gB", [16, TLOC], F32)
        gMi = sbt("gMi", [16, TLOC], F32)
        gU = gI
        gM = gF
        GrowT = sbt("GrowT", [16, TLOC], BF16)
        gSc = sbt("gSc", [16, NROW], F32)
        gTf = sbt("gTf", [128, NC * 16], F32)
        gTb = sbt("gTb", [128, NC * 16], BF16)
        arep = sbt("arep", [128, SL * NSL], F32)
        areptmp = sbt("areptmp", [128, HP * NSL], F32)
        rrep = sbt("rrep", [128, HP * NC], F32)
        rhorep = sbt("rhorep", [128, HP * NC], F32)
        cpxrep = sbt("cpxrep", [128, HP * NC], F32)
        Grep = sbt("Grep", [128, TLOC], BF16)
        dslab = sbt("dslab", [128, 1024], BF16)
        sId = sbt("sId", [128, 128], BF16)
        sMk = sbt("sMk", [128, 128], BF16)
        sOne = sbt("sOne", [128, 1], BF16)
        sIsr = sbt("sIsr", [128, 1], F32)
        sBif = sbt("sBif", [16, 2], F32)
        sBo = sbt("sBo", [128, FC], F32)
        sGam = sbt("sGam", [128, DM], BF16)
        sBet = sbt("sBet", [128, DM], BF16)
        sA1 = sbt("sA1", [128, HC], F32)
        sA2 = sbt("sA2", [128, FC], F32)
        sB1 = sbt("sB1", [128, HC], F32)
        sB2 = sbt("sB2", [128, FC], F32)
        sMu = sbt("sMu", [128, 6 * NT + 40], F32)
        sIdF = sbt("sIdF", [16, 16], F32)
        sSend = sbt("sSend", [128, HP * SL], F32)
        sXsum = sbt("sXsum", [128, HP * SL], F32)
        fscr = sbt("fscr", [128, 2 * DM], F32)
        fscr2 = sbt("fscr2", [128, max(DF // 2, CPB * DM, TLOC, (FBLK // 128) * DM)], BF16)
        g1 = sbt("g1", [128, max(HC * FBLK, NT * DM, 14 * DM)], BF16)
        x2T = xT if FC * BLK >= FC * FBLK else sbt("x2T", [128, FC * FBLK], BF16)
        f2win = Vfm if FC * BLK >= HC * 128 else sbt("f2win", [128, HC * 128], BF16)
        ffnO = Kfm if FC * BLK >= max(FC * FBLK, (FBLK // 128) * DM) else sbt("ffnO", [128, max(FC * FBLK, (FBLK // 128) * DM)], BF16)

        nc.dbg = dict(gI=gI[:], gF=gF[:], gB=gB[:], gMi=gMi[:], gSc=gSc[:],
                      Wwin=Wwin[:], xT=xT[:], fscr=fscr[:], fscr2=fscr2[:],
                      sBif=sBif[:])

        PFa = pst("PFa", [128, 512], F32)
        PFb = pst("PFb", [128, 512], F32)
        PFc = pst("PFc", [128, 512], F32)
        PFd = pst("PFd", [128, 512], F32)
        PBa = pst("PBa", [128, 1024], BF16)
        PBb = pst("PBb", [128, 1024], BF16)
        PXa = pst("PXa", [128, 512], F32)
        PXb = pst("PXb", [128, 512], F32)
        PF = [PFa, PFb, PFc, PFd]
        PB = [PBa, PBb]
        PX = [PXa, PXb]

        class Ctr:
            def __init__(self, name, step=1):
                self.sem = ctx.enter_context(nc.semaphore(name))
                self.val = 0
                self.step = step

            def inc(self, bi, n=None):
                n = self.step if n is None else n
                self.val += n
                return bi.then_inc(self.sem, n)

        dfin = gMi
        den0 = gB.bitcast(BF16)[:, 0:TLOC]
        denXr = gB.bitcast(BF16)[:, TLOC:2 * TLOC]
        Grow = GrowT
        ld = Ctr("ld", 16)
        ldw = Ctr("ldw", 16)
        gd = Ctr("gd", 16)
        cc = Ctr("cc", 1)
        tC = Ctr("tC", 1)
        vC = Ctr("vC", 1)
        sC = Ctr("sC", 1)
        bT = Ctr("bT", 1)
        bV = Ctr("bV", 1)
        bS = Ctr("bS", 1)
        st = Ctr("st", 16)

        block = ctx.enter_context(nc.Block())

        def T(fn):
            @block.tensor
            def _(t):
                fn(t)

        def V(fn):
            @block.vector
            def _(v):
                fn(v)

        def S(fn):
            @block.scalar
            def _(s):
                fn(s)

        def Y(fn):
            @block.sync
            def _(s):
                fn(s)

        def G(fn):
            @block.gpsimd
            def _(g):
                fn(g)

        mmrb = [None]

        def mmn(t, rb):
            if MMSAFE == 1:
                t.drain()
                mmrb[0] = rb
                return
            if rb is None:
                mmrb[0] = None
                return
            if mmrb[0] is not None and mmrb[0] != rb:
                t.drain() if MMSAFE == 2 else t.nop()
            mmrb[0] = rb

        def barrier():
            tv, vv, sv = bT.val + 1, bV.val + 1, bS.val + 1

            def tf(t):
                t.drain()
                bT.inc(t.nop())
                t.wait_ge(bV.sem, vv)
                t.wait_ge(bS.sem, sv)

            def vf(v):
                DR_V and v.drain()
                bV.inc(v.engine_nop())
                v.wait_ge(bT.sem, tv)
                v.wait_ge(bS.sem, sv)

            def sf(s):
                DR_S and s.drain()
                bS.inc(s.nop())
                s.wait_ge(bT.sem, tv)
                s.wait_ge(bV.sem, vv)
            T(tf)
            V(vf)
            S(sf)

        def loads(s):
            for ap_, src in [(sId[:], ident), (sMk[:], maskT_in), (sIsr[:], isr_in),
                             (sBif[:], b_if), (sBo[:], b_o_c),
                             (sB1[:], b1_c), (sB2[:], b2_c)]:
                ld.inc(s.dma_start(out=ap_, in_=src))
        Y(loads)

        def loadx(g):
            for j in range(NT):
                gd.inc(g.dma_start(out=xres[:, j * DM:(j + 1) * DM],
                                   in_=x_in[j * 128:(j + 1) * 128, :]))
            g.wait_ge(gd.sem, gd.val)
        G(loadx)
        LDA0 = ld.val
        LDX = gd.val

        def gbl(s):
            ld.inc(s.dma_start(out=fscr[:, DM:2 * DM], in_=gamma_r))
        Y(gbl)
        LDG = ld.val

        def gbc(v):
            v.wait_ge(ld.sem, LDG)
            v.tensor_copy(sGam[:, 0:DM], fscr[:, DM:2 * DM])
            DR_V and v.drain()
            vC.inc(v.engine_nop())
        V(gbc)
        VGB = vC.val

        def gbl2(s):
            s.wait_ge(vC.sem, VGB)
            ld.inc(s.dma_start(out=fscr[:, DM:2 * DM], in_=beta_r))
        Y(gbl2)
        LDB = ld.val

        def gbc2(v):
            v.wait_ge(ld.sem, LDB)
            v.tensor_copy(sBet[:, 0:DM], fscr[:, DM:2 * DM])
            DR_V and v.drain()
            vC.inc(v.engine_nop())
        V(gbc2)
        LDA = ld.val
        VGB2 = vC.val

        def init_consts(v):
            v.wait_ge(ld.sem, LDG)
            v.tensor_copy(sIdF[:], sId[0:16, 0:16])
            DR_V and v.drain()
            v.memset(sOne[:], 1.0)
            v.memset(gI[:], 0.0)
            v.memset(gF[:], 0.0)
            v.memset(gB[:], 0.0)
            v.memset(gMi[:], 0.0)
            v.memset(fscr[:, 0:512], 0.0)
            v.memset(fscr2[:, 0:1024], 0.0)
            DR_V and v.drain()
            for pb_ in (PFa, PFb, PFc, PFd, PXa, PXb):
                v.tensor_copy(pb_[:], fscr[:, 0:512])
            v.memset(gSc[:], 0.0)
            v.memset(rawZ[:], 0.0)
            v.memset(scanZ[:], 0.0)
            DR_V and v.drain()
            vC.inc(v.engine_nop())
        V(init_consts)
        VIC = vC.val

        def wifload(s):
            s.wait_ge(vC.sem, VIC)
            for fc in range(FC):
                ld.inc(s.dma_start(out=fscr[:, fc * 32:(fc + 1) * 32],
                                   in_=Wif_in[fc * 128:(fc + 1) * 128, :]))
        Y(wifload)
        LDWIF = ld.val

        def init_pb(t):
            t.wait_ge(vC.sem, VIC)
            t.wait_ge(ld.sem, LDG)
            for k in range(8):
                t.transpose(PBa[:, k * 128:(k + 1) * 128], fscr2[:, 0:128], sId[:])
                t.transpose(PBb[:, k * 128:(k + 1) * 128], fscr2[:, 0:128], sId[:])
            t.drain()
            tC.inc(t.nop())
        T(init_pb)

        def ln1_sums(v):
            v.wait_ge(ld.sem, LDG)
            v.wait_ge(gd.sem, LDX)
            for j in range(NT):
                v.reduce_sum(sMu[:, j:j + 1], xres[:, j * DM:(j + 1) * DM], axis=AX.X)
            DR_V and v.drain()
            v.tensor_scalar_mul(sMu[:, 0:NT], sMu[:, 0:NT], 1.0 / DM)
            DR_V and v.drain()
            vC.inc(v.engine_nop())
        V(ln1_sums)

        def ln1_sq(s):
            s.wait_ge(ld.sem, LDA)
            s.wait_ge(gd.sem, LDX)
            s.wait_ge(vC.sem, VGB2)
            for j in range(NT):
                s.activation(fscr[:, DM:2 * DM], xres[:, j * DM:(j + 1) * DM], AF.Square,
                             accum_out=sMu[:, NT + j:NT + j + 1])
                DR_S and s.drain()
            sC.inc(s.nop())
        S(ln1_sq)
        SQ1 = sC.val

        def ln1_var(v):
            v.wait_ge(sC.sem, SQ1)
            mu, ex2 = sMu[:, 0:NT], sMu[:, NT:2 * NT]
            var = sMu[:, 4 * NT:5 * NT]
            v.tensor_scalar_mul(ex2, ex2, 1.0 / DM)
            DR_V and v.drain()
            v.tensor_tensor(var, mu, mu, ALU.mult)
            DR_V and v.drain()
            v.tensor_sub(var, ex2, var)
            DR_V and v.drain()
            v.tensor_scalar_add(var, var, LN_EPS)
            DR_V and v.drain()
            vC.inc(v.engine_nop())
        V(ln1_var)
        VV1 = vC.val

        def ln1_rs(s):
            s.wait_ge(vC.sem, VV1)
            var, rs = sMu[:, 4 * NT:5 * NT], sMu[:, 3 * NT:4 * NT]
            s.activation(rs, var, AF.Ln)
            DR_S and s.drain()
            s.activation(rs, rs, AF.Exp, scale=-0.5)
            DR_S and s.drain()
            sC.inc(s.nop())
        S(ln1_rs)
        SQ2 = sC.val

        def ln1_bias(v):
            v.wait_ge(sC.sem, SQ2)
            bias = sMu[:, 2 * NT:3 * NT]
            v.tensor_tensor(bias, sMu[:, 0:NT], sMu[:, 3 * NT:4 * NT], ALU.mult)
            DR_V and v.drain()
            v.tensor_scalar_mul(bias, bias, -1.0)
            DR_V and v.drain()
            vC.inc(v.engine_nop())
        V(ln1_bias)

        def ln1_apply_block(s, b):
            for jj in range(CPB):
                j = b * CPB + jj
                s.activation(fscr2[:, jj * DM:(jj + 1) * DM],
                             xres[:, j * DM:(j + 1) * DM], AF.Identity,
                             bias=sMu[:, 2 * NT + j:2 * NT + j + 1],
                             scale=sMu[:, 3 * NT + j:3 * NT + j + 1])

        def xpose_block_emit(pb_free=None):
            n = CPB * FC
            tv0, vv0 = tC.val, vC.val
            if pb_free is None:
                pb_free = vv0

            def tfn(t, pb_free=pb_free):
                t.wait_ge(sC.sem, sC.val)
                for i in range(n):
                    jj, fc = divmod(i, FC)
                    if i >= 2:
                        t.wait_ge(vC.sem, vv0 + i - 1)
                    else:
                        t.wait_ge(vC.sem, pb_free)
                    tC.inc(t.transpose(PB[i % 2][:, 0:128],
                                fscr2[:, jj * DM + fc * 128:jj * DM + fc * 128 + 128],
                                sId[:]))

            def vfn(v):
                for i in range(n):
                    jj, fc = divmod(i, FC)
                    v.wait_ge(tC.sem, tv0 + i + 1)
                    vC.inc(v.tensor_copy(
                        xT[:, fc * BLK + jj * 128:fc * BLK + jj * 128 + 128],
                        PB[i % 2][:, 0:128]))
            T(tfn)
            V(vfn)

        if KLIM < 1:
            def tail1(s):
                for j in range(NT):
                    st.inc(s.dma_start(out=y_out[j * 128:(j + 1) * 128, :],
                                       in_=fscr[:, 0:DM]))
                s.wait_ge(st.sem, st.val)
            Y(tail1)

            def tailg1(g):
                g.wait_ge(st.sem, st.val)
            G(tailg1)
            return nc
        def wif_cast(v):
            v.wait_ge(ld.sem, LDWIF)
            v.tensor_copy(Wwin[:, 0:FC * 32], fscr[:, 0:FC * 32])
            DR_V and v.drain()
            vC.inc(v.engine_nop())
        V(wif_cast)

        for b in range(NBLK):
            tb = b * BLK

            def sA_(s, b=b):
                s.wait_ge(vC.sem, vC.val)
                s.wait_ge(tC.sem, tC.val)
                ln1_apply_block(s, b)
                DR_S and s.drain()
                sC.inc(s.nop())
            S(sA_)
            xpose_block_emit()

            def tif(t, b=b):
                t.wait_ge(vC.sem, vC.val)
                wid = min(BLK, 512)
                for q in range(BLK // wid):
                    for fc in range(FC):
                        t.matmul(PXa[0:16, q * wid:q * wid + wid],
                                 Wwin[0:128, fc * 32:fc * 32 + 16],
                                 xT[:, fc * BLK + q * wid:fc * BLK + (q + 1) * wid],
                                 start=(fc == 0), stop=(fc == FC - 1))
                    for fc in range(FC):
                        t.matmul(PXb[0:16, q * wid:q * wid + wid],
                                 Wwin[0:128, fc * 32 + 16:fc * 32 + 32],
                                 xT[:, fc * BLK + q * wid:fc * BLK + (q + 1) * wid],
                                 start=(fc == 0), stop=(fc == FC - 1))
                t.drain()
                tC.inc(t.nop())
            T(tif)
            TIF = tC.val

            def vif(v, b=b, tb=tb, TIF=TIF):
                v.wait_ge(tC.sem, TIF)
                v.tensor_scalar_add(gI[:, tb:tb + BLK], PXa[0:16, 0:BLK], sBif[0:16, 0:1])
                DR_V and v.drain()
                v.tensor_scalar_add(gF[:, tb:tb + BLK], PXb[0:16, 0:BLK], sBif[0:16, 1:2])
                DR_V and v.drain()
                binit = 0.0 if b == 0 else gB[:, tb - 1:tb]
                v.tensor_tensor_scan(gB[:, tb:tb + BLK], gF[:, tb:tb + BLK],
                                     gSc[:, 0:1].to_broadcast([16, BLK]),
                                     binit, ALU.add, ALU.add)
                DR_V and v.drain()
                minit = NEG_BIG if b == 0 else gMi[:, tb - 1:tb]
                v.tensor_tensor_scan(gMi[:, tb:tb + BLK], gF[:, tb:tb + BLK],
                                     gI[:, tb:tb + BLK], minit, ALU.add, ALU.max)
                DR_V and v.drain()
                vC.inc(v.engine_nop())
            V(vif)

        if KLIM < 2:
            def tail2(s):
                s.wait_ge(vC.sem, vC.val)
                for j in range(NT):
                    st.inc(s.dma_start(out=y_out[j * 128:(j + 1) * 128, :],
                                       in_=fscr[:, 0:DM]))
                s.wait_ge(st.sem, st.val)
            Y(tail2)

            def tailg2(g):
                g.wait_ge(st.sem, st.val)
            G(tailg2)
            return nc
        S2END, V2END, T2END = sC.val, vC.val, tC.val
        def msend_prep(v):
            v.tensor_tensor(gSc[:, 2:3], gMi[:, TLOC - 1:TLOC], gB[:, TLOC - 1:TLOC], ALU.max)
            DR_V and v.drain()
            vC.inc(v.engine_nop())
        V(msend_prep)
        VMS = vC.val

        def msend(g):
            g.wait_ge(vC.sem, VMS)
            gd.inc(g.dma_start(out=cm_in[:, :], in_=gSc[:, 2:3]))
            g.wait_ge(gd.sem, gd.val)
            cc.inc(g.collective_compute(
                "AllReduce", ALU.add, replica_groups=groups,
                ins=[cm_in.ap().opt()], outs=[cm_out.ap().opt()]))
            g.wait_ge(cc.sem, cc.val)
            gd.inc(g.dma_start(out=gSc[:, 3:4], in_=cm_out[:, :]))
            g.wait_ge(gd.sem, gd.val)
        G(msend)
        GD_M = gd.val

        if KLIM < 2.5:
            def tail25(s):
                for j in range(NT):
                    st.inc(s.dma_start(out=y_out[j * 128:(j + 1) * 128, :],
                                       in_=fscr[:, 0:DM]))
                s.wait_ge(st.sem, st.val)
            Y(tail25)

            def tailg25(g):
                g.wait_ge(st.sem, st.val)
            G(tailg25)
            return nc
        BP, MI, MB = 8, 8 + NC, 8 + 2 * NC
        EB = 8 + 3 * NC
        r_, a_, rho_, cpx_ = EB, EB + NC, EB + 2 * NC, EB + 3 * NC
        S1_ = EB + 4 * NC
        AB = S1_ + 2 * NC

        def cview(rowap):
            return rowap.rearrange("p (c o) -> p c o", o=1).to_broadcast([16, NC, L])

        def lview(rowap):
            return rowap.rearrange("p (c l) -> p c l", l=L)

        def gatefin(v):
            v.wait_ge(gd.sem, GD_M)
            v.tensor_sub(gSc[:, 4:5], gSc[:, 3:4], gSc[:, 2:3])
            DR_V and v.drain()
            v.tensor_tensor(gSc[:, 4:5], gSc[:, 4:5], sIsr[0:16, 0:1], ALU.mult)
            DR_V and v.drain()
            v.tensor_scalar_add(gM[:, 0:TLOC], gB[:, 0:TLOC], gSc[:, 4:5])
            DR_V and v.drain()
            v.tensor_tensor(gMi[:, 0:TLOC], gMi[:, 0:TLOC], gM[:, 0:TLOC], ALU.max)
            DR_V and v.drain()
            v.memset(gSc[:, BP:BP + 1], 0.0)
            DR_V and v.drain()
            if NC > 1:
                v.tensor_copy(gSc[:, BP + 1:BP + NC], gB[:, L - 1:TLOC - 1:L])
                DR_V and v.drain()
                v.tensor_copy(gSc[:, MI + 1:MI + NC], gMi[:, L - 1:TLOC - 1:L])
                DR_V and v.drain()
            v.tensor_copy(gSc[:, MI:MI + 1], gSc[:, 4:5])
            DR_V and v.drain()
            bl = gM
            v.tensor_tensor(lview(bl[:, 0:TLOC]), lview(gB[:, 0:TLOC]),
                            cview(gSc[:, BP:BP + NC]), ALU.subtract)
            DR_V and v.drain()
            v.tensor_sub(gU[:, 0:TLOC], gI[:, 0:TLOC], bl[:, 0:TLOC])
            DR_V and v.drain()
            v.tensor_scalar_mul(gM[:, 0:TLOC], gM[:, 0:TLOC], -1.0)
            DR_V and v.drain()
            v.tensor_add(gM[:, 0:TLOC], gM[:, 0:TLOC], gMi[:, 0:TLOC])
            DR_V and v.drain()
            v.reduce_max(gSc[:, MB:MB + NC], lview(gM[:, 0:TLOC]), axis=AX.X)
            DR_V and v.drain()
            v.tensor_tensor(lview(gU[:, 0:TLOC]), lview(gU[:, 0:TLOC]),
                            cview(gSc[:, MB:MB + NC]), ALU.subtract)
            DR_V and v.drain()
            v.tensor_scalar_mul(gM[:, 0:TLOC], gM[:, 0:TLOC], -1.0)
            DR_V and v.drain()
            v.tensor_tensor(lview(gM[:, 0:TLOC]), lview(gM[:, 0:TLOC]),
                            cview(gSc[:, MB:MB + NC]), ALU.add)
            DR_V and v.drain()
            v.tensor_copy(gSc[:, S1_:S1_ + NC], gB[:, L - 1:TLOC:L])
            DR_V and v.drain()
            v.tensor_copy(gSc[:, S1_ + NC:S1_ + 2 * NC], gMi[:, L - 1:TLOC:L])
            DR_V and v.drain()
            v.tensor_sub(gSc[:, r_:r_ + NC], gSc[:, MB:MB + NC], gSc[:, S1_ + NC:S1_ + 2 * NC])
            DR_V and v.drain()
            v.tensor_add(gSc[:, r_:r_ + NC], gSc[:, r_:r_ + NC], gSc[:, S1_:S1_ + NC])
            DR_V and v.drain()
            v.tensor_sub(gSc[:, r_:r_ + NC], gSc[:, r_:r_ + NC], gSc[:, BP:BP + NC])
            DR_V and v.drain()
            v.tensor_sub(gSc[:, a_:a_ + NC], gSc[:, MI:MI + NC], gSc[:, S1_ + NC:S1_ + 2 * NC])
            DR_V and v.drain()
            v.tensor_add(gSc[:, a_:a_ + NC], gSc[:, a_:a_ + NC], gSc[:, S1_:S1_ + NC])
            DR_V and v.drain()
            v.tensor_sub(gSc[:, a_:a_ + NC], gSc[:, a_:a_ + NC], gSc[:, BP:BP + NC])
            DR_V and v.drain()
            v.tensor_sub(gSc[:, rho_:rho_ + NC], gSc[:, MI:MI + NC], gSc[:, MB:MB + NC])
            DR_V and v.drain()
            v.tensor_scalar_add(gSc[:, cpx_:cpx_ + NC], gSc[:, BP:BP + NC], gSc[:, 4:5])
            DR_V and v.drain()
            v.tensor_sub(gSc[:, cpx_:cpx_ + NC], gSc[:, cpx_:cpx_ + NC], gSc[:, MB:MB + NC])
            DR_V and v.drain()
            vC.inc(v.engine_nop())
        V(gatefin)
        VGF = vC.val

        if KLIM < 3.2:
            def tailx32(s):
                for j in range(NT):
                    st.inc(s.dma_start(out=y_out[j * 128:(j + 1) * 128, :],
                                       in_=fscr[:, 0:DM]))
                s.wait_ge(st.sem, st.val)
            Y(tailx32)

            def tailgx32(g):
                g.wait_ge(st.sem, st.val)
            G(tailgx32)
            return nc
        def gateexp(s):
            s.wait_ge(vC.sem, VGF)
            s.activation(gU[:, 0:TLOC], gU[:, 0:TLOC], AF.Exp)
            DR_S and s.drain()
            s.activation(gM[:, 0:TLOC], gM[:, 0:TLOC], AF.Exp)
            DR_S and s.drain()
            s.activation(gSc[:, EB:EB + 4 * NC], gSc[:, EB:EB + 4 * NC], AF.Exp)
            DR_S and s.drain()
            sC.inc(s.nop())
        S(gateexp)
        SGE = sC.val

        if KLIM < 3.4:
            def tailx34(s):
                for j in range(NT):
                    st.inc(s.dma_start(out=y_out[j * 128:(j + 1) * 128, :],
                                       in_=fscr[:, 0:DM]))
                s.wait_ge(st.sem, st.val)
            Y(tailx34)

            def tailgx34(g):
                g.wait_ge(st.sem, st.val)
            G(tailgx34)
            return nc
        def ahat_rows(v):
            v.wait_ge(sC.sem, SGE)
            v.memset(gB[:], 0.0)
            DR_V and v.drain()
            for b in range(NBLK):
                v.memset(gSc[:, AB + b * NSL:AB + b * NSL + 1], 0.0)
                DR_V and v.drain()
                v.tensor_copy(gSc[:, AB + b * NSL + 1:AB + (b + 1) * NSL],
                              gSc[:, a_ + b * CPB:a_ + (b + 1) * CPB])
                DR_V and v.drain()
            vC.inc(v.engine_nop())
        V(ahat_rows)
        VAH = vC.val

        if KLIM < 3.5:
            def tailx35(s):
                for j in range(NT):
                    st.inc(s.dma_start(out=y_out[j * 128:(j + 1) * 128, :],
                                       in_=fscr[:, 0:DM]))
                s.wait_ge(st.sem, st.val)
            Y(tailx35)

            def tailgx35(g):
                g.wait_ge(st.sem, st.val)
            G(tailgx35)
            return nc
        def rows_ship(g):
            g.wait_ge(vC.sem, VAH)
            gd.inc(g.dma_start(out=rowsc[:, :], in_=gSc[:, 0:NROW]))
            g.wait_ge(gd.sem, gd.val)

            def rep2(dst, base, X):
                for p in range(HP):
                    src = rowsc[2 * p:2 * p + 2, base:base + X]
                    gd.inc(g.dma_start(
                        out=dst[:, p * X:(p + 1) * X],
                        in_=src.rearrange("h (o w) -> h o w", o=1).to_broadcast([2, 64, X])))
            rep2(rrep, r_, NC)
            rep2(rhorep, rho_, NC)
            rep2(cpxrep, cpx_, NC)
            g.wait_ge(gd.sem, gd.val)
        G(rows_ship)
        GD_REP = gd.val

        def arep_build(g, b):
            for p in range(HP):
                src = rowsc[2 * p:2 * p + 2, AB + b * NSL:AB + (b + 1) * NSL]
                gd.inc(g.dma_start(
                    out=areptmp[:, p * NSL:(p + 1) * NSL],
                    in_=src.rearrange("h (o w) -> h o w", o=1).to_broadcast([2, 64, NSL])))

        if KLIM < 3.7:
            def tailx37(s):
                for j in range(NT):
                    st.inc(s.dma_start(out=y_out[j * 128:(j + 1) * 128, :],
                                       in_=fscr[:, 0:DM]))
                s.wait_ge(st.sem, st.val)
            Y(tailx37)

            def tailgx37(g):
                g.wait_ge(st.sem, st.val)
            G(tailgx37)
            return nc
        def pcols():
            sv0 = sC.val
            tv0, vv0 = tC.val, vC.val
            PXp = [PXa, PXb]

            def tfn(t):
                t.wait_ge(sC.sem, sv0)
                for c in range(NC):
                    if c >= 2:
                        t.wait_ge(vC.sem, vv0 + 2 * (c - 1))
                    tC.inc(t.transpose(PXp[c % 2][0:128, 0:16],
                                gU[0:16, c * L:(c + 1) * L], sIdF[:]))

            def vfn(v):
                for c in range(NC):
                    v.wait_ge(tC.sem, tv0 + c + 1)
                    v.tensor_copy(gTb[:, c * 16:(c + 1) * 16], PXp[c % 2][0:128, 0:16])
                    DR_V and v.drain()
                    vC.inc(v.tensor_copy(gTf[:, c * 16:(c + 1) * 16], PXp[c % 2][0:128, 0:16]))
                    vC.inc(v.engine_nop())
            T(tfn)
            V(vfn)
        pcols()
        VPCOLS = vC.val

        def ffnprep_build():
            g1f = g1[:].bitcast(F32)
            SGB = 8 * DM
            DH = DM
            ND = 4
            thunks = []
            sc_hist = [0] * ND
            gd_hist = [gd.val] * ND

            def sign_iter(i, src_ap, dst_ap, first):
                half = i % ND
                sc2 = sc_hist[-ND]

                def yl(s, i=i, half=half, sc2=sc2, src_ap=src_ap, first=first):
                    if first:
                        s.wait_ge(sC.sem, sC.val)
                        s.wait_ge(vC.sem, vC.val)
                    if sc2 > 0:
                        s.wait_ge(sC.sem, sc2)
                    ldw.inc(s.dma_start(out=g1f[:, half * DH:(half + 1) * DH],
                                        in_=src_ap(i)))
                    s.wait_ge(ldw.sem, ldw.val)
                Y(yl)
                lv = ldw.val
                gd2 = gd_hist[-ND]

                def sg(s, lv=lv, half=half, gd2=gd2):
                    s.wait_ge(ldw.sem, lv)
                    if gd2 > 0:
                        s.wait_ge(gd.sem, gd2)
                    s.activation(g1[:, SGB + half * DH:SGB + (half + 1) * DH],
                                 g1f[:, half * DH:(half + 1) * DH], AF.Sign)
                    s.drain()
                    sC.inc(s.nop())
                S(sg)
                sv = sC.val
                sc_hist.append(sv)

                def yst(g, i=i, half=half, sv=sv, dst_ap=dst_ap):
                    g.wait_ge(sC.sem, sv)
                    g.wait_ge(gd.sem, gd.val)
                    gd.inc(g.dma_start(out=dst_ap(i),
                                       in_=g1[:, SGB + half * DH:SGB + (half + 1) * DH]))
                    g.wait_ge(gd.sem, gd.val)
                G(yst)
                gd_hist.append(gd.val)

            vh = [0] * ND

            def alpha_iter(i, src_ap, red_fn, first):
                half = i % ND
                v2 = vh[-ND]

                def yl(s, i=i, half=half, v2=v2, src_ap=src_ap, first=first):
                    if first:
                        s.wait_ge(sC.sem, sC.val)
                        s.wait_ge(vC.sem, vC.val)
                    if v2 > 0:
                        s.wait_ge(vC.sem, v2)
                    ldw.inc(s.dma_start(out=g1f[:, half * DH:(half + 1) * DH],
                                        in_=src_ap(i)))
                    s.wait_ge(ldw.sem, ldw.val)
                Y(yl)
                lv = ldw.val

                def va(v, i=i, half=half, lv=lv, red_fn=red_fn):
                    v.wait_ge(ldw.sem, lv)
                    red_fn(v, i, g1f[:, half * DH:(half + 1) * DH])
                    vC.inc(v.engine_nop())
                V(va)
                vh.append(vC.val)

            NS1 = DF // DH
            for i in range(FC * NS1):
                thunks.append(lambda i=i: sign_iter(
                    i,
                    lambda i: w1T_in[(i // NS1) * 128:(i // NS1) * 128 + 128,
                                     (i % NS1) * DH:(i % NS1 + 1) * DH],
                    lambda i: s1sc[(i // NS1) * 128:(i // NS1) * 128 + 128,
                                   (i % NS1) * DH:(i % NS1 + 1) * DH],
                    i < ND))

            def red1(v, i, ap_):
                v.reduce_sum(sA1[:, i:i + 1], ap_[:, 0:DM], axis=AX.X,
                             apply_absolute_value=True)
                DR_V and v.drain()
            for i in range(HC):
                thunks.append(lambda i=i: alpha_iter(
                    i, lambda i: w1h_in[i * 128:(i + 1) * 128, :], red1, i < ND))

            def vsc1(v):
                v.tensor_scalar_mul(sA1[:, 0:HC], sA1[:, 0:HC], 1.0 / DM)
                DR_V and v.drain()
                vC.inc(v.engine_nop())
            thunks.append(lambda: V(vsc1))

            for i in range(HC):
                thunks.append(lambda i=i: sign_iter(
                    i, lambda i: w2T_in[i * 128:(i + 1) * 128, :],
                    lambda i: s2sc[i * 128:(i + 1) * 128, :], i < ND))

            NS2 = DF // DH
            A2 = 5 * NT + 5

            def red2(v, i, ap_):
                d, hseg = divmod(i, NS2)
                v.reduce_sum(sMu[:, A2 + hseg:A2 + hseg + 1], ap_[:, 0:DH],
                             axis=AX.X, apply_absolute_value=True)
                DR_V and v.drain()
                if hseg == NS2 - 1:
                    for h2 in range(1, NS2):
                        v.tensor_add(sMu[:, A2:A2 + 1], sMu[:, A2:A2 + 1],
                                     sMu[:, A2 + h2:A2 + h2 + 1])
                        DR_V and v.drain()
                    v.tensor_copy(sA2[:, d:d + 1], sMu[:, A2:A2 + 1])
                    DR_V and v.drain()
            for i in range(FC * NS2):
                thunks.append(lambda i=i: alpha_iter(
                    i,
                    lambda i: w2h_in[(i // NS2) * 128:(i // NS2) * 128 + 128,
                                     (i % NS2) * DH:(i % NS2 + 1) * DH],
                    red2, i < ND))

            def vsc2(v):
                v.tensor_scalar_mul(sA2[:, 0:FC], sA2[:, 0:FC], 1.0 / DF)
                DR_V and v.drain()
                vC.inc(v.engine_nop())
            thunks.append(lambda: V(vsc2))
            return thunks

        prep_thunks = ffnprep_build() if KLIM >= 3.9 else []

        slot_gate = [None, None]


        def wait_gate(eng, gate):
            if gate is None:
                eng.wait_ge(bT.sem, bT.val)
                return
            ctr, val = gate
            eng.wait_ge(ctr.sem, val)

        t60_end = [T2END]
        blkT = [T2END]
        scanV = [V2END]
        carry_gate = [None]
        for b in (range(NBLK) if KLIM >= 4 else []):
            tb = b * BLK
            c0 = b * CPB
            t60p = t60_end[-1]

            def s60(s, b=b, t60p=t60p):
                if b == 0:
                    s.wait_ge(vC.sem, V2END)
                s.wait_ge(tC.sem, t60p)
                ln1_apply_block(s, b)
                DR_S and s.drain()
                sC.inc(s.nop())
            S(s60)
            xpose_block_emit(pb_free=(V2END if b == 0 else None))
            t60_end.append(tC.val)
            XTE = vC.val

            if KSUB < 1:
                continue
            bank_gate = {0: None, 1: None}
            spill_gd = []
            pairs = [(w, e) for w in ("k", "v", "q", "o") for e in range(FC)]
            for i, (which, e) in enumerate(pairs):
                slot = i % 2
                bank = i % 2
                Wsrc = {"k": Wk, "v": Wv, "q": Wq, "o": Wo}[which]
                sg = slot_gate[1] if i >= 1 else None

                def yload(g, Wsrc=Wsrc, e=e, slot=slot, sg=sg, b=b, i=i):
                    if b == 0 and i == 0:
                        g.wait_ge(tC.sem, T2END)
                    wait_gate(g, sg)
                    g.wait_ge(gd.sem, gd.val)
                    gd.inc(g.dma_start(
                        out=Wwin[:, slot * FC * 128:(slot + 1) * FC * 128].rearrange(
                            "p (c n) -> p c n", c=FC),
                        in_=Wsrc[0:DM, e * 128:(e + 1) * 128].rearrange(
                            "(c p) n -> p c n", p=128)))
                    g.wait_ge(gd.sem, gd.val)
                G(yload)
                LV = gd.val
                bg = bank_gate[bank]

                def tmm(t, slot=slot, bank=bank, VV=LV, bg=bg, XTE=XTE):
                    t.wait_ge(vC.sem, XTE)
                    t.wait_ge(gd.sem, VV)
                    wait_gate(t, bg)
                    for fc in range(FC):
                        t.matmul(PF[bank][:, 0:BLK],
                                 Wwin[:, slot * FC * 128 + fc * 128:slot * FC * 128 + fc * 128 + 128],
                                 xT[:, fc * BLK:fc * BLK + BLK],
                                 start=(fc == 0), stop=(fc == FC - 1))
                    t.drain()
                    tC.inc(t.nop())
                T(tmm)
                TV = tC.val
                slot_gate[slot] = (tC, TV)
                slot_gate[1] = (tC, TV)
                if which in ("k", "v"):
                    dstT = Kfm if which == "k" else Vfm

                    def ev(v, e=e, bank=bank, TV=TV, dstT=dstT, i=i, bTp=blkT[-1]):
                        if i == 0:
                            v.wait_ge(tC.sem, bTp)
                        v.wait_ge(tC.sem, TV)
                        vC.inc(v.tensor_copy(dstT[:, e * BLK:(e + 1) * BLK], PF[bank][:, 0:BLK]))
                    V(ev)
                    bank_gate[bank] = (vC, vC.val)
                elif which == "q":
                    def ev(s, e=e, bank=bank, TV=TV, tb=tb, bTp=blkT[-1]):
                        if e == 0:
                            s.wait_ge(tC.sem, bTp)
                        s.wait_ge(tC.sem, TV)
                        s.activation(Qwin[:, e * BLK:(e + 1) * BLK],
                                     PF[bank][:, 0:BLK], AF.Identity, scale=0.125)
                        DR_S and s.drain()
                        sC.inc(s.nop())
                    S(ev)
                    bank_gate[bank] = (sC, sC.val)

                    def qsp(g, e=e, sv=sC.val, tb=tb):
                        g.wait_ge(sC.sem, sv)
                        g.wait_ge(gd.sem, gd.val)
                        gd.inc(g.dma_start(out=Qhbm[:, e * TLOC + tb:e * TLOC + tb + BLK],
                                           in_=Qwin[:, e * BLK:(e + 1) * BLK]))
                        g.wait_ge(gd.sem, gd.val)
                    G(qsp)
                    spill_gd.append(gd.val)
                else:
                    def ev(s, e=e, bank=bank, TV=TV, tb=tb, b=b, bTp=blkT[-1]):
                        if e == 0:
                            s.wait_ge(tC.sem, bTp)
                        s.wait_ge(tC.sem, TV)
                        s.activation(SHR[:, e * BLK:(e + 1) * BLK],
                                     PF[bank][:, 0:BLK], AF.Sigmoid, bias=sBo[:, e:e + 1])
                        DR_S and s.drain()
                        sC.inc(s.nop())
                    S(ev)
                    bank_gate[bank] = (sC, sC.val)

                    def osp(g, e=e, sv=sC.val, tb=tb):
                        g.wait_ge(sC.sem, sv)
                        g.wait_ge(gd.sem, gd.val)
                        gd.inc(g.dma_start(out=Ohbm[:, e * TLOC + tb:e * TLOC + tb + BLK],
                                           in_=SHR[:, e * BLK:(e + 1) * BLK]))
                        g.wait_ge(gd.sem, gd.val)
                    G(osp)
                    spill_gd.append(gd.val)
            GD_SPILL = gd.val

            if KSUB < 2:
                continue
            items = [(src, cb, fc) for src in ("k", "v") for cb in range(CPB) for fc in range(FC)]
            tv0, vv0 = tC.val, vC.val

            def t62(t, items=items, tv0=tv0, vv0=vv0):
                for i, (src, cb, fc) in enumerate(items):
                    if i >= 2:
                        t.wait_ge(vC.sem, vv0 + 2 * (i - 1))
                    else:
                        t.wait_ge(vC.sem, vv0)
                    buf = Kfm if src == "k" else Vfm
                    tC.inc(t.transpose(PB[i % 2][:, 0:128],
                                buf[:, fc * BLK + cb * 128:fc * BLK + cb * 128 + 128],
                                sId[:]))
            T(t62)

            def v62(v, items=items, tv0=tv0, c0=c0, bTp=blkT[-1]):
                v.wait_ge(tC.sem, bTp)
                v.wait_ge(gd.sem, GD_SPILL)
                for i, (src, cb, fc) in enumerate(items):
                    v.wait_ge(tC.sem, tv0 + i + 1)
                    cg = c0 + cb
                    if src == "k":
                        for hh in range(2):
                            h = 2 * fc + hh
                            v.tensor_scalar(
                                KPT[:, cb * DM + h * 64:cb * DM + (h + 1) * 64],
                                PB[i % 2][:, hh * 64:(hh + 1) * 64],
                                gTf[:, cg * 16 + h:cg * 16 + h + 1], None, ALU.mult)
                        DR_V and v.drain()
                        vC.inc(v.engine_nop())
                        vC.inc(v.engine_nop())
                    else:
                        vC.inc(v.tensor_copy(
                            VTp[:, cb * H * 64 + fc * 128:cb * H * 64 + (fc + 1) * 128],
                            PB[i % 2][:, 0:128]))
                        vC.inc(v.engine_nop())
            V(v62)
            V62END = vC.val

            if KSUB < 3:
                continue
            def g63(g, b=b, sV=scanV[-1]):
                g.wait_ge(vC.sem, sV)
                g.wait_ge(gd.sem, gd.val)
                arep_build(g, b)
                g.wait_ge(gd.sem, gd.val)
            G(g63)
            GAR = gd.val

            raw_gate = [carry_gate[0]]
            for cb in range(CPB):
                cg = c0 + cb
                rg = raw_gate[0]

                def t63(t, cb=cb, rg=rg, b=b, V62END=V62END):
                    if cb == 0:
                        if b == 0:
                            t.wait_ge(vC.sem, VPCOLS)
                        t.wait_ge(vC.sem, V62END)
                        t.wait_ge(gd.sem, GD_REP)
                    wait_gate(t, rg)
                    for h in range(H):
                        p = h // 2
                        bankp = PX[p // 4]
                        rowb = (h % 2) * 64
                        colb = (p % 4) * SL
                        t.matmul(bankp[rowb:rowb + 64, colb:colb + 64],
                                 VTp[:, cb * H * 64 + h * 64:cb * H * 64 + (h + 1) * 64],
                                 KPT[:, cb * DM + h * 64:cb * DM + (h + 1) * 64],
                                 start=True, stop=True)
                        t.matmul(bankp[rowb:rowb + 64, colb + 64:colb + 65],
                                 KPT[:, cb * DM + h * 64:cb * DM + (h + 1) * 64],
                                 sOne[:, 0:1], start=True, stop=True)
                    t.drain()
                    tC.inc(t.nop())
                T(t63)
                TV63 = tC.val

                def v63(v, cb=cb, cg=cg, TV63=TV63):
                    v.wait_ge(tC.sem, TV63)
                    for bk in range(NBRAW):
                        npair = min(4, HP - bk * 4)
                        pv = PX[bk][:, 0:npair * SL].rearrange("p (q e) -> p q e", e=SL)
                        v.tensor_tensor(
                            rawZ[:, 0:HP * SL * NSL].rearrange(
                                "p (q e n) -> p q e n", e=SL, n=NSL)[:, bk * 4:bk * 4 + npair, :, cb + 1:cb + 2],
                            pv.rearrange("p q (e o) -> p q e o", o=1),
                            rrep[:, 0:HP * NC].rearrange("p (q c) -> p q c", c=NC)[
                                :, bk * 4:bk * 4 + npair, cg:cg + 1]
                            .rearrange("p q (c o) -> p q c o", o=1)
                            .to_broadcast([128, npair, SL, 1]),
                            ALU.mult)
                        DR_V and v.drain()
                    vC.inc(v.engine_nop())
                V(v63)
                raw_gate[0] = (vC, vC.val)

            if KSUB < 4:
                continue
            def v64(v, b=b, c0=c0, bTp=blkT[-1]):
                v.wait_ge(tC.sem, bTp)
                v.wait_ge(gd.sem, GAR)
                if b > 0:
                    v.tensor_copy(
                        rawZ[:, 0:HP * SL * NSL].rearrange("p (q e n) -> p q e n", e=SL, n=NSL)[:, :, :, 0:1],
                        scanZ[:, 0:HP * SL * NSL].rearrange("p (q e n) -> p q e n", e=SL, n=NSL)[:, :, :, CPB:CPB + 1])
                    DR_V and v.drain()
                for p in range(HP):
                    v.tensor_copy(
                        arep[:, 0:SL * NSL].rearrange("p (e n) -> p e n", n=NSL),
                        areptmp[:, p * NSL:(p + 1) * NSL].rearrange("p (o n) -> p o n", o=1)
                        .to_broadcast([128, SL, NSL]))
                    DR_V and v.drain()
                    v.tensor_tensor_scan(
                        scanZ[:, p * SL * NSL:(p + 1) * SL * NSL],
                        arep[:, 0:SL * NSL],
                        rawZ[:, p * SL * NSL:(p + 1) * SL * NSL],
                        0.0, ALU.mult, ALU.add)
                    DR_V and v.drain()
                v.tensor_tensor(
                    scanS[:, 0:HP * SL * CPB].rearrange("p (q n e) -> p q n e", n=CPB, e=SL),
                    scanZ[:, 0:HP * SL * NSL].rearrange("p (q e n) -> p q n e", e=SL, n=NSL)[:, :, 0:CPB, :],
                    rhorep[:, 0:HP * NC].rearrange("p (q c) -> p q c", c=NC)[:, :, c0:c0 + CPB]
                    .rearrange("p q (c o) -> p q c o", o=1).to_broadcast([128, HP, CPB, SL]),
                    ALU.mult)
                DR_V and v.drain()
                vC.inc(v.engine_nop())
            V(v64)
            scanV.append(vC.val)

            if KSUB < 5:
                continue
            chunk_gate = carry_gate[0]
            for cb in range(CPB):
                cg = c0 + cb
                grp_gate = chunk_gate
                for gq in range(H // 4):
                    def t65(t, cb=cb, gq=gq, gg=grp_gate, tb=tb):
                        wait_gate(t, gg)
                        for hh in (0, 2, 1, 3):
                            h = gq * 4 + hh
                            rb = (h % 2) * 64
                            fc = h // 2
                            mmn(t, rb)
                            t.matmul(PFa[:, hh * 128:(hh + 1) * 128],
                                     Vfm[rb:rb + 64, fc * BLK + cb * 128:fc * BLK + cb * 128 + 128],
                                     Qwin[rb:rb + 64, fc * BLK + cb * 128:fc * BLK + cb * 128 + 128],
                                     start=True, stop=True)
                        for hh in (0, 2, 1, 3):
                            h = gq * 4 + hh
                            rb = (h % 2) * 64
                            fc = h // 2
                            mmn(t, rb)
                            t.matmul(PFb[:, hh * 128:(hh + 1) * 128],
                                     Kfm[rb:rb + 64, fc * BLK + cb * 128:fc * BLK + cb * 128 + 128],
                                     Qwin[rb:rb + 64, fc * BLK + cb * 128:fc * BLK + cb * 128 + 128],
                                     start=True, stop=True)
                        t.drain()
                        tC.inc(t.nop())
                    T(t65)
                    TV65 = tC.val

                    def v65(v, gq=gq, TV65=TV65):
                        v.wait_ge(tC.sem, TV65)
                        mview = sMk[:, 0:128].rearrange("p (o l) -> p o l", o=1).to_broadcast([128, 4, 128])
                        v.tensor_tensor(
                            BvB[:, gq * 512:(gq + 1) * 512].rearrange("p (q l) -> p q l", q=4),
                            PFa[:, 0:512].rearrange("p (q l) -> p q l", q=4),
                            mview, ALU.mult)
                        DR_V and v.drain()
                        v.tensor_tensor(
                            BkB[:, gq * 512:(gq + 1) * 512].rearrange("p (q l) -> p q l", q=4),
                            PFb[:, 0:512].rearrange("p (q l) -> p q l", q=4),
                            mview, ALU.mult)
                        DR_V and v.drain()
                        vC.inc(v.engine_nop())
                    V(v65)
                    grp_gate = (vC, vC.val)
                    VB65 = vC.val
                    if KSUB < 5.5:
                        continue

                    def t66(t, cb=cb, cg=cg, gq=gq, VB65=VB65, tb=tb):
                        t.wait_ge(vC.sem, VB65)
                        for hh in range(4):
                            h = gq * 4 + hh
                            p = h // 2
                            rb = (h % 2) * 64
                            fc = h // 2
                            comb = PF[2 + (p % 2)]
                            ccol = (p // 2) * 128
                            qap = Qwin[rb:rb + 64, fc * BLK + cb * 128:fc * BLK + cb * 128 + 128]
                            mmn(t, None)
                            t.matmul(comb[rb:rb + 64, ccol:ccol + 128],
                                     KPT[:, cb * DM + h * 64:cb * DM + (h + 1) * 64],
                                     BvB[:, h * 128:(h + 1) * 128],
                                     start=True, stop=False)
                            mmn(t, rb)
                            t.matmul(comb[rb:rb + 64, ccol:ccol + 128],
                                     scanS[rb:rb + 64, p * SL * CPB + cb * SL:p * SL * CPB + cb * SL + 64],
                                     qap, start=False, stop=True)
                            dbk = PX[h // 8]
                            drow = 32 * ((h % 8) // 4)
                            dcol = (h % 4) * 128
                            t.matmul(dbk[drow:drow + 1, dcol:dcol + 128],
                                     scanS[rb:rb + 64, p * SL * CPB + cb * SL + 64:p * SL * CPB + cb * SL + 65],
                                     qap, start=True, stop=False)
                            mmn(t, None)
                            t.matmul(dbk[drow:drow + 1, dcol:dcol + 128],
                                     gTb[:, cg * 16 + h:cg * 16 + h + 1],
                                     BkB[:, h * 128:(h + 1) * 128],
                                     start=False, stop=True)
                        t.drain()
                        tC.inc(t.nop())
                    T(t66)
                TV66 = tC.val
                if KSUB < 5.7:
                    chunk_gate = grp_gate
                    continue

                def v66(v, cb=cb, TV66=TV66, tb=tb, nsg=None):
                    v.wait_ge(tC.sem, TV66)
                    if nsg is not None:
                        v.wait_ge(gd.sem, nsg)
                    for bk in range(2):
                        v.tensor_copy(
                            N0stg[:, 0:HP * 128].rearrange("p (f t) -> p f t", f=HP)[
                                :, bk::2, :],
                            PF[2 + bk][:, 0:PPB * 128].rearrange("p (q l) -> p q l", q=PPB))
                        DR_V and v.drain()
                    for bk in range(NBDEN):
                        v.tensor_copy(dslab[:, bk * 512:bk * 512 + 512], PX[bk][:, 0:512])
                        DR_V and v.drain()
                    vC.inc(v.engine_nop())
                V(v66)
                VD = vC.val
                chunk_gate = (vC, VD)
                gprev = gd.val

                def g66(g, cg=cg, VD=VD, gprev=gprev):
                    g.wait_ge(vC.sem, VD)
                    g.wait_ge(gd.sem, gd.val)
                    gd.inc(g.dma_start(
                        out=N0hbm[:, 0:HP * TLOC].rearrange("p (q t) -> p q t", q=HP)[
                            :, :, cg * 128:(cg + 1) * 128],
                        in_=N0stg[:, 0:HP * 128].rearrange("p (q t) -> p q t", q=HP)))
                    for half in range(NBDEN):
                        nh = min(8, H - half * 8)
                        gd.inc(g.dma_start(
                            out=den0[8 * half:8 * half + nh, cg * 128:(cg + 1) * 128],
                            in_=dslab[:, half * 512:(half + 1) * 512]
                            .rearrange("p (q l) -> p q l", q=4)[0:64:32, :, :][0:(nh + 3) // 4, 0:min(4, nh), :]))
                G(g66)
                chunk_gate_gd = gd.val

                def vwait_gd(v, gdv=chunk_gate_gd):
                    v.wait_ge(gd.sem, gdv)
                    vC.inc(v.engine_nop())
                V(vwait_gd)
                chunk_gate = (vC, vC.val)
                npt = (len(prep_thunks) + NBLK * CPB - 1) // (NBLK * CPB)
                i0t = (b * CPB + cb) * npt
                for th in prep_thunks[i0t:i0t + npt]:
                    th()
            blkT.append(tC.val)
            carry_gate[0] = chunk_gate
        _npt = (len(prep_thunks) + NBLK * CPB - 1) // (NBLK * CPB)
        for th in prep_thunks[((NBLK * CPB if KLIM >= 4 else 0)) * _npt:]:
            th()
        GD_PREP = gd.val
        VPREP = vC.val

        if KLIM < 5:
            def tail(s):
                for j in range(NT):
                    st.inc(s.dma_start(out=y_out[j * 128:(j + 1) * 128, :],
                                       in_=fscr[:, 0:DM]))
            Y(tail)

            def tailw(s):
                s.wait_ge(st.sem, st.val)
            Y(tailw)

            def tailg(g):
                g.wait_ge(st.sem, st.val)
            G(tailg)
            return nc

        def v70(v):
            v.tensor_copy(
                sSend[:, 0:HP * SL].rearrange("p (q e o) -> p q e o", e=SL, o=1),
                scanZ[:, 0:HP * SL * NSL].rearrange("p (q e n) -> p q e n", e=SL, n=NSL)[:, :, :, CPB:CPB + 1])
            DR_V and v.drain()
            vC.inc(v.engine_nop())
        V(v70)
        V70 = vC.val

        def g70(g):
            g.wait_ge(vC.sem, V70)
            gd.inc(g.dma_start(out=cs_in[:, :], in_=sSend[:]))
            g.wait_ge(gd.sem, gd.val)
            cc.inc(g.collective_compute(
                "AllReduce", ALU.add, replica_groups=groups,
                ins=[cs_in.ap().opt()], outs=[cs_out.ap().opt()]))
            g.wait_ge(cc.sem, cc.val)
            gd.inc(g.dma_start(out=sXsum[:, :], in_=cs_out[:, :]))
            g.wait_ge(gd.sem, gd.val)
        G(g70)
        GD_X = gd.val

        def v71(v):
            v.wait_ge(gd.sem, GD_X)
            v.tensor_sub(sXsum[:], sXsum[:], sSend[:])
            DR_V and v.drain()
            v.tensor_scalar(sXsum[:], sXsum[:], sIsr[:, 0:1], None, ALU.mult)
            DR_V and v.drain()
            vC.inc(v.engine_nop())
        V(v71)
        barrier()

        chunk_gate = None
        gate_hist = [None, None]
        vd2_hist = [0]
        tv72_hist = [0]
        g72_hist = [0]
        for cg in range(NC):
            b = cg // CPB
            tb = b * BLK
            slot = cg % 2
            VO = slot * FC * 128
            N0b = N0stg if slot == 0 else Kfm
            NO = 0
            XO = 0
            gg2 = gate_hist[-2]

            def gql(g, cg=cg, gg=gg2, VO=VO, NO=NO, N0b=N0b):
                wait_gate(g, gg)
                g.wait_ge(gd.sem, gd.val)
                gd.inc(g.dma_start(
                    out=Vfm[:, VO:VO + FC * 128],
                    in_=Qhbm[:, 0:FC * TLOC].rearrange("p (f t) -> p f t", f=FC)[
                        :, :, cg * 128:(cg + 1) * 128]))
                gd.inc(g.dma_start(
                    out=N0b[:, NO:NO + HP * 128],
                    in_=N0hbm[:, 0:HP * TLOC].rearrange("p (q t) -> p q t", q=HP)[
                        :, :, cg * 128:(cg + 1) * 128]))
                g.wait_ge(gd.sem, gd.val)
            G(gql)
            GQL = gd.val

            tv72p = tv72_hist[-1]

            def v72(v, cg=cg, GQL=GQL, XO=XO, tv72p=tv72p):
                if tv72p > 0:
                    v.wait_ge(tC.sem, tv72p)
                v.wait_ge(gd.sem, GQL)
                v.tensor_tensor(
                    sXcC[:, XO:XO + HP * SL].rearrange("p (q e o) -> p q e o", e=SL, o=1),
                    sXsum[:, 0:HP * SL].rearrange("p (q e o) -> p q e o", e=SL, o=1),
                    cpxrep[:, 0:HP * NC].rearrange("p (q c) -> p q c", c=NC)[:, :, cg:cg + 1]
                    .rearrange("p q (c o) -> p q c o", o=1).to_broadcast([128, HP, SL, 1]),
                    ALU.mult)
                DR_V and v.drain()
                vC.inc(v.engine_nop())
            V(v72)
            VXC = vC.val
            vd2p = vd2_hist[-1]

            def t72(t, cg=cg, VXC=VXC, VO=VO, XO=XO, vd2p=vd2p):
                t.wait_ge(vC.sem, VXC)
                if vd2p > 0:
                    t.wait_ge(vC.sem, vd2p)
                for h in list(range(0, H, 2)) + list(range(1, H, 2)):
                    p = h // 2
                    rb = (h % 2) * 64
                    fc = h // 2
                    comb = PF[2 + (p % 2)]
                    ccol = (p // 2) * 128
                    qap = Vfm[rb:rb + 64, VO + fc * 128:VO + (fc + 1) * 128]
                    mmn(t, rb)
                    t.matmul(comb[rb:rb + 64, ccol:ccol + 128],
                             sXcC[rb:rb + 64, XO + p * SL:XO + p * SL + 64],
                             qap, start=True, stop=True)
                    dbk = PX[h // 8]
                    drow = 32 * ((h % 8) // 4)
                    dcol = (h % 4) * 128
                    t.matmul(dbk[drow:drow + 1, dcol:dcol + 128],
                             sXcC[rb:rb + 64, XO + p * SL + 64:XO + p * SL + 65],
                             qap, start=True, stop=True)
                t.drain()
                tC.inc(t.nop())
            T(t72)
            TV72 = tC.val
            tv72_hist.append(TV72)
            g72p = g72_hist[-1]

            def v73(v, cg=cg, TV72=TV72, NO=NO, g72p=g72p, N0b=N0b):
                v.wait_ge(tC.sem, TV72)
                if g72p > 0:
                    v.wait_ge(gd.sem, g72p)
                for bk in range(2):
                    nview = N0b[:, NO:NO + HP * 128].rearrange("p (f t) -> p f t", f=HP)[
                        :, bk::2, :]
                    v.tensor_tensor(nview, PF[2 + bk][:, 0:PPB * 128].rearrange("p (q l) -> p q l", q=PPB),
                                    nview, ALU.add)
                    DR_V and v.drain()
                for bk in range(NBDEN):
                    v.tensor_copy(dslab[:, bk * 512:bk * 512 + 512], PX[bk][:, 0:512])
                    DR_V and v.drain()
                vC.inc(v.engine_nop())
            V(v73)
            VD2 = vC.val
            vd2_hist.append(VD2)

            def g72(g, cg=cg, VD2=VD2, NO=NO, N0b=N0b):
                g.wait_ge(vC.sem, VD2)
                g.wait_ge(gd.sem, gd.val)
                gd.inc(g.dma_start(
                    out=N0hbm[:, 0:HP * TLOC].rearrange("p (q t) -> p q t", q=HP)[
                        :, :, cg * 128:(cg + 1) * 128],
                    in_=N0b[:, NO:NO + HP * 128].rearrange("p (q t) -> p q t", q=HP)))
                for half in range(NBDEN):
                    nh = min(8, H - half * 8)
                    gd.inc(g.dma_start(
                        out=denXr[8 * half:8 * half + nh, cg * 128:(cg + 1) * 128],
                        in_=dslab[:, half * 512:(half + 1) * 512]
                        .rearrange("p (q l) -> p q l", q=4)[0:64:32, :, :][0:(nh + 3) // 4, 0:min(4, nh), :]))
                g.wait_ge(gd.sem, gd.val)
            G(g72)
            gdv = gd.val
            g72_hist.append(gdv)

            def vw(v, gdv=gdv):
                v.wait_ge(gd.sem, gdv)
                vC.inc(v.engine_nop())
            V(vw)
            chunk_gate = (vC, vC.val)
            gate_hist.append(chunk_gate)

        def v74(v):
            v.tensor_add(dfin[:, 0:TLOC], den0[:, 0:TLOC], denXr[:, 0:TLOC])
            DR_V and v.drain()
            v.tensor_tensor(dfin[:, 0:TLOC], dfin[:, 0:TLOC], gM[:, 0:TLOC], ALU.mult)
            DR_V and v.drain()
            vC.inc(v.engine_nop())
        V(v74)
        V74 = vC.val

        def s74(s):
            s.wait_ge(vC.sem, V74)
            s.activation(dfin[:, 0:TLOC], dfin[:, 0:TLOC], AF.Abs)
            DR_S and s.drain()
            sC.inc(s.nop())
        S(s74)
        S74 = sC.val

        def v75(v):
            v.wait_ge(sC.sem, S74)
            v.tensor_scalar_max(dfin[:, 0:TLOC], dfin[:, 0:TLOC], 1.0)
            DR_V and v.drain()
            v.reciprocal(dfin[:, 0:TLOC], dfin[:, 0:TLOC])
            DR_V and v.drain()
            v.tensor_tensor(Grow[:, 0:TLOC], gM[:, 0:TLOC], dfin[:, 0:TLOC], ALU.mult)
            DR_V and v.drain()
            vC.inc(v.engine_nop())
        V(v75)
        V75 = vC.val

        def g75(g):
            g.wait_ge(vC.sem, V75)
            gd.inc(g.dma_start(out=gsc[:, :], in_=Grow[:, 0:TLOC]))
            g.wait_ge(gd.sem, gd.val)
        G(g75)
        GD_G = gd.val

        hq_gate = None
        for p in range(FC):
            gprev = gd.val

            def g76(g, p=p):
                if p == 0:
                    g.wait_ge(gd.sem, GD_G)
                g.wait_ge(gd.sem, gd.val)
                gd.inc(g.dma_start(
                    out=Grep[:, 0:TLOC],
                    in_=gsc[2 * p:2 * p + 2, :].rearrange("h (o w) -> h o w", o=1)
                    .to_broadcast([2, 64, TLOC])))
                gd.inc(g.dma_start(out=Kfm[:, 0:TLOC],
                                   in_=Ohbm[:, p * TLOC:(p + 1) * TLOC]))
                gd.inc(g.dma_start(out=Kfm[:, TLOC:2 * TLOC],
                                   in_=N0hbm[:, p * TLOC:(p + 1) * TLOC]))
            def g76w(g, hq=hq_gate):
                wait_gate(g, hq)
            G(g76w)
            G(g76)
            GDS = gd.val

            def v76(v, p=p, GDS=GDS):
                v.wait_ge(gd.sem, GDS)
                v.tensor_tensor(fscr2[:, 0:TLOC], Kfm[:, 0:TLOC],
                                Grep[:, 0:TLOC], ALU.mult)
                DR_V and v.drain()
                v.tensor_tensor(fscr2[:, 0:TLOC], fscr2[:, 0:TLOC],
                                Kfm[:, TLOC:2 * TLOC], ALU.mult)
                DR_V and v.drain()
                vC.inc(v.engine_nop())
            V(v76)
            hq_gate = (vC, vC.val)
            VH = vC.val
            tv0, vv0 = tC.val, vC.val

            def t76(t, p=p, VH=VH, tv0=tv0, vv0=vv0):
                t.wait_ge(vC.sem, VH)
                for j in range(NT):
                    if j >= 2:
                        t.wait_ge(vC.sem, vv0 + j - 1)
                    else:
                        t.wait_ge(vC.sem, vv0)
                    tC.inc(t.transpose(PB[j % 2][:, 0:128], fscr2[:, j * 128:(j + 1) * 128], sId[:]))
            T(t76)

            def v77(v, p=p, tv0=tv0):
                for j in range(NT):
                    v.wait_ge(tC.sem, tv0 + j + 1)
                    vC.inc(v.tensor_copy(
                        g1[:, j * DM + p * 128:j * DM + p * 128 + 128],
                        PB[j % 2][:, 0:128]))
            V(v77)
            hq_gate = (vC, vC.val)
        barrier()

        if KLIM < 6:
            def tail(s):
                for j in range(NT):
                    st.inc(s.dma_start(out=y_out[j * 128:(j + 1) * 128, :],
                                       in_=fscr[:, 0:DM]))
                s.wait_ge(st.sem, st.val)
            Y(tail)

            def tailg(g):
                g.wait_ge(st.sem, st.val)
            G(tailg)
            return nc
        NG8 = NT // 4
        LA = 6 * NT + 8
        LB = 6 * NT + 24
        fscr2_free = [None]

        def ln8(src_view, base, writer):
            for g in range(NG8):
                g0 = g * 4

                def vsumA(v, g0=g0, base=base):
                    v.reduce_sum(sMu[:, base:base + 4],
                                 src_view(g0, 4).rearrange("p (t d) -> p t d", t=4),
                                 axis=AX.X)
                    DR_V and v.drain()
                    v.tensor_scalar_mul(sMu[:, base:base + 4], sMu[:, base:base + 4], 1.0 / DM)
                    DR_V and v.drain()
                    vC.inc(v.engine_nop())
                V(vsumA)
                VMU = vC.val
                ffree = fscr2_free[0]

                def ssqA(s, g0=g0, VMU=VMU, ffree=ffree):
                    s.wait_ge(vC.sem, VMU)
                    if ffree is not None:
                        s.wait_ge(ffree[0].sem, ffree[1])
                    s.activation(fscr2[:, 0:4 * DM], src_view(g0, 4), AF.Square)
                    DR_S and s.drain()
                    sC.inc(s.nop())
                S(ssqA)
                SSQ = sC.val

                def vvarA(v, SSQ=SSQ, base=base):
                    v.wait_ge(sC.sem, SSQ)
                    mu4 = sMu[:, base:base + 4]
                    var4 = sMu[:, base + 4:base + 8]
                    nb4 = sMu[:, base + 12:base + 16]
                    v.reduce_sum(var4, fscr2[:, 0:4 * DM].rearrange(
                        "p (t d) -> p t d", t=4), axis=AX.X)
                    DR_V and v.drain()
                    v.tensor_scalar_mul(var4, var4, 1.0 / DM)
                    DR_V and v.drain()
                    v.tensor_tensor(nb4, mu4, mu4, ALU.mult)
                    DR_V and v.drain()
                    v.tensor_sub(var4, var4, nb4)
                    DR_V and v.drain()
                    v.tensor_scalar_add(var4, var4, LN_EPS)
                    DR_V and v.drain()
                    vC.inc(v.engine_nop())
                V(vvarA)
                VV = vC.val

                def srsA(s, VV=VV, base=base):
                    s.wait_ge(vC.sem, VV)
                    var4 = sMu[:, base + 4:base + 8]
                    rs4 = sMu[:, base + 8:base + 12]
                    s.activation(rs4, var4, AF.Ln)
                    DR_S and s.drain()
                    s.activation(rs4, rs4, AF.Exp, scale=-0.5)
                    DR_S and s.drain()
                    sC.inc(s.nop())
                S(srsA)
                SR = sC.val

                def vbiasA(v, SR=SR, base=base):
                    v.wait_ge(sC.sem, SR)
                    mu4 = sMu[:, base:base + 4]
                    rs4 = sMu[:, base + 8:base + 12]
                    nb4 = sMu[:, base + 12:base + 16]
                    v.tensor_tensor(nb4, mu4, rs4, ALU.mult)
                    DR_V and v.drain()
                    v.tensor_scalar_mul(nb4, nb4, -1.0)
                    DR_V and v.drain()
                    vC.inc(v.engine_nop())
                V(vbiasA)
                VB = vC.val

                def sapA(s, g0=g0, VB=VB, base=base):
                    s.wait_ge(vC.sem, VB)
                    for j in range(4):
                        s.activation(fscr2[:, j * DM:(j + 1) * DM],
                                     src_view(g0 + j, 1), AF.Identity,
                                     bias=sMu[:, base + 12 + j:base + 13 + j],
                                     scale=sMu[:, base + 8 + j:base + 9 + j])
                    s.drain()
                    sC.inc(s.nop())
                S(sapA)
                SA = sC.val
                writer(g0, SA)
                fscr2_free[0] = (vC, vC.val)

        def g1view(j0, n):
            return g1[:, j0 * DM:(j0 + n) * DM]

        def lna_write(g0, SA):
            def vw(v, g0=g0, SA=SA):
                v.wait_ge(sC.sem, SA)
                fview = fscr2[:, 0:4 * DM].rearrange("p (t d) -> p t d", t=4)
                gv = sGam[:, 0:DM].rearrange("p (o d) -> p o d", o=1).to_broadcast([128, 4, DM])
                bv = sBet[:, 0:DM].rearrange("p (o d) -> p o d", o=1).to_broadcast([128, 4, DM])
                v.tensor_tensor(fview, fview, gv, ALU.mult)
                DR_V and v.drain()
                v.tensor_tensor(g1[:, g0 * DM:(g0 + 4) * DM].rearrange(
                    "p (t d) -> p t d", t=4), fview, bv, ALU.add)
                DR_V and v.drain()
                vC.inc(v.engine_nop())
            V(vw)
        ln8(g1view, LA, lna_write)

        def lnb_write(g0, SA):
            def vw(v, g0=g0, SA=SA):
                v.wait_ge(sC.sem, SA)
                xv = xres[:, g0 * DM:(g0 + 4) * DM]
                v.tensor_tensor(xv, xv, fscr2[:, 0:4 * DM], ALU.add)
                DR_V and v.drain()
                vC.inc(v.engine_nop())
            V(vw)
        ln8(g1view, LB, lnb_write)
        LNBEND = vC.val

        if KLIM < 7:
            def tail(s):
                for j in range(NT):
                    st.inc(s.dma_start(out=y_out[j * 128:(j + 1) * 128, :],
                                       in_=fscr[:, 0:DM]))
                s.wait_ge(st.sem, st.val)
            Y(tail)

            def tailg(g):
                g.wait_ge(st.sem, st.val)
            G(tailg)
            return nc
        KPTf = KPT[:].bitcast(F32)
        f2bufs = [f2win, Qwin]
        L2 = 6 * NT + 8
        L3 = 6 * NT + 24
        pf_gate = [None, None]
        w1_gate = [None, None]
        f2_gate = [None, None]
        t90_prev = tC.val
        st_hist = [0, 0]
        ln2_sap = {}

        def emit_ln2(fb):
            jt0 = (fb * FBLK) // 128
            TPB = FBLK // 128

            def ln2_src(j, jt0=jt0):
                return xres[:, (jt0 + j) * DM:(jt0 + j + 1) * DM]

            def vsum2(v, jt0=jt0):
                v.reduce_sum(sMu[:, L2:L2 + TPB],
                             xres[:, jt0 * DM:(jt0 + TPB) * DM].rearrange(
                                 "p (t d) -> p t d", t=TPB), axis=AX.X)
                DR_V and v.drain()
                v.tensor_scalar_mul(sMu[:, L2:L2 + TPB], sMu[:, L2:L2 + TPB], 1.0 / DM)
                DR_V and v.drain()
                vC.inc(v.engine_nop())
            V(vsum2)
            VMU = vC.val

            def ssq2(s, jt0=jt0, tp=t90_prev, VMU=VMU, fb=fb):
                if fb == 0:
                    s.wait_ge(vC.sem, LNBEND)
                s.wait_ge(vC.sem, VMU)
                s.wait_ge(tC.sem, tp)
                s.activation(fscr2[:, 0:TPB * DM], xres[:, jt0 * DM:(jt0 + TPB) * DM],
                             AF.Square)
                DR_S and s.drain()
                sC.inc(s.nop())
            S(ssq2)
            SSQ = sC.val

            def vvar2(v, SSQ=SSQ):
                v.wait_ge(sC.sem, SSQ)
                mu4 = sMu[:, L2:L2 + TPB]
                var4 = sMu[:, L2 + 4:L2 + 4 + TPB]
                nb4 = sMu[:, L2 + 12:L2 + 12 + TPB]
                v.reduce_sum(var4, fscr2[:, 0:TPB * DM].rearrange(
                    "p (t d) -> p t d", t=TPB), axis=AX.X)
                DR_V and v.drain()
                v.tensor_scalar_mul(var4, var4, 1.0 / DM)
                DR_V and v.drain()
                v.tensor_tensor(nb4, mu4, mu4, ALU.mult)
                DR_V and v.drain()
                v.tensor_sub(var4, var4, nb4)
                DR_V and v.drain()
                v.tensor_scalar_add(var4, var4, LN_EPS)
                DR_V and v.drain()
                vC.inc(v.engine_nop())
            V(vvar2)
            VV = vC.val

            def srs2(s, VV=VV):
                s.wait_ge(vC.sem, VV)
                var4 = sMu[:, L2 + 4:L2 + 4 + TPB]
                rs4 = sMu[:, L2 + 8:L2 + 8 + TPB]
                s.activation(rs4, var4, AF.Ln)
                DR_S and s.drain()
                s.activation(rs4, rs4, AF.Exp, scale=-0.5)
                DR_S and s.drain()
                sC.inc(s.nop())
            S(srs2)
            SR = sC.val

            def vbias2(v, SR=SR):
                v.wait_ge(sC.sem, SR)
                mu4 = sMu[:, L2:L2 + TPB]
                rs4 = sMu[:, L2 + 8:L2 + 8 + TPB]
                nb4 = sMu[:, L2 + 12:L2 + 12 + TPB]
                v.tensor_tensor(nb4, mu4, rs4, ALU.mult)
                DR_V and v.drain()
                v.tensor_scalar_mul(nb4, nb4, -1.0)
                DR_V and v.drain()
                vC.inc(v.engine_nop())
            V(vbias2)
            VB = vC.val

            def sap2(s, jt0=jt0, VB=VB):
                s.wait_ge(vC.sem, VB)
                for j in range(TPB):
                    s.activation(fscr2[:, j * DM:(j + 1) * DM], ln2_src(j),
                                 AF.Identity,
                                 bias=sMu[:, L2 + 12 + j:L2 + 13 + j],
                                 scale=sMu[:, L2 + 8 + j:L2 + 9 + j])
                s.drain()
                sC.inc(s.nop())
            S(sap2)
            ln2_sap[fb] = sC.val

        emit_ln2(0)
        for fb in range(NFB):
            t0 = fb * FBLK
            jt0 = t0 // 128
            TPB = FBLK // 128
            SAP2 = ln2_sap[fb]

            n = TPB * FC
            tv0, vv0 = tC.val, vC.val

            def t90(t, n=n, tv0=tv0, vv0=vv0, SAP2=SAP2):
                t.wait_ge(sC.sem, SAP2)
                for i in range(n):
                    jj, fc = divmod(i, FC)
                    if i >= 2:
                        t.wait_ge(vC.sem, vv0 + i - 1)
                    else:
                        t.wait_ge(vC.sem, vv0)
                    tC.inc(t.transpose(PB[i % 2][:, 0:128],
                                fscr2[:, jj * DM + fc * 128:jj * DM + fc * 128 + 128], sId[:]))
            T(t90)
            t90_prev = tC.val

            def v90(v, n=n, tv0=tv0):
                for i in range(n):
                    jj, fc = divmod(i, FC)
                    v.wait_ge(tC.sem, tv0 + i + 1)
                    vC.inc(v.tensor_copy(
                        x2T[:, fc * FBLK + jj * 128:fc * FBLK + jj * 128 + 128],
                        PB[i % 2][:, 0:128]))
            V(v90)
            VX2T = vC.val
            if fb + 1 < NFB:
                emit_ln2(fb + 1)

            for hcx in range(HC):
                slot = hcx % 2
                bank = hcx % 2
                sg = w1_gate[slot]

                def yf1(s, hcx=hcx, slot=slot, sg=sg, fb=fb):
                    if fb == 0 and hcx < 2:
                        s.wait_ge(gd.sem, GD_PREP)
                    wait_gate(s, sg)
                    ldw.inc(s.dma_start(
                        out=Wwin[:, slot * FC * 128:(slot + 1) * FC * 128].rearrange(
                            "p (c n) -> p c n", c=FC),
                        in_=s1sc[0:DM, hcx * 128:(hcx + 1) * 128].rearrange(
                            "(c p) n -> p c n", p=128)))
                    s.wait_ge(ldw.sem, ldw.val)
                Y(yf1)
                LV = ldw.val
                bg = pf_gate[bank]

                def tf1(t, hcx=hcx, bank=bank, LV=LV, bg=bg, slot=slot, VX2T=VX2T):
                    t.wait_ge(ldw.sem, LV)
                    t.wait_ge(vC.sem, VX2T)
                    wait_gate(t, bg)
                    for fc in range(FC):
                        t.matmul(PF[bank][:, 0:FBLK],
                                 Wwin[:, slot * FC * 128 + fc * 128:slot * FC * 128 + fc * 128 + 128],
                                 x2T[:, fc * FBLK:fc * FBLK + FBLK],
                                 start=(fc == 0), stop=(fc == FC - 1))
                    t.drain()
                    tC.inc(t.nop())
                T(tf1)
                TV = tC.val
                w1_gate[slot] = (tC, TV)

                def sf1(s, hcx=hcx, bank=bank, TV=TV, fb=fb):
                    if fb == 0 and hcx < 2:
                        s.wait_ge(vC.sem, VPREP)
                    s.wait_ge(tC.sem, TV)
                    s.activation(g1[:, hcx * FBLK:(hcx + 1) * FBLK],
                                 PF[bank][:, 0:FBLK],
                                 AF.Gelu, bias=sB1[:, hcx:hcx + 1], scale=sA1[:, hcx:hcx + 1])
                    s.drain()
                    sC.inc(s.nop())
                S(sf1)
                pf_gate[bank] = (sC, sC.val)
            GF1 = sC.val

            for e in range(FC):
                sl2 = e % 2
                bank = e % 2
                buf = f2bufs[sl2]
                sg = f2_gate[sl2]

                def yf2(s, e=e, buf=buf, sg=sg, fb=fb):
                    if fb == 0 and e < 2:
                        s.wait_ge(gd.sem, GD_PREP)
                    wait_gate(s, sg)
                    ldw.inc(s.dma_start(
                        out=buf[:, 0:HC * 128].rearrange("p (c n) -> p c n", c=HC),
                        in_=s2sc[0:DF, e * 128:(e + 1) * 128].rearrange(
                            "(c p) n -> p c n", p=128)))
                    s.wait_ge(ldw.sem, ldw.val)
                Y(yf2)
                LV = ldw.val
                bg = pf_gate[bank]

                def tf2(t, e=e, buf=buf, bank=bank, LV=LV, bg=bg, GF1=GF1):
                    t.wait_ge(ldw.sem, LV)
                    t.wait_ge(sC.sem, GF1)
                    wait_gate(t, bg)
                    for hcx in range(HC):
                        t.matmul(PF[bank][:, 0:FBLK],
                                 buf[:, hcx * 128:hcx * 128 + 128],
                                 g1[:, hcx * FBLK:hcx * FBLK + FBLK],
                                 start=(hcx == 0), stop=(hcx == HC - 1))
                    t.drain()
                    tC.inc(t.nop())
                T(tf2)
                TV = tC.val
                f2_gate[sl2] = (tC, TV)

                def sf2(s, e=e, bank=bank, TV=TV, fb=fb):
                    if fb == 0 and e < 2:
                        s.wait_ge(vC.sem, VPREP)
                    s.wait_ge(tC.sem, TV)
                    s.activation(ffnO[:, e * FBLK:(e + 1) * FBLK],
                                 PF[bank][:, 0:FBLK],
                                 AF.Identity, bias=sB2[:, e:e + 1], scale=sA2[:, e:e + 1])
                    s.drain()
                    sC.inc(s.nop())
                S(sf2)
                pf_gate[bank] = (sC, sC.val)
            GF2 = sC.val

            n = TPB * FC
            tv0, vv0 = tC.val, vC.val

            def t91(t, n=n, tv0=tv0, vv0=vv0, GF2=GF2):
                t.wait_ge(sC.sem, GF2)
                for i in range(n):
                    jj, fc = divmod(i, FC)
                    if i >= 2:
                        t.wait_ge(vC.sem, vv0 + i - 1)
                    else:
                        t.wait_ge(vC.sem, vv0)
                    tC.inc(t.transpose(PB[i % 2][:, 0:128],
                                ffnO[:, fc * FBLK + jj * 128:fc * FBLK + jj * 128 + 128], sId[:]))
            T(t91)

            def v91(v, n=n, tv0=tv0):
                for i in range(n):
                    jj, fc = divmod(i, FC)
                    v.wait_ge(tC.sem, tv0 + i + 1)
                    vC.inc(v.tensor_copy(
                        g1[:, jj * DM + fc * 128:jj * DM + fc * 128 + 128],
                        PB[i % 2][:, 0:128]))
            V(v91)
            V91 = vC.val

            def vsum3(v, V91=V91):
                v.wait_ge(vC.sem, V91)
                v.reduce_sum(sMu[:, L3:L3 + TPB],
                             g1[:, 0:TPB * DM].rearrange("p (t d) -> p t d", t=TPB),
                             axis=AX.X)
                DR_V and v.drain()
                v.tensor_scalar_mul(sMu[:, L3:L3 + TPB], sMu[:, L3:L3 + TPB], 1.0 / DM)
                DR_V and v.drain()
                vC.inc(v.engine_nop())
            V(vsum3)
            VMU3 = vC.val

            HTP = TPB // 2
            SSQH = []
            for hh in range(2):
                def ssq3(s, VMU3=VMU3, hh=hh):
                    if hh == 0:
                        s.wait_ge(vC.sem, VMU3)
                    else:
                        s.wait_ge(vC.sem, vC.val)
                    s.activation(fscr[:, 0:HTP * DM],
                                 g1[:, hh * HTP * DM:(hh + 1) * HTP * DM], AF.Square)
                    DR_S and s.drain()
                    sC.inc(s.nop())
                S(ssq3)
                SSQ3 = sC.val
                SSQH.append(SSQ3)

                def vred3(v, SSQ3=SSQ3, hh=hh):
                    v.wait_ge(sC.sem, SSQ3)
                    v.reduce_sum(sMu[:, L3 + 4 + hh * HTP:L3 + 4 + (hh + 1) * HTP],
                                 fscr[:, 0:HTP * DM].rearrange(
                                     "p (t d) -> p t d", t=HTP), axis=AX.X)
                    DR_V and v.drain()
                    vC.inc(v.engine_nop())
                V(vred3)

            def vvar3(v):
                mu4 = sMu[:, L3:L3 + TPB]
                var4 = sMu[:, L3 + 4:L3 + 4 + TPB]
                nb4 = sMu[:, L3 + 12:L3 + 12 + TPB]
                v.tensor_scalar_mul(var4, var4, 1.0 / DM)
                DR_V and v.drain()
                v.tensor_tensor(nb4, mu4, mu4, ALU.mult)
                DR_V and v.drain()
                v.tensor_sub(var4, var4, nb4)
                DR_V and v.drain()
                v.tensor_scalar_add(var4, var4, LN_EPS)
                DR_V and v.drain()
                vC.inc(v.engine_nop())
            V(vvar3)
            VV3 = vC.val

            def srs3(s, VV3=VV3):
                s.wait_ge(vC.sem, VV3)
                var4 = sMu[:, L3 + 4:L3 + 4 + TPB]
                rs4 = sMu[:, L3 + 8:L3 + 8 + TPB]
                s.activation(rs4, var4, AF.Ln)
                DR_S and s.drain()
                s.activation(rs4, rs4, AF.Exp, scale=-0.5)
                DR_S and s.drain()
                sC.inc(s.nop())
            S(srs3)
            SR3 = sC.val

            def vbias3(v, SR3=SR3):
                v.wait_ge(sC.sem, SR3)
                mu4 = sMu[:, L3:L3 + TPB]
                rs4 = sMu[:, L3 + 8:L3 + 8 + TPB]
                nb4 = sMu[:, L3 + 12:L3 + 12 + TPB]
                v.tensor_tensor(nb4, mu4, rs4, ALU.mult)
                DR_V and v.drain()
                v.tensor_scalar_mul(nb4, nb4, -1.0)
                DR_V and v.drain()
                vC.inc(v.engine_nop())
            V(vbias3)
            VB3 = vC.val

            for j in range(TPB):
                jglob = jt0 + j
                c = j % 2
                stv2 = st_hist[-2]

                def sap3(s, j=j, c=c, VB3=VB3, stv2=stv2):
                    s.wait_ge(vC.sem, VB3)
                    if stv2 > 0:
                        s.wait_ge(st.sem, stv2)
                    s.activation(KPTf[:, c * DM:(c + 1) * DM], g1[:, j * DM:(j + 1) * DM],
                                 AF.Identity,
                                 bias=sMu[:, L3 + 12 + j:L3 + 13 + j],
                                 scale=sMu[:, L3 + 8 + j:L3 + 9 + j])
                    s.drain()
                    sC.inc(s.nop())
                S(sap3)
                SA = sC.val

                def vadd3(v, jglob=jglob, c=c, SA=SA):
                    v.wait_ge(sC.sem, SA)
                    vC.inc(v.tensor_tensor(KPTf[:, c * DM:(c + 1) * DM],
                                           xres[:, jglob * DM:(jglob + 1) * DM],
                                           KPTf[:, c * DM:(c + 1) * DM], ALU.add))
                V(vadd3)
                VF = vC.val

                def ystore(s, jglob=jglob, c=c, VF=VF):
                    s.wait_ge(vC.sem, VF)
                    st.inc(s.dma_start(out=y_out[jglob * 128:(jglob + 1) * 128, :],
                                       in_=KPTf[:, c * DM:(c + 1) * DM]))
                    s.wait_ge(st.sem, st.val)
                Y(ystore)
                st_hist.append(st.val)

        def yfin(s):
            s.wait_ge(st.sem, st.val)
        Y(yfin)

        def gfin(g):
            g.wait_ge(st.sem, st.val)
        G(gfin)
    return nc


def kernel(**inputs):
    cfg = Cfg()
    x = np.ascontiguousarray(np.asarray(inputs["L0_attn_x"], np.float32))
    B, T, DM = x.shape
    TLOC = T // 2
    assert TLOC == cfg.TLOC and DM == cfg.DM

    Wq = np.ascontiguousarray(np.asarray(inputs["L0_attn_W_q"], np.float32))
    Wk = np.ascontiguousarray(np.asarray(inputs["L0_attn_W_k"], np.float32))
    Wv = np.ascontiguousarray(np.asarray(inputs["L0_attn_W_v"], np.float32))
    Wo = np.ascontiguousarray(np.asarray(inputs["L0_attn_W_o"], np.float32))
    w_i = np.asarray(inputs["L0_attn_w_i"], np.float32)
    w_f = np.asarray(inputs["L0_attn_w_f"], np.float32)
    b_i = np.asarray(inputs["L0_attn_b_i"], np.float32)
    b_f = np.asarray(inputs["L0_attn_b_f"], np.float32)
    b_o = np.asarray(inputs["L0_attn_b_o"], np.float32)
    gamma = np.asarray(inputs["L0_attn_gamma"], np.float32)
    beta = np.asarray(inputs["L0_attn_beta"], np.float32)
    w1 = np.asarray(inputs["L0_ffn_w1"], np.float32)
    b1 = np.asarray(inputs["L0_ffn_b1"], np.float32)
    w2 = np.asarray(inputs["L0_ffn_w2"], np.float32)
    b2 = np.asarray(inputs["L0_ffn_b2"], np.float32)

    FC, HC = cfg.FC, cfg.HC
    Wif = np.ascontiguousarray(np.concatenate([w_i, w_f], axis=1))
    b_if = np.ascontiguousarray(np.stack([b_i, b_f], axis=1))
    b_o_c = np.ascontiguousarray(b_o.reshape(FC, 128).T)
    gamma_r = np.ascontiguousarray(np.tile(gamma[None, :], (128, 1)))
    beta_r = np.ascontiguousarray(np.tile(beta[None, :], (128, 1)))
    w1T = np.ascontiguousarray(w1.T)
    w2T = np.ascontiguousarray(w2.T)
    b1_c = np.ascontiguousarray(b1.reshape(HC, 128).T)
    b2_c = np.ascontiguousarray(b2.reshape(FC, 128).T)
    import ml_dtypes
    identity = np.eye(128, dtype=ml_dtypes.bfloat16)
    maskT = np.triu(np.ones((128, 128), np.float32)).astype(ml_dtypes.bfloat16)

    groups = [[2 * i, 2 * i + 1] for i in range(cfg.NCORE // 2)]
    nc = build_nc(cfg, groups)

    in_maps = []
    for c in range(cfg.NCORE):
        bidx, half = c // 2, c % 2
        xs = np.ascontiguousarray(x[bidx, half * TLOC:(half + 1) * TLOC, :])
        in_maps.append(dict(
            x=xs, Wq=Wq, Wk=Wk, Wv=Wv, Wo=Wo, Wif=Wif, b_if=b_if, b_o_c=b_o_c,
            gamma_r=gamma_r, beta_r=beta_r, w1T=w1T, w1h=w1, w2T=w2T, w2h=w2,
            b1_c=b1_c, b2_c=b2_c, ident=identity, maskT=maskT,
            isr=np.full((128, 1), float(half), np.float32),
        ))

    res = run_bass_kernel_spmd(nc, in_maps, core_ids=list(range(cfg.NCORE)))
    out = np.zeros((B, T, DM), np.float32)
    for c in range(cfg.NCORE):
        bidx, half = c // 2, c % 2
        out[bidx, half * TLOC:(half + 1) * TLOC, :] = res.results[c]["y"]
    return out


# version 33
# speedup vs baseline: 1.0067x; 1.0067x over previous
import os
import numpy as np
from contextlib import ExitStack

KLIM = float(os.environ.get("KLIM", "99"))
KSUB = float(os.environ.get("KSUB", "99"))
MMSAFE = int(os.environ.get("MMSAFE", "2"))
DR_S = int(os.environ.get("DR_S", "1"))
DR_V = int(os.environ.get("DR_V", "1"))

import concourse.bass as bass
import concourse.mybir as mybir
from concourse.bass_utils import run_bass_kernel_spmd

F32 = mybir.dt.float32
BF16 = mybir.dt.bfloat16
AF = mybir.ActivationFunctionType
ALU = mybir.AluOpType
AX = mybir.AxisListType

LN_EPS = 1e-5
NEG_BIG = -1.0e30


class Cfg:
    def __init__(self, DM=1024, H=16, DF=4096, TLOC=2048, BLK=512, NCORE=8):
        self.DM, self.H, self.HD, self.DF = DM, H, 64, DF
        assert H * 64 == DM
        self.TLOC, self.L, self.BLK = TLOC, 128, BLK
        self.NBLK = TLOC // BLK
        self.NC = TLOC // 128
        self.CPB = BLK // 128
        self.FC = DM // 128
        self.HC = DF // 128
        self.NT = TLOC // 128
        self.HP = H // 2
        self.NCORE = NCORE
        self.FBLK = min(512, TLOC)
        self.NFB = TLOC // self.FBLK


def build_nc(cfg: Cfg, groups):
    nc = bass.Bass("TRN2", target_bir_lowering=False)
    DM, H, DF, TLOC = cfg.DM, cfg.H, cfg.DF, cfg.TLOC
    L, NC, BLK, NBLK, CPB = cfg.L, cfg.NC, cfg.BLK, cfg.NBLK, cfg.CPB
    FC, HC, NT, HP = cfg.FC, cfg.HC, cfg.NT, cfg.HP
    FBLK, NFB = cfg.FBLK, cfg.NFB
    SL = 65
    NSL = CPB + 1
    NBRAW = max(1, HP // 4)
    NBDEN = max(1, H // 8)
    PPB = HP // 2

    dt = nc.dram_tensor
    x_in = dt("x", [TLOC, DM], F32, kind="ExternalInput").ap()
    Wq = dt("Wq", [DM, DM], F32, kind="ExternalInput").ap()
    Wk = dt("Wk", [DM, DM], F32, kind="ExternalInput").ap()
    Wv = dt("Wv", [DM, DM], F32, kind="ExternalInput").ap()
    Wo = dt("Wo", [DM, DM], F32, kind="ExternalInput").ap()
    Wif_in = dt("Wif", [DM, 32], F32, kind="ExternalInput").ap()
    b_if = dt("b_if", [16, 2], F32, kind="ExternalInput").ap()
    b_o_c = dt("b_o_c", [128, FC], F32, kind="ExternalInput").ap()
    gamma_r = dt("gamma_r", [128, DM], F32, kind="ExternalInput").ap()
    beta_r = dt("beta_r", [128, DM], F32, kind="ExternalInput").ap()
    w1T_in = dt("w1T", [DM, DF], F32, kind="ExternalInput").ap()
    w1h_in = dt("w1h", [DF, DM], F32, kind="ExternalInput").ap()
    w2T_in = dt("w2T", [DF, DM], F32, kind="ExternalInput").ap()
    w2h_in = dt("w2h", [DM, DF], F32, kind="ExternalInput").ap()
    b1_c = dt("b1_c", [128, HC], F32, kind="ExternalInput").ap()
    b2_c = dt("b2_c", [128, FC], F32, kind="ExternalInput").ap()
    ident = dt("ident", [128, 128], BF16, kind="ExternalInput").ap()
    maskT_in = dt("maskT", [128, 128], BF16, kind="ExternalInput").ap()
    isr_in = dt("isr", [128, 1], F32, kind="ExternalInput").ap()
    y_out = dt("y", [TLOC, DM], F32, kind="ExternalOutput").ap()

    NROW = 10 * NC + 2 * NBLK * NSL
    s1sc = dt("s1sc", [DM, DF], BF16, kind="Internal")
    s2sc = dt("s2sc", [DF, DM], BF16, kind="Internal")
    rowsc = dt("rowsc", [16, NROW], F32, kind="Internal")
    gsc = dt("gsc", [16, TLOC], BF16, kind="Internal")
    prow = dt("prow", [TLOC, 16], F32, kind="Internal")
    cm_in = dt("cm_in", [16, 1], F32, kind="Internal")
    cm_out = dt("cm_out", [16, 1], F32, kind="Internal")
    cs_in = dt("cs_in", [128, HP * SL], F32, kind="Internal")
    cs_out = dt("cs_out", [128, HP * SL], F32, kind="Internal")
    Qhbm = dt("Qhbm", [128, FC * TLOC], BF16, kind="Internal")
    Ohbm = dt("Ohbm", [128, FC * TLOC], BF16, kind="Internal")
    N0hbm = dt("N0hbm", [128, HP * TLOC], BF16, kind="Internal")

    with ExitStack() as ctx:
        def sbt(name, shape, dtyp):
            return ctx.enter_context(nc.sbuf_tensor(name, shape, dtyp))

        def pst(name, shape, dtyp):
            return ctx.enter_context(nc.psum_tensor(name, shape, dtyp))

        xres = sbt("xres", [128, NT * DM], BF16)
        xT = sbt("xT", [128, FC * BLK], BF16)
        SHR = sbt("SHR", [128, max(FC * BLK, 2 * H * L, CPB * H * 64)], BF16)
        Qwin = sbt("Qwin", [128, FC * BLK], BF16)
        N0stg = sbt("N0stg", [128, HP * 128], BF16)
        Kfm = sbt("Kfm", [128, FC * BLK], BF16)
        Vfm = sbt("Vfm", [128, FC * BLK], BF16)
        KPT = sbt("KPT", [128, max(CPB * DM, 4 * DM)], BF16)
        VTp = SHR
        BvB = SHR[:, 0:H * L]
        BkB = SHR[:, H * L:2 * H * L]
        Wwin = sbt("Wwin", [128, 2 * FC * 128], BF16)
        rawZ = sbt("rawZ", [128, HP * SL * NSL], BF16)
        scanZ = sbt("scanZ", [128, HP * SL * NSL], BF16)
        scanS = sbt("scanS", [128, HP * SL * CPB], BF16)
        sXcC = sbt("sXcC", [128, HP * SL], BF16)
        gI = sbt("gI", [16, TLOC], F32)
        gF = sbt("gF", [16, TLOC], F32)
        gB = sbt("gB", [16, TLOC], F32)
        gMi = sbt("gMi", [16, TLOC], F32)
        gU = gI
        gM = gF
        GrowT = sbt("GrowT", [16, TLOC], BF16)
        gSc = sbt("gSc", [16, NROW], F32)
        gTf = sbt("gTf", [128, NC * 16], F32)
        gTb = sbt("gTb", [128, NC * 16], BF16)
        arep = sbt("arep", [128, SL * NSL], F32)
        areptmp = sbt("areptmp", [128, HP * NSL], F32)
        rrep = sbt("rrep", [128, HP * NC], F32)
        rhorep = sbt("rhorep", [128, HP * NC], F32)
        cpxrep = sbt("cpxrep", [128, HP * NC], F32)
        Grep = sbt("Grep", [128, TLOC], BF16)
        dslab = sbt("dslab", [128, 1024], BF16)
        sId = sbt("sId", [128, 128], BF16)
        sMk = sbt("sMk", [128, 128], BF16)
        sOne = sbt("sOne", [128, 1], BF16)
        sIsr = sbt("sIsr", [128, 1], F32)
        sBif = sbt("sBif", [16, 2], F32)
        sBo = sbt("sBo", [128, FC], F32)
        sGam = sbt("sGam", [128, DM], BF16)
        sBet = sbt("sBet", [128, DM], BF16)
        sA1 = sbt("sA1", [128, HC], F32)
        sA2 = sbt("sA2", [128, FC], F32)
        sB1 = sbt("sB1", [128, HC], F32)
        sB2 = sbt("sB2", [128, FC], F32)
        sMu = sbt("sMu", [128, 6 * NT + 40], F32)
        sIdF = sbt("sIdF", [16, 16], F32)
        sSend = sbt("sSend", [128, HP * SL], F32)
        sXsum = sbt("sXsum", [128, HP * SL], F32)
        fscr = sbt("fscr", [128, 2 * DM], F32)
        fscr2 = sbt("fscr2", [128, max(DF // 2, CPB * DM, TLOC, (FBLK // 128) * DM)], BF16)
        g1 = sbt("g1", [128, max(HC * FBLK, NT * DM, 14 * DM)], BF16)
        x2T = xT if FC * BLK >= FC * FBLK else sbt("x2T", [128, FC * FBLK], BF16)
        f2win = Vfm if FC * BLK >= HC * 128 else sbt("f2win", [128, HC * 128], BF16)
        ffnO = Kfm if FC * BLK >= max(FC * FBLK, (FBLK // 128) * DM) else sbt("ffnO", [128, max(FC * FBLK, (FBLK // 128) * DM)], BF16)

        nc.dbg = dict(gI=gI[:], gF=gF[:], gB=gB[:], gMi=gMi[:], gSc=gSc[:],
                      Wwin=Wwin[:], xT=xT[:], fscr=fscr[:], fscr2=fscr2[:],
                      sBif=sBif[:])

        PFa = pst("PFa", [128, 512], F32)
        PFb = pst("PFb", [128, 512], F32)
        PFc = pst("PFc", [128, 512], F32)
        PFd = pst("PFd", [128, 512], F32)
        PBa = pst("PBa", [128, 1024], BF16)
        PBb = pst("PBb", [128, 1024], BF16)
        PXa = pst("PXa", [128, 512], F32)
        PXb = pst("PXb", [128, 512], F32)
        PF = [PFa, PFb, PFc, PFd]
        PB = [PBa, PBb]
        PX = [PXa, PXb]

        class Ctr:
            def __init__(self, name, step=1):
                self.sem = ctx.enter_context(nc.semaphore(name))
                self.val = 0
                self.step = step

            def inc(self, bi, n=None):
                n = self.step if n is None else n
                self.val += n
                return bi.then_inc(self.sem, n)

        dfin = gMi
        den0 = gB.bitcast(BF16)[:, 0:TLOC]
        denXr = gB.bitcast(BF16)[:, TLOC:2 * TLOC]
        Grow = GrowT
        ld = Ctr("ld", 16)
        ldw = Ctr("ldw", 16)
        gd = Ctr("gd", 16)
        cc = Ctr("cc", 1)
        tC = Ctr("tC", 1)
        vC = Ctr("vC", 1)
        sC = Ctr("sC", 1)
        bT = Ctr("bT", 1)
        bV = Ctr("bV", 1)
        bS = Ctr("bS", 1)
        st = Ctr("st", 16)

        block = ctx.enter_context(nc.Block())

        def T(fn):
            @block.tensor
            def _(t):
                fn(t)

        def V(fn):
            @block.vector
            def _(v):
                fn(v)

        def S(fn):
            @block.scalar
            def _(s):
                fn(s)

        def Y(fn):
            @block.sync
            def _(s):
                fn(s)

        def G(fn):
            @block.gpsimd
            def _(g):
                fn(g)

        mmrb = [None]

        def mmn(t, rb):
            if MMSAFE == 1:
                t.drain()
                mmrb[0] = rb
                return
            if rb is None:
                mmrb[0] = None
                return
            if mmrb[0] is not None and mmrb[0] != rb:
                t.drain() if MMSAFE == 2 else t.nop()
            mmrb[0] = rb

        def barrier():
            tv, vv, sv = bT.val + 1, bV.val + 1, bS.val + 1

            def tf(t):
                t.drain()
                bT.inc(t.nop())
                t.wait_ge(bV.sem, vv)
                t.wait_ge(bS.sem, sv)

            def vf(v):
                DR_V and v.drain()
                bV.inc(v.engine_nop())
                v.wait_ge(bT.sem, tv)
                v.wait_ge(bS.sem, sv)

            def sf(s):
                DR_S and s.drain()
                bS.inc(s.nop())
                s.wait_ge(bT.sem, tv)
                s.wait_ge(bV.sem, vv)
            T(tf)
            V(vf)
            S(sf)

        def loads(s):
            for ap_, src in [(sId[:], ident), (sMk[:], maskT_in), (sIsr[:], isr_in),
                             (sBif[:], b_if), (sBo[:], b_o_c),
                             (sB1[:], b1_c), (sB2[:], b2_c)]:
                ld.inc(s.dma_start(out=ap_, in_=src))
        Y(loads)

        def loadx(g):
            for j in range(NT):
                gd.inc(g.dma_start(out=xres[:, j * DM:(j + 1) * DM],
                                   in_=x_in[j * 128:(j + 1) * 128, :]))
            g.wait_ge(gd.sem, gd.val)
        G(loadx)
        LDA0 = ld.val
        LDX = gd.val

        def gbl(s):
            ld.inc(s.dma_start(out=fscr[:, DM:2 * DM], in_=gamma_r))
        Y(gbl)
        LDG = ld.val

        def gbc(v):
            v.wait_ge(ld.sem, LDG)
            v.tensor_copy(sGam[:, 0:DM], fscr[:, DM:2 * DM])
            DR_V and v.drain()
            vC.inc(v.engine_nop())
        V(gbc)
        VGB = vC.val

        def gbl2(s):
            s.wait_ge(vC.sem, VGB)
            ld.inc(s.dma_start(out=fscr[:, DM:2 * DM], in_=beta_r))
        Y(gbl2)
        LDB = ld.val

        def gbc2(v):
            v.wait_ge(ld.sem, LDB)
            v.tensor_copy(sBet[:, 0:DM], fscr[:, DM:2 * DM])
            DR_V and v.drain()
            vC.inc(v.engine_nop())
        V(gbc2)
        LDA = ld.val
        VGB2 = vC.val

        def init_consts(v):
            v.wait_ge(ld.sem, LDG)
            v.tensor_copy(sIdF[:], sId[0:16, 0:16])
            DR_V and v.drain()
            v.memset(sOne[:], 1.0)
            v.memset(gI[:], 0.0)
            v.memset(gF[:], 0.0)
            v.memset(gB[:], 0.0)
            v.memset(gMi[:], 0.0)
            v.memset(fscr[:, 0:512], 0.0)
            v.memset(fscr2[:, 0:1024], 0.0)
            DR_V and v.drain()
            for pb_ in (PFa, PFb, PFc, PFd, PXa, PXb):
                v.tensor_copy(pb_[:], fscr[:, 0:512])
            v.memset(gSc[:], 0.0)
            v.memset(rawZ[:], 0.0)
            v.memset(scanZ[:], 0.0)
            DR_V and v.drain()
            vC.inc(v.engine_nop())
        V(init_consts)
        VIC = vC.val

        def wifload(s):
            s.wait_ge(vC.sem, VIC)
            for fc in range(FC):
                ld.inc(s.dma_start(out=fscr[:, fc * 32:(fc + 1) * 32],
                                   in_=Wif_in[fc * 128:(fc + 1) * 128, :]))
        Y(wifload)
        LDWIF = ld.val

        def init_pb(t):
            t.wait_ge(vC.sem, VIC)
            t.wait_ge(ld.sem, LDG)
            for k in range(8):
                t.transpose(PBa[:, k * 128:(k + 1) * 128], fscr2[:, 0:128], sId[:])
                t.transpose(PBb[:, k * 128:(k + 1) * 128], fscr2[:, 0:128], sId[:])
            t.drain()
            tC.inc(t.nop())
        T(init_pb)

        def ln1_sums(v):
            v.wait_ge(ld.sem, LDG)
            v.wait_ge(gd.sem, LDX)
            for j in range(NT):
                v.reduce_sum(sMu[:, j:j + 1], xres[:, j * DM:(j + 1) * DM], axis=AX.X)
            DR_V and v.drain()
            v.tensor_scalar_mul(sMu[:, 0:NT], sMu[:, 0:NT], 1.0 / DM)
            DR_V and v.drain()
            vC.inc(v.engine_nop())
        V(ln1_sums)

        def ln1_sq(s):
            s.wait_ge(ld.sem, LDA)
            s.wait_ge(gd.sem, LDX)
            s.wait_ge(vC.sem, VGB2)
            for j in range(NT):
                s.activation(fscr[:, DM:2 * DM], xres[:, j * DM:(j + 1) * DM], AF.Square,
                             accum_out=sMu[:, NT + j:NT + j + 1])
                DR_S and s.drain()
            sC.inc(s.nop())
        S(ln1_sq)
        SQ1 = sC.val

        def ln1_var(v):
            v.wait_ge(sC.sem, SQ1)
            mu, ex2 = sMu[:, 0:NT], sMu[:, NT:2 * NT]
            var = sMu[:, 4 * NT:5 * NT]
            v.tensor_scalar_mul(ex2, ex2, 1.0 / DM)
            DR_V and v.drain()
            v.tensor_tensor(var, mu, mu, ALU.mult)
            DR_V and v.drain()
            v.tensor_sub(var, ex2, var)
            DR_V and v.drain()
            v.tensor_scalar_add(var, var, LN_EPS)
            DR_V and v.drain()
            vC.inc(v.engine_nop())
        V(ln1_var)
        VV1 = vC.val

        def ln1_rs(s):
            s.wait_ge(vC.sem, VV1)
            var, rs = sMu[:, 4 * NT:5 * NT], sMu[:, 3 * NT:4 * NT]
            s.activation(rs, var, AF.Ln)
            DR_S and s.drain()
            s.activation(rs, rs, AF.Exp, scale=-0.5)
            DR_S and s.drain()
            sC.inc(s.nop())
        S(ln1_rs)
        SQ2 = sC.val

        def ln1_bias(v):
            v.wait_ge(sC.sem, SQ2)
            bias = sMu[:, 2 * NT:3 * NT]
            v.tensor_tensor(bias, sMu[:, 0:NT], sMu[:, 3 * NT:4 * NT], ALU.mult)
            DR_V and v.drain()
            v.tensor_scalar_mul(bias, bias, -1.0)
            DR_V and v.drain()
            vC.inc(v.engine_nop())
        V(ln1_bias)

        def ln1_apply_block(s, b):
            for jj in range(CPB):
                j = b * CPB + jj
                s.activation(fscr2[:, jj * DM:(jj + 1) * DM],
                             xres[:, j * DM:(j + 1) * DM], AF.Identity,
                             bias=sMu[:, 2 * NT + j:2 * NT + j + 1],
                             scale=sMu[:, 3 * NT + j:3 * NT + j + 1])

        def xpose_block_emit(pb_free=None):
            n = CPB * FC
            tv0, vv0 = tC.val, vC.val
            if pb_free is None:
                pb_free = vv0

            def tfn(t, pb_free=pb_free):
                t.wait_ge(sC.sem, sC.val)
                for i in range(n):
                    jj, fc = divmod(i, FC)
                    if i >= 2:
                        t.wait_ge(vC.sem, vv0 + i - 1)
                    else:
                        t.wait_ge(vC.sem, pb_free)
                    tC.inc(t.transpose(PB[i % 2][:, 0:128],
                                fscr2[:, jj * DM + fc * 128:jj * DM + fc * 128 + 128],
                                sId[:]))

            def vfn(v):
                for i in range(n):
                    jj, fc = divmod(i, FC)
                    v.wait_ge(tC.sem, tv0 + i + 1)
                    vC.inc(v.tensor_copy(
                        xT[:, fc * BLK + jj * 128:fc * BLK + jj * 128 + 128],
                        PB[i % 2][:, 0:128]))
            T(tfn)
            V(vfn)

        if KLIM < 1:
            def tail1(s):
                for j in range(NT):
                    st.inc(s.dma_start(out=y_out[j * 128:(j + 1) * 128, :],
                                       in_=fscr[:, 0:DM]))
                s.wait_ge(st.sem, st.val)
            Y(tail1)

            def tailg1(g):
                g.wait_ge(st.sem, st.val)
            G(tailg1)
            return nc
        def wif_cast(v):
            v.wait_ge(ld.sem, LDWIF)
            v.tensor_copy(Wwin[:, 0:FC * 32], fscr[:, 0:FC * 32])
            DR_V and v.drain()
            vC.inc(v.engine_nop())
        V(wif_cast)

        for b in range(NBLK):
            tb = b * BLK

            def sA_(s, b=b):
                s.wait_ge(vC.sem, vC.val)
                s.wait_ge(tC.sem, tC.val)
                ln1_apply_block(s, b)
                DR_S and s.drain()
                sC.inc(s.nop())
            S(sA_)
            xpose_block_emit()

            def tif(t, b=b):
                t.wait_ge(vC.sem, vC.val)
                wid = min(BLK, 512)
                for q in range(BLK // wid):
                    for fc in range(FC):
                        t.matmul(PXa[0:16, q * wid:q * wid + wid],
                                 Wwin[0:128, fc * 32:fc * 32 + 16],
                                 xT[:, fc * BLK + q * wid:fc * BLK + (q + 1) * wid],
                                 start=(fc == 0), stop=(fc == FC - 1))
                    for fc in range(FC):
                        t.matmul(PXb[0:16, q * wid:q * wid + wid],
                                 Wwin[0:128, fc * 32 + 16:fc * 32 + 32],
                                 xT[:, fc * BLK + q * wid:fc * BLK + (q + 1) * wid],
                                 start=(fc == 0), stop=(fc == FC - 1))
                t.drain()
                tC.inc(t.nop())
            T(tif)
            TIF = tC.val

            def vif(v, b=b, tb=tb, TIF=TIF):
                v.wait_ge(tC.sem, TIF)
                v.tensor_scalar_add(gI[:, tb:tb + BLK], PXa[0:16, 0:BLK], sBif[0:16, 0:1])
                DR_V and v.drain()
                v.tensor_scalar_add(gF[:, tb:tb + BLK], PXb[0:16, 0:BLK], sBif[0:16, 1:2])
                DR_V and v.drain()
                binit = 0.0 if b == 0 else gB[:, tb - 1:tb]
                v.tensor_tensor_scan(gB[:, tb:tb + BLK], gF[:, tb:tb + BLK],
                                     gSc[:, 0:1].to_broadcast([16, BLK]),
                                     binit, ALU.add, ALU.add)
                DR_V and v.drain()
                minit = NEG_BIG if b == 0 else gMi[:, tb - 1:tb]
                v.tensor_tensor_scan(gMi[:, tb:tb + BLK], gF[:, tb:tb + BLK],
                                     gI[:, tb:tb + BLK], minit, ALU.add, ALU.max)
                DR_V and v.drain()
                vC.inc(v.engine_nop())
            V(vif)

        if KLIM < 2:
            def tail2(s):
                s.wait_ge(vC.sem, vC.val)
                for j in range(NT):
                    st.inc(s.dma_start(out=y_out[j * 128:(j + 1) * 128, :],
                                       in_=fscr[:, 0:DM]))
                s.wait_ge(st.sem, st.val)
            Y(tail2)

            def tailg2(g):
                g.wait_ge(st.sem, st.val)
            G(tailg2)
            return nc
        S2END, V2END, T2END = sC.val, vC.val, tC.val
        def msend_prep(v):
            v.tensor_tensor(gSc[:, 2:3], gMi[:, TLOC - 1:TLOC], gB[:, TLOC - 1:TLOC], ALU.max)
            DR_V and v.drain()
            vC.inc(v.engine_nop())
        V(msend_prep)
        VMS = vC.val

        def msend(g):
            g.wait_ge(vC.sem, VMS)
            gd.inc(g.dma_start(out=cm_in[:, :], in_=gSc[:, 2:3]))
            g.wait_ge(gd.sem, gd.val)
            cc.inc(g.collective_compute(
                "AllReduce", ALU.add, replica_groups=groups,
                ins=[cm_in.ap().opt()], outs=[cm_out.ap().opt()]))
            g.wait_ge(cc.sem, cc.val)
            gd.inc(g.dma_start(out=gSc[:, 3:4], in_=cm_out[:, :]))
            g.wait_ge(gd.sem, gd.val)
        G(msend)
        GD_M = gd.val

        if KLIM < 2.5:
            def tail25(s):
                for j in range(NT):
                    st.inc(s.dma_start(out=y_out[j * 128:(j + 1) * 128, :],
                                       in_=fscr[:, 0:DM]))
                s.wait_ge(st.sem, st.val)
            Y(tail25)

            def tailg25(g):
                g.wait_ge(st.sem, st.val)
            G(tailg25)
            return nc
        BP, MI, MB = 8, 8 + NC, 8 + 2 * NC
        EB = 8 + 3 * NC
        r_, a_, rho_, cpx_ = EB, EB + NC, EB + 2 * NC, EB + 3 * NC
        S1_ = EB + 4 * NC
        AB = S1_ + 2 * NC

        def cview(rowap):
            return rowap.rearrange("p (c o) -> p c o", o=1).to_broadcast([16, NC, L])

        def lview(rowap):
            return rowap.rearrange("p (c l) -> p c l", l=L)

        def gatefin(v):
            v.wait_ge(gd.sem, GD_M)
            v.tensor_sub(gSc[:, 4:5], gSc[:, 3:4], gSc[:, 2:3])
            DR_V and v.drain()
            v.tensor_tensor(gSc[:, 4:5], gSc[:, 4:5], sIsr[0:16, 0:1], ALU.mult)
            DR_V and v.drain()
            v.tensor_scalar_add(gM[:, 0:TLOC], gB[:, 0:TLOC], gSc[:, 4:5])
            DR_V and v.drain()
            v.tensor_tensor(gMi[:, 0:TLOC], gMi[:, 0:TLOC], gM[:, 0:TLOC], ALU.max)
            DR_V and v.drain()
            v.memset(gSc[:, BP:BP + 1], 0.0)
            DR_V and v.drain()
            if NC > 1:
                v.tensor_copy(gSc[:, BP + 1:BP + NC], gB[:, L - 1:TLOC - 1:L])
                DR_V and v.drain()
                v.tensor_copy(gSc[:, MI + 1:MI + NC], gMi[:, L - 1:TLOC - 1:L])
                DR_V and v.drain()
            v.tensor_copy(gSc[:, MI:MI + 1], gSc[:, 4:5])
            DR_V and v.drain()
            bl = gM
            v.tensor_tensor(lview(bl[:, 0:TLOC]), lview(gB[:, 0:TLOC]),
                            cview(gSc[:, BP:BP + NC]), ALU.subtract)
            DR_V and v.drain()
            v.tensor_sub(gU[:, 0:TLOC], gI[:, 0:TLOC], bl[:, 0:TLOC])
            DR_V and v.drain()
            v.tensor_scalar_mul(gM[:, 0:TLOC], gM[:, 0:TLOC], -1.0)
            DR_V and v.drain()
            v.tensor_add(gM[:, 0:TLOC], gM[:, 0:TLOC], gMi[:, 0:TLOC])
            DR_V and v.drain()
            v.reduce_max(gSc[:, MB:MB + NC], lview(gM[:, 0:TLOC]), axis=AX.X)
            DR_V and v.drain()
            v.tensor_tensor(lview(gU[:, 0:TLOC]), lview(gU[:, 0:TLOC]),
                            cview(gSc[:, MB:MB + NC]), ALU.subtract)
            DR_V and v.drain()
            v.tensor_scalar_mul(gM[:, 0:TLOC], gM[:, 0:TLOC], -1.0)
            DR_V and v.drain()
            v.tensor_tensor(lview(gM[:, 0:TLOC]), lview(gM[:, 0:TLOC]),
                            cview(gSc[:, MB:MB + NC]), ALU.add)
            DR_V and v.drain()
            v.tensor_copy(gSc[:, S1_:S1_ + NC], gB[:, L - 1:TLOC:L])
            DR_V and v.drain()
            v.tensor_copy(gSc[:, S1_ + NC:S1_ + 2 * NC], gMi[:, L - 1:TLOC:L])
            DR_V and v.drain()
            v.tensor_sub(gSc[:, r_:r_ + NC], gSc[:, MB:MB + NC], gSc[:, S1_ + NC:S1_ + 2 * NC])
            DR_V and v.drain()
            v.tensor_add(gSc[:, r_:r_ + NC], gSc[:, r_:r_ + NC], gSc[:, S1_:S1_ + NC])
            DR_V and v.drain()
            v.tensor_sub(gSc[:, r_:r_ + NC], gSc[:, r_:r_ + NC], gSc[:, BP:BP + NC])
            DR_V and v.drain()
            v.tensor_sub(gSc[:, a_:a_ + NC], gSc[:, MI:MI + NC], gSc[:, S1_ + NC:S1_ + 2 * NC])
            DR_V and v.drain()
            v.tensor_add(gSc[:, a_:a_ + NC], gSc[:, a_:a_ + NC], gSc[:, S1_:S1_ + NC])
            DR_V and v.drain()
            v.tensor_sub(gSc[:, a_:a_ + NC], gSc[:, a_:a_ + NC], gSc[:, BP:BP + NC])
            DR_V and v.drain()
            v.tensor_sub(gSc[:, rho_:rho_ + NC], gSc[:, MI:MI + NC], gSc[:, MB:MB + NC])
            DR_V and v.drain()
            v.tensor_scalar_add(gSc[:, cpx_:cpx_ + NC], gSc[:, BP:BP + NC], gSc[:, 4:5])
            DR_V and v.drain()
            v.tensor_sub(gSc[:, cpx_:cpx_ + NC], gSc[:, cpx_:cpx_ + NC], gSc[:, MB:MB + NC])
            DR_V and v.drain()
            vC.inc(v.engine_nop())
        V(gatefin)
        VGF = vC.val

        if KLIM < 3.2:
            def tailx32(s):
                for j in range(NT):
                    st.inc(s.dma_start(out=y_out[j * 128:(j + 1) * 128, :],
                                       in_=fscr[:, 0:DM]))
                s.wait_ge(st.sem, st.val)
            Y(tailx32)

            def tailgx32(g):
                g.wait_ge(st.sem, st.val)
            G(tailgx32)
            return nc
        def gateexp(s):
            s.wait_ge(vC.sem, VGF)
            s.activation(gU[:, 0:TLOC], gU[:, 0:TLOC], AF.Exp)
            DR_S and s.drain()
            s.activation(gM[:, 0:TLOC], gM[:, 0:TLOC], AF.Exp)
            DR_S and s.drain()
            s.activation(gSc[:, EB:EB + 4 * NC], gSc[:, EB:EB + 4 * NC], AF.Exp)
            DR_S and s.drain()
            sC.inc(s.nop())
        S(gateexp)
        SGE = sC.val

        if KLIM < 3.4:
            def tailx34(s):
                for j in range(NT):
                    st.inc(s.dma_start(out=y_out[j * 128:(j + 1) * 128, :],
                                       in_=fscr[:, 0:DM]))
                s.wait_ge(st.sem, st.val)
            Y(tailx34)

            def tailgx34(g):
                g.wait_ge(st.sem, st.val)
            G(tailgx34)
            return nc
        def ahat_rows(v):
            v.wait_ge(sC.sem, SGE)
            v.memset(gB[:], 0.0)
            DR_V and v.drain()
            for b in range(NBLK):
                v.memset(gSc[:, AB + b * NSL:AB + b * NSL + 1], 0.0)
                DR_V and v.drain()
                v.tensor_copy(gSc[:, AB + b * NSL + 1:AB + (b + 1) * NSL],
                              gSc[:, a_ + b * CPB:a_ + (b + 1) * CPB])
                DR_V and v.drain()
            vC.inc(v.engine_nop())
        V(ahat_rows)
        VAH = vC.val

        if KLIM < 3.5:
            def tailx35(s):
                for j in range(NT):
                    st.inc(s.dma_start(out=y_out[j * 128:(j + 1) * 128, :],
                                       in_=fscr[:, 0:DM]))
                s.wait_ge(st.sem, st.val)
            Y(tailx35)

            def tailgx35(g):
                g.wait_ge(st.sem, st.val)
            G(tailgx35)
            return nc
        def rows_ship(g):
            g.wait_ge(vC.sem, VAH)
            gd.inc(g.dma_start(out=rowsc[:, :], in_=gSc[:, 0:NROW]))
            g.wait_ge(gd.sem, gd.val)

            def rep2(dst, base, X):
                for p in range(HP):
                    src = rowsc[2 * p:2 * p + 2, base:base + X]
                    gd.inc(g.dma_start(
                        out=dst[:, p * X:(p + 1) * X],
                        in_=src.rearrange("h (o w) -> h o w", o=1).to_broadcast([2, 64, X])))
            rep2(rrep, r_, NC)
            rep2(rhorep, rho_, NC)
            rep2(cpxrep, cpx_, NC)
            g.wait_ge(gd.sem, gd.val)
        G(rows_ship)
        GD_REP = gd.val

        def arep_build(g, b):
            for p in range(HP):
                src = rowsc[2 * p:2 * p + 2, AB + b * NSL:AB + (b + 1) * NSL]
                gd.inc(g.dma_start(
                    out=areptmp[:, p * NSL:(p + 1) * NSL],
                    in_=src.rearrange("h (o w) -> h o w", o=1).to_broadcast([2, 64, NSL])))

        if KLIM < 3.7:
            def tailx37(s):
                for j in range(NT):
                    st.inc(s.dma_start(out=y_out[j * 128:(j + 1) * 128, :],
                                       in_=fscr[:, 0:DM]))
                s.wait_ge(st.sem, st.val)
            Y(tailx37)

            def tailgx37(g):
                g.wait_ge(st.sem, st.val)
            G(tailgx37)
            return nc
        def pcols():
            sv0 = sC.val
            tv0, vv0 = tC.val, vC.val
            PXp = [PXa, PXb]

            def tfn(t):
                t.wait_ge(sC.sem, sv0)
                for c in range(NC):
                    if c >= 2:
                        t.wait_ge(vC.sem, vv0 + 2 * (c - 1))
                    tC.inc(t.transpose(PXp[c % 2][0:128, 0:16],
                                gU[0:16, c * L:(c + 1) * L], sIdF[:]))

            def vfn(v):
                for c in range(NC):
                    v.wait_ge(tC.sem, tv0 + c + 1)
                    v.tensor_copy(gTb[:, c * 16:(c + 1) * 16], PXp[c % 2][0:128, 0:16])
                    DR_V and v.drain()
                    vC.inc(v.tensor_copy(gTf[:, c * 16:(c + 1) * 16], PXp[c % 2][0:128, 0:16]))
                    vC.inc(v.engine_nop())
            T(tfn)
            V(vfn)
        pcols()
        VPCOLS = vC.val

        def ffnprep_build():
            g1f = g1[:].bitcast(F32)
            SGB = 8 * DM
            DH = DM
            ND = 4
            thunks = []
            sc_hist = [0] * ND
            gd_hist = [gd.val] * ND

            def sign_iter(i, src_ap, dst_ap, first):
                half = i % ND
                sc2 = sc_hist[-ND]

                def yl(s, i=i, half=half, sc2=sc2, src_ap=src_ap, first=first):
                    if first:
                        s.wait_ge(sC.sem, sC.val)
                        s.wait_ge(vC.sem, vC.val)
                    if sc2 > 0:
                        s.wait_ge(sC.sem, sc2)
                    ldw.inc(s.dma_start(out=g1f[:, half * DH:(half + 1) * DH],
                                        in_=src_ap(i)))
                    s.wait_ge(ldw.sem, ldw.val)
                Y(yl)
                lv = ldw.val
                gd2 = gd_hist[-ND]

                def sg(s, lv=lv, half=half, gd2=gd2):
                    s.wait_ge(ldw.sem, lv)
                    if gd2 > 0:
                        s.wait_ge(gd.sem, gd2)
                    s.activation(g1[:, SGB + half * DH:SGB + (half + 1) * DH],
                                 g1f[:, half * DH:(half + 1) * DH], AF.Sign)
                    s.drain()
                    sC.inc(s.nop())
                S(sg)
                sv = sC.val
                sc_hist.append(sv)

                def yst(g, i=i, half=half, sv=sv, dst_ap=dst_ap):
                    g.wait_ge(sC.sem, sv)
                    g.wait_ge(gd.sem, gd.val)
                    gd.inc(g.dma_start(out=dst_ap(i),
                                       in_=g1[:, SGB + half * DH:SGB + (half + 1) * DH]))
                    g.wait_ge(gd.sem, gd.val)
                G(yst)
                gd_hist.append(gd.val)

            vh = [0] * ND

            def alpha_iter(i, src_ap, red_fn, first):
                half = i % ND
                v2 = vh[-ND]

                def yl(s, i=i, half=half, v2=v2, src_ap=src_ap, first=first):
                    if first:
                        s.wait_ge(sC.sem, sC.val)
                        s.wait_ge(vC.sem, vC.val)
                    if v2 > 0:
                        s.wait_ge(vC.sem, v2)
                    ldw.inc(s.dma_start(out=g1f[:, half * DH:(half + 1) * DH],
                                        in_=src_ap(i)))
                    s.wait_ge(ldw.sem, ldw.val)
                Y(yl)
                lv = ldw.val

                def va(v, i=i, half=half, lv=lv, red_fn=red_fn):
                    v.wait_ge(ldw.sem, lv)
                    red_fn(v, i, g1f[:, half * DH:(half + 1) * DH])
                    vC.inc(v.engine_nop())
                V(va)
                vh.append(vC.val)

            NS1 = DF // DH
            for i in range(FC * NS1):
                thunks.append(lambda i=i: sign_iter(
                    i,
                    lambda i: w1T_in[(i // NS1) * 128:(i // NS1) * 128 + 128,
                                     (i % NS1) * DH:(i % NS1 + 1) * DH],
                    lambda i: s1sc[(i // NS1) * 128:(i // NS1) * 128 + 128,
                                   (i % NS1) * DH:(i % NS1 + 1) * DH],
                    i < ND))

            def red1(v, i, ap_):
                v.reduce_sum(sA1[:, i:i + 1], ap_[:, 0:DM], axis=AX.X,
                             apply_absolute_value=True)
                DR_V and v.drain()
            for i in range(HC):
                thunks.append(lambda i=i: alpha_iter(
                    i, lambda i: w1h_in[i * 128:(i + 1) * 128, :], red1, i < ND))

            def vsc1(v):
                v.tensor_scalar_mul(sA1[:, 0:HC], sA1[:, 0:HC], 1.0 / DM)
                DR_V and v.drain()
                vC.inc(v.engine_nop())
            thunks.append(lambda: V(vsc1))

            for i in range(HC):
                thunks.append(lambda i=i: sign_iter(
                    i, lambda i: w2T_in[i * 128:(i + 1) * 128, :],
                    lambda i: s2sc[i * 128:(i + 1) * 128, :], i < ND))

            NS2 = DF // DH
            A2 = 5 * NT + 5

            def red2(v, i, ap_):
                d, hseg = divmod(i, NS2)
                v.reduce_sum(sMu[:, A2 + hseg:A2 + hseg + 1], ap_[:, 0:DH],
                             axis=AX.X, apply_absolute_value=True)
                DR_V and v.drain()
                if hseg == NS2 - 1:
                    for h2 in range(1, NS2):
                        v.tensor_add(sMu[:, A2:A2 + 1], sMu[:, A2:A2 + 1],
                                     sMu[:, A2 + h2:A2 + h2 + 1])
                        DR_V and v.drain()
                    v.tensor_copy(sA2[:, d:d + 1], sMu[:, A2:A2 + 1])
                    DR_V and v.drain()
            for i in range(FC * NS2):
                thunks.append(lambda i=i: alpha_iter(
                    i,
                    lambda i: w2h_in[(i // NS2) * 128:(i // NS2) * 128 + 128,
                                     (i % NS2) * DH:(i % NS2 + 1) * DH],
                    red2, i < ND))

            def vsc2(v):
                v.tensor_scalar_mul(sA2[:, 0:FC], sA2[:, 0:FC], 1.0 / DF)
                DR_V and v.drain()
                vC.inc(v.engine_nop())
            thunks.append(lambda: V(vsc2))
            return thunks

        prep_thunks = ffnprep_build() if KLIM >= 3.9 else []

        slot_gate = [None, None]


        def wait_gate(eng, gate):
            if gate is None:
                eng.wait_ge(bT.sem, bT.val)
                return
            ctr, val = gate
            eng.wait_ge(ctr.sem, val)

        t60_end = [T2END]
        blkT = [T2END]
        scanV = [V2END]
        carry_gate = [None]
        for b in (range(NBLK) if KLIM >= 4 else []):
            tb = b * BLK
            c0 = b * CPB
            t60p = t60_end[-1]

            def s60(s, b=b, t60p=t60p):
                if b == 0:
                    s.wait_ge(vC.sem, V2END)
                s.wait_ge(tC.sem, t60p)
                ln1_apply_block(s, b)
                DR_S and s.drain()
                sC.inc(s.nop())
            S(s60)
            xpose_block_emit(pb_free=(V2END if b == 0 else None))
            t60_end.append(tC.val)
            XTE = vC.val

            if KSUB < 1:
                continue
            bank_gate = {0: None, 1: None}
            spill_gd = []
            pairs = [(w, e) for w in ("k", "v", "q", "o") for e in range(FC)]
            for i, (which, e) in enumerate(pairs):
                slot = i % 2
                bank = i % 2
                Wsrc = {"k": Wk, "v": Wv, "q": Wq, "o": Wo}[which]
                sg = slot_gate[1] if i >= 1 else None

                def yload(g, Wsrc=Wsrc, e=e, slot=slot, sg=sg, b=b, i=i):
                    if b == 0 and i == 0:
                        g.wait_ge(tC.sem, T2END)
                    wait_gate(g, sg)
                    g.wait_ge(gd.sem, gd.val)
                    gd.inc(g.dma_start(
                        out=Wwin[:, slot * FC * 128:(slot + 1) * FC * 128].rearrange(
                            "p (c n) -> p c n", c=FC),
                        in_=Wsrc[0:DM, e * 128:(e + 1) * 128].rearrange(
                            "(c p) n -> p c n", p=128)))
                    g.wait_ge(gd.sem, gd.val)
                G(yload)
                LV = gd.val
                bg = bank_gate[bank]

                def tmm(t, slot=slot, bank=bank, VV=LV, bg=bg, XTE=XTE):
                    t.wait_ge(vC.sem, XTE)
                    t.wait_ge(gd.sem, VV)
                    wait_gate(t, bg)
                    for fc in range(FC):
                        t.matmul(PF[bank][:, 0:BLK],
                                 Wwin[:, slot * FC * 128 + fc * 128:slot * FC * 128 + fc * 128 + 128],
                                 xT[:, fc * BLK:fc * BLK + BLK],
                                 start=(fc == 0), stop=(fc == FC - 1))
                    t.drain()
                    tC.inc(t.nop())
                T(tmm)
                TV = tC.val
                slot_gate[slot] = (tC, TV)
                slot_gate[1] = (tC, TV)
                if which in ("k", "v"):
                    dstT = Kfm if which == "k" else Vfm

                    def ev(v, e=e, bank=bank, TV=TV, dstT=dstT, i=i, bTp=blkT[-1]):
                        if i == 0:
                            v.wait_ge(tC.sem, bTp)
                        v.wait_ge(tC.sem, TV)
                        vC.inc(v.tensor_copy(dstT[:, e * BLK:(e + 1) * BLK], PF[bank][:, 0:BLK]))
                    V(ev)
                    bank_gate[bank] = (vC, vC.val)
                elif which == "q":
                    def ev(s, e=e, bank=bank, TV=TV, tb=tb, bTp=blkT[-1]):
                        if e == 0:
                            s.wait_ge(tC.sem, bTp)
                        s.wait_ge(tC.sem, TV)
                        s.activation(Qwin[:, e * BLK:(e + 1) * BLK],
                                     PF[bank][:, 0:BLK], AF.Identity, scale=0.125)
                        DR_S and s.drain()
                        sC.inc(s.nop())
                    S(ev)
                    bank_gate[bank] = (sC, sC.val)

                    def qsp(g, e=e, sv=sC.val, tb=tb):
                        g.wait_ge(sC.sem, sv)
                        g.wait_ge(gd.sem, gd.val)
                        gd.inc(g.dma_start(out=Qhbm[:, e * TLOC + tb:e * TLOC + tb + BLK],
                                           in_=Qwin[:, e * BLK:(e + 1) * BLK]))
                        g.wait_ge(gd.sem, gd.val)
                    G(qsp)
                    spill_gd.append(gd.val)
                else:
                    def ev(s, e=e, bank=bank, TV=TV, tb=tb, b=b, bTp=blkT[-1]):
                        if e == 0:
                            s.wait_ge(tC.sem, bTp)
                        s.wait_ge(tC.sem, TV)
                        s.activation(SHR[:, e * BLK:(e + 1) * BLK],
                                     PF[bank][:, 0:BLK], AF.Sigmoid, bias=sBo[:, e:e + 1])
                        DR_S and s.drain()
                        sC.inc(s.nop())
                    S(ev)
                    bank_gate[bank] = (sC, sC.val)

                    def osp(g, e=e, sv=sC.val, tb=tb):
                        g.wait_ge(sC.sem, sv)
                        g.wait_ge(gd.sem, gd.val)
                        gd.inc(g.dma_start(out=Ohbm[:, e * TLOC + tb:e * TLOC + tb + BLK],
                                           in_=SHR[:, e * BLK:(e + 1) * BLK]))
                        g.wait_ge(gd.sem, gd.val)
                    G(osp)
                    spill_gd.append(gd.val)
            GD_SPILL = gd.val

            if KSUB < 2:
                continue
            items = [(src, cb, fc) for src in ("k", "v") for cb in range(CPB) for fc in range(FC)]
            tv0, vv0 = tC.val, vC.val

            def t62(t, items=items, tv0=tv0, vv0=vv0):
                for i, (src, cb, fc) in enumerate(items):
                    if i >= 2:
                        t.wait_ge(vC.sem, vv0 + 2 * (i - 1))
                    else:
                        t.wait_ge(vC.sem, vv0)
                    buf = Kfm if src == "k" else Vfm
                    tC.inc(t.transpose(PB[i % 2][:, 0:128],
                                buf[:, fc * BLK + cb * 128:fc * BLK + cb * 128 + 128],
                                sId[:]))
            T(t62)

            def v62(v, items=items, tv0=tv0, c0=c0, bTp=blkT[-1]):
                v.wait_ge(tC.sem, bTp)
                v.wait_ge(gd.sem, GD_SPILL)
                for i, (src, cb, fc) in enumerate(items):
                    v.wait_ge(tC.sem, tv0 + i + 1)
                    cg = c0 + cb
                    if src == "k":
                        for hh in range(2):
                            h = 2 * fc + hh
                            v.tensor_scalar(
                                KPT[:, cb * DM + h * 64:cb * DM + (h + 1) * 64],
                                PB[i % 2][:, hh * 64:(hh + 1) * 64],
                                gTf[:, cg * 16 + h:cg * 16 + h + 1], None, ALU.mult)
                        DR_V and v.drain()
                        vC.inc(v.engine_nop())
                        vC.inc(v.engine_nop())
                    else:
                        vC.inc(v.tensor_copy(
                            VTp[:, cb * H * 64 + fc * 128:cb * H * 64 + (fc + 1) * 128],
                            PB[i % 2][:, 0:128]))
                        vC.inc(v.engine_nop())
            V(v62)
            V62END = vC.val

            if KSUB < 3:
                continue
            def g63(g, b=b, sV=scanV[-1]):
                g.wait_ge(vC.sem, sV)
                g.wait_ge(gd.sem, gd.val)
                arep_build(g, b)
                g.wait_ge(gd.sem, gd.val)
            G(g63)
            GAR = gd.val

            raw_gate = [carry_gate[0]]
            for cb in range(CPB):
                cg = c0 + cb
                rg = raw_gate[0]

                def t63(t, cb=cb, rg=rg, b=b, V62END=V62END):
                    if cb == 0:
                        if b == 0:
                            t.wait_ge(vC.sem, VPCOLS)
                        t.wait_ge(vC.sem, V62END)
                        t.wait_ge(gd.sem, GD_REP)
                    wait_gate(t, rg)
                    for h in range(H):
                        p = h // 2
                        bankp = PX[p // 4]
                        rowb = (h % 2) * 64
                        colb = (p % 4) * SL
                        t.matmul(bankp[rowb:rowb + 64, colb:colb + 64],
                                 VTp[:, cb * H * 64 + h * 64:cb * H * 64 + (h + 1) * 64],
                                 KPT[:, cb * DM + h * 64:cb * DM + (h + 1) * 64],
                                 start=True, stop=True)
                        t.matmul(bankp[rowb:rowb + 64, colb + 64:colb + 65],
                                 KPT[:, cb * DM + h * 64:cb * DM + (h + 1) * 64],
                                 sOne[:, 0:1], start=True, stop=True)
                    t.drain()
                    tC.inc(t.nop())
                T(t63)
                TV63 = tC.val

                def v63(v, cb=cb, cg=cg, TV63=TV63):
                    v.wait_ge(tC.sem, TV63)
                    for bk in range(NBRAW):
                        npair = min(4, HP - bk * 4)
                        pv = PX[bk][:, 0:npair * SL].rearrange("p (q e) -> p q e", e=SL)
                        v.tensor_tensor(
                            rawZ[:, 0:HP * SL * NSL].rearrange(
                                "p (q e n) -> p q e n", e=SL, n=NSL)[:, bk * 4:bk * 4 + npair, :, cb + 1:cb + 2],
                            pv.rearrange("p q (e o) -> p q e o", o=1),
                            rrep[:, 0:HP * NC].rearrange("p (q c) -> p q c", c=NC)[
                                :, bk * 4:bk * 4 + npair, cg:cg + 1]
                            .rearrange("p q (c o) -> p q c o", o=1)
                            .to_broadcast([128, npair, SL, 1]),
                            ALU.mult)
                        DR_V and v.drain()
                    vC.inc(v.engine_nop())
                V(v63)
                raw_gate[0] = (vC, vC.val)

            if KSUB < 4:
                continue
            def v64(v, b=b, c0=c0, bTp=blkT[-1]):
                v.wait_ge(tC.sem, bTp)
                v.wait_ge(gd.sem, GAR)
                if b > 0:
                    v.tensor_copy(
                        rawZ[:, 0:HP * SL * NSL].rearrange("p (q e n) -> p q e n", e=SL, n=NSL)[:, :, :, 0:1],
                        scanZ[:, 0:HP * SL * NSL].rearrange("p (q e n) -> p q e n", e=SL, n=NSL)[:, :, :, CPB:CPB + 1])
                    DR_V and v.drain()
                for p in range(HP):
                    v.tensor_copy(
                        arep[:, 0:SL * NSL].rearrange("p (e n) -> p e n", n=NSL),
                        areptmp[:, p * NSL:(p + 1) * NSL].rearrange("p (o n) -> p o n", o=1)
                        .to_broadcast([128, SL, NSL]))
                    DR_V and v.drain()
                    v.tensor_tensor_scan(
                        scanZ[:, p * SL * NSL:(p + 1) * SL * NSL],
                        arep[:, 0:SL * NSL],
                        rawZ[:, p * SL * NSL:(p + 1) * SL * NSL],
                        0.0, ALU.mult, ALU.add)
                    DR_V and v.drain()
                v.tensor_tensor(
                    scanS[:, 0:HP * SL * CPB].rearrange("p (q n e) -> p q n e", n=CPB, e=SL),
                    scanZ[:, 0:HP * SL * NSL].rearrange("p (q e n) -> p q n e", e=SL, n=NSL)[:, :, 0:CPB, :],
                    rhorep[:, 0:HP * NC].rearrange("p (q c) -> p q c", c=NC)[:, :, c0:c0 + CPB]
                    .rearrange("p q (c o) -> p q c o", o=1).to_broadcast([128, HP, CPB, SL]),
                    ALU.mult)
                DR_V and v.drain()
                vC.inc(v.engine_nop())
            V(v64)
            scanV.append(vC.val)

            if KSUB < 5:
                continue
            chunk_gate = carry_gate[0]
            for cb in range(CPB):
                cg = c0 + cb
                grp_gate = chunk_gate
                for gq in range(H // 4):
                    def t65(t, cb=cb, gq=gq, gg=grp_gate, tb=tb):
                        wait_gate(t, gg)
                        for rb, hhs in ((0, (0, 2)), (64, (1, 3))):
                            for dst, srcb in ((PFa, Vfm), (PFb, Kfm)):
                                for hh in hhs:
                                    h = gq * 4 + hh
                                    fc = h // 2
                                    mmn(t, rb)
                                    t.matmul(dst[:, hh * 128:(hh + 1) * 128],
                                             srcb[rb:rb + 64, fc * BLK + cb * 128:fc * BLK + cb * 128 + 128],
                                             Qwin[rb:rb + 64, fc * BLK + cb * 128:fc * BLK + cb * 128 + 128],
                                             start=True, stop=True)
                        t.drain()
                        tC.inc(t.nop())
                    T(t65)
                    TV65 = tC.val

                    def v65(v, gq=gq, TV65=TV65):
                        v.wait_ge(tC.sem, TV65)
                        mview = sMk[:, 0:128].rearrange("p (o l) -> p o l", o=1).to_broadcast([128, 4, 128])
                        v.tensor_tensor(
                            BvB[:, gq * 512:(gq + 1) * 512].rearrange("p (q l) -> p q l", q=4),
                            PFa[:, 0:512].rearrange("p (q l) -> p q l", q=4),
                            mview, ALU.mult)
                        DR_V and v.drain()
                        v.tensor_tensor(
                            BkB[:, gq * 512:(gq + 1) * 512].rearrange("p (q l) -> p q l", q=4),
                            PFb[:, 0:512].rearrange("p (q l) -> p q l", q=4),
                            mview, ALU.mult)
                        DR_V and v.drain()
                        vC.inc(v.engine_nop())
                    V(v65)
                    grp_gate = (vC, vC.val)
                    VB65 = vC.val
                    if KSUB < 5.5:
                        continue

                    def t66(t, cb=cb, cg=cg, gq=gq, VB65=VB65, tb=tb):
                        t.wait_ge(vC.sem, VB65)
                        for hh in range(4):
                            h = gq * 4 + hh
                            p = h // 2
                            rb = (h % 2) * 64
                            fc = h // 2
                            comb = PF[2 + (p % 2)]
                            ccol = (p // 2) * 128
                            qap = Qwin[rb:rb + 64, fc * BLK + cb * 128:fc * BLK + cb * 128 + 128]
                            mmn(t, None)
                            t.matmul(comb[rb:rb + 64, ccol:ccol + 128],
                                     KPT[:, cb * DM + h * 64:cb * DM + (h + 1) * 64],
                                     BvB[:, h * 128:(h + 1) * 128],
                                     start=True, stop=False)
                            mmn(t, rb)
                            t.matmul(comb[rb:rb + 64, ccol:ccol + 128],
                                     scanS[rb:rb + 64, p * SL * CPB + cb * SL:p * SL * CPB + cb * SL + 64],
                                     qap, start=False, stop=True)
                            dbk = PX[h // 8]
                            drow = 32 * ((h % 8) // 4)
                            dcol = (h % 4) * 128
                            t.matmul(dbk[drow:drow + 1, dcol:dcol + 128],
                                     scanS[rb:rb + 64, p * SL * CPB + cb * SL + 64:p * SL * CPB + cb * SL + 65],
                                     qap, start=True, stop=False)
                            mmn(t, None)
                            t.matmul(dbk[drow:drow + 1, dcol:dcol + 128],
                                     gTb[:, cg * 16 + h:cg * 16 + h + 1],
                                     BkB[:, h * 128:(h + 1) * 128],
                                     start=False, stop=True)
                        t.drain()
                        tC.inc(t.nop())
                    T(t66)
                TV66 = tC.val
                if KSUB < 5.7:
                    chunk_gate = grp_gate
                    continue

                def v66(v, cb=cb, TV66=TV66, tb=tb, nsg=None):
                    v.wait_ge(tC.sem, TV66)
                    if nsg is not None:
                        v.wait_ge(gd.sem, nsg)
                    for bk in range(2):
                        v.tensor_copy(
                            N0stg[:, 0:HP * 128].rearrange("p (f t) -> p f t", f=HP)[
                                :, bk::2, :],
                            PF[2 + bk][:, 0:PPB * 128].rearrange("p (q l) -> p q l", q=PPB))
                        DR_V and v.drain()
                    for bk in range(NBDEN):
                        v.tensor_copy(dslab[:, bk * 512:bk * 512 + 512], PX[bk][:, 0:512])
                        DR_V and v.drain()
                    vC.inc(v.engine_nop())
                V(v66)
                VD = vC.val
                chunk_gate = (vC, VD)
                gprev = gd.val

                def g66(g, cg=cg, VD=VD, gprev=gprev):
                    g.wait_ge(vC.sem, VD)
                    g.wait_ge(gd.sem, gd.val)
                    gd.inc(g.dma_start(
                        out=N0hbm[:, 0:HP * TLOC].rearrange("p (q t) -> p q t", q=HP)[
                            :, :, cg * 128:(cg + 1) * 128],
                        in_=N0stg[:, 0:HP * 128].rearrange("p (q t) -> p q t", q=HP)))
                    for half in range(NBDEN):
                        nh = min(8, H - half * 8)
                        gd.inc(g.dma_start(
                            out=den0[8 * half:8 * half + nh, cg * 128:(cg + 1) * 128],
                            in_=dslab[:, half * 512:(half + 1) * 512]
                            .rearrange("p (q l) -> p q l", q=4)[0:64:32, :, :][0:(nh + 3) // 4, 0:min(4, nh), :]))
                G(g66)
                chunk_gate_gd = gd.val

                def vwait_gd(v, gdv=chunk_gate_gd):
                    v.wait_ge(gd.sem, gdv)
                    vC.inc(v.engine_nop())
                V(vwait_gd)
                chunk_gate = (vC, vC.val)
                npt = (len(prep_thunks) + NBLK * CPB - 1) // (NBLK * CPB)
                i0t = (b * CPB + cb) * npt
                for th in prep_thunks[i0t:i0t + npt]:
                    th()
            blkT.append(tC.val)
            carry_gate[0] = chunk_gate
        _npt = (len(prep_thunks) + NBLK * CPB - 1) // (NBLK * CPB)
        for th in prep_thunks[((NBLK * CPB if KLIM >= 4 else 0)) * _npt:]:
            th()
        GD_PREP = gd.val
        VPREP = vC.val

        if KLIM < 5:
            def tail(s):
                for j in range(NT):
                    st.inc(s.dma_start(out=y_out[j * 128:(j + 1) * 128, :],
                                       in_=fscr[:, 0:DM]))
            Y(tail)

            def tailw(s):
                s.wait_ge(st.sem, st.val)
            Y(tailw)

            def tailg(g):
                g.wait_ge(st.sem, st.val)
            G(tailg)
            return nc

        def v70(v):
            v.tensor_copy(
                sSend[:, 0:HP * SL].rearrange("p (q e o) -> p q e o", e=SL, o=1),
                scanZ[:, 0:HP * SL * NSL].rearrange("p (q e n) -> p q e n", e=SL, n=NSL)[:, :, :, CPB:CPB + 1])
            DR_V and v.drain()
            vC.inc(v.engine_nop())
        V(v70)
        V70 = vC.val

        def g70(g):
            g.wait_ge(vC.sem, V70)
            gd.inc(g.dma_start(out=cs_in[:, :], in_=sSend[:]))
            g.wait_ge(gd.sem, gd.val)
            cc.inc(g.collective_compute(
                "AllReduce", ALU.add, replica_groups=groups,
                ins=[cs_in.ap().opt()], outs=[cs_out.ap().opt()]))
            g.wait_ge(cc.sem, cc.val)
            gd.inc(g.dma_start(out=sXsum[:, :], in_=cs_out[:, :]))
            g.wait_ge(gd.sem, gd.val)
        G(g70)
        GD_X = gd.val

        def v71(v):
            v.wait_ge(gd.sem, GD_X)
            v.tensor_sub(sXsum[:], sXsum[:], sSend[:])
            DR_V and v.drain()
            v.tensor_scalar(sXsum[:], sXsum[:], sIsr[:, 0:1], None, ALU.mult)
            DR_V and v.drain()
            vC.inc(v.engine_nop())
        V(v71)
        barrier()

        chunk_gate = None
        gate_hist = [None, None]
        vd2_hist = [0]
        tv72_hist = [0]
        g72_hist = [0]
        for cg in range(NC):
            b = cg // CPB
            tb = b * BLK
            slot = cg % 2
            VO = slot * FC * 128
            N0b = N0stg if slot == 0 else Kfm
            NO = 0
            XO = 0
            gg2 = gate_hist[-2]

            def gql(g, cg=cg, gg=gg2, VO=VO, NO=NO, N0b=N0b):
                wait_gate(g, gg)
                g.wait_ge(gd.sem, gd.val)
                gd.inc(g.dma_start(
                    out=Vfm[:, VO:VO + FC * 128],
                    in_=Qhbm[:, 0:FC * TLOC].rearrange("p (f t) -> p f t", f=FC)[
                        :, :, cg * 128:(cg + 1) * 128]))
                gd.inc(g.dma_start(
                    out=N0b[:, NO:NO + HP * 128],
                    in_=N0hbm[:, 0:HP * TLOC].rearrange("p (q t) -> p q t", q=HP)[
                        :, :, cg * 128:(cg + 1) * 128]))
                g.wait_ge(gd.sem, gd.val)
            G(gql)
            GQL = gd.val

            tv72p = tv72_hist[-1]

            def v72(v, cg=cg, GQL=GQL, XO=XO, tv72p=tv72p):
                if tv72p > 0:
                    v.wait_ge(tC.sem, tv72p)
                v.wait_ge(gd.sem, GQL)
                v.tensor_tensor(
                    sXcC[:, XO:XO + HP * SL].rearrange("p (q e o) -> p q e o", e=SL, o=1),
                    sXsum[:, 0:HP * SL].rearrange("p (q e o) -> p q e o", e=SL, o=1),
                    cpxrep[:, 0:HP * NC].rearrange("p (q c) -> p q c", c=NC)[:, :, cg:cg + 1]
                    .rearrange("p q (c o) -> p q c o", o=1).to_broadcast([128, HP, SL, 1]),
                    ALU.mult)
                DR_V and v.drain()
                vC.inc(v.engine_nop())
            V(v72)
            VXC = vC.val
            vd2p = vd2_hist[-1]

            def t72(t, cg=cg, VXC=VXC, VO=VO, XO=XO, vd2p=vd2p):
                t.wait_ge(vC.sem, VXC)
                if vd2p > 0:
                    t.wait_ge(vC.sem, vd2p)
                for h in list(range(0, H, 2)) + list(range(1, H, 2)):
                    p = h // 2
                    rb = (h % 2) * 64
                    fc = h // 2
                    comb = PF[2 + (p % 2)]
                    ccol = (p // 2) * 128
                    qap = Vfm[rb:rb + 64, VO + fc * 128:VO + (fc + 1) * 128]
                    mmn(t, rb)
                    t.matmul(comb[rb:rb + 64, ccol:ccol + 128],
                             sXcC[rb:rb + 64, XO + p * SL:XO + p * SL + 64],
                             qap, start=True, stop=True)
                    dbk = PX[h // 8]
                    drow = 32 * ((h % 8) // 4)
                    dcol = (h % 4) * 128
                    t.matmul(dbk[drow:drow + 1, dcol:dcol + 128],
                             sXcC[rb:rb + 64, XO + p * SL + 64:XO + p * SL + 65],
                             qap, start=True, stop=True)
                t.drain()
                tC.inc(t.nop())
            T(t72)
            TV72 = tC.val
            tv72_hist.append(TV72)
            g72p = g72_hist[-1]

            def v73(v, cg=cg, TV72=TV72, NO=NO, g72p=g72p, N0b=N0b):
                v.wait_ge(tC.sem, TV72)
                if g72p > 0:
                    v.wait_ge(gd.sem, g72p)
                for bk in range(2):
                    nview = N0b[:, NO:NO + HP * 128].rearrange("p (f t) -> p f t", f=HP)[
                        :, bk::2, :]
                    v.tensor_tensor(nview, PF[2 + bk][:, 0:PPB * 128].rearrange("p (q l) -> p q l", q=PPB),
                                    nview, ALU.add)
                    DR_V and v.drain()
                for bk in range(NBDEN):
                    v.tensor_copy(dslab[:, bk * 512:bk * 512 + 512], PX[bk][:, 0:512])
                    DR_V and v.drain()
                vC.inc(v.engine_nop())
            V(v73)
            VD2 = vC.val
            vd2_hist.append(VD2)

            def g72(g, cg=cg, VD2=VD2, NO=NO, N0b=N0b):
                g.wait_ge(vC.sem, VD2)
                g.wait_ge(gd.sem, gd.val)
                gd.inc(g.dma_start(
                    out=N0hbm[:, 0:HP * TLOC].rearrange("p (q t) -> p q t", q=HP)[
                        :, :, cg * 128:(cg + 1) * 128],
                    in_=N0b[:, NO:NO + HP * 128].rearrange("p (q t) -> p q t", q=HP)))
                for half in range(NBDEN):
                    nh = min(8, H - half * 8)
                    gd.inc(g.dma_start(
                        out=denXr[8 * half:8 * half + nh, cg * 128:(cg + 1) * 128],
                        in_=dslab[:, half * 512:(half + 1) * 512]
                        .rearrange("p (q l) -> p q l", q=4)[0:64:32, :, :][0:(nh + 3) // 4, 0:min(4, nh), :]))
                g.wait_ge(gd.sem, gd.val)
            G(g72)
            gdv = gd.val
            g72_hist.append(gdv)

            def vw(v, gdv=gdv):
                v.wait_ge(gd.sem, gdv)
                vC.inc(v.engine_nop())
            V(vw)
            chunk_gate = (vC, vC.val)
            gate_hist.append(chunk_gate)

        def v74(v):
            v.tensor_add(dfin[:, 0:TLOC], den0[:, 0:TLOC], denXr[:, 0:TLOC])
            DR_V and v.drain()
            v.tensor_tensor(dfin[:, 0:TLOC], dfin[:, 0:TLOC], gM[:, 0:TLOC], ALU.mult)
            DR_V and v.drain()
            vC.inc(v.engine_nop())
        V(v74)
        V74 = vC.val

        def s74(s):
            s.wait_ge(vC.sem, V74)
            s.activation(dfin[:, 0:TLOC], dfin[:, 0:TLOC], AF.Abs)
            DR_S and s.drain()
            sC.inc(s.nop())
        S(s74)
        S74 = sC.val

        def v75(v):
            v.wait_ge(sC.sem, S74)
            v.tensor_scalar_max(dfin[:, 0:TLOC], dfin[:, 0:TLOC], 1.0)
            DR_V and v.drain()
            v.reciprocal(dfin[:, 0:TLOC], dfin[:, 0:TLOC])
            DR_V and v.drain()
            v.tensor_tensor(Grow[:, 0:TLOC], gM[:, 0:TLOC], dfin[:, 0:TLOC], ALU.mult)
            DR_V and v.drain()
            vC.inc(v.engine_nop())
        V(v75)
        V75 = vC.val

        def g75(g):
            g.wait_ge(vC.sem, V75)
            gd.inc(g.dma_start(out=gsc[:, :], in_=Grow[:, 0:TLOC]))
            g.wait_ge(gd.sem, gd.val)
        G(g75)
        GD_G = gd.val

        hq_gate = None
        for p in range(FC):
            gprev = gd.val

            def g76(g, p=p):
                if p == 0:
                    g.wait_ge(gd.sem, GD_G)
                g.wait_ge(gd.sem, gd.val)
                gd.inc(g.dma_start(
                    out=Grep[:, 0:TLOC],
                    in_=gsc[2 * p:2 * p + 2, :].rearrange("h (o w) -> h o w", o=1)
                    .to_broadcast([2, 64, TLOC])))
                gd.inc(g.dma_start(out=Kfm[:, 0:TLOC],
                                   in_=Ohbm[:, p * TLOC:(p + 1) * TLOC]))
                gd.inc(g.dma_start(out=Kfm[:, TLOC:2 * TLOC],
                                   in_=N0hbm[:, p * TLOC:(p + 1) * TLOC]))
            def g76w(g, hq=hq_gate):
                wait_gate(g, hq)
            G(g76w)
            G(g76)
            GDS = gd.val

            def v76(v, p=p, GDS=GDS):
                v.wait_ge(gd.sem, GDS)
                v.tensor_tensor(fscr2[:, 0:TLOC], Kfm[:, 0:TLOC],
                                Grep[:, 0:TLOC], ALU.mult)
                DR_V and v.drain()
                v.tensor_tensor(fscr2[:, 0:TLOC], fscr2[:, 0:TLOC],
                                Kfm[:, TLOC:2 * TLOC], ALU.mult)
                DR_V and v.drain()
                vC.inc(v.engine_nop())
            V(v76)
            hq_gate = (vC, vC.val)
            VH = vC.val
            tv0, vv0 = tC.val, vC.val

            def t76(t, p=p, VH=VH, tv0=tv0, vv0=vv0):
                t.wait_ge(vC.sem, VH)
                for j in range(NT):
                    if j >= 2:
                        t.wait_ge(vC.sem, vv0 + j - 1)
                    else:
                        t.wait_ge(vC.sem, vv0)
                    tC.inc(t.transpose(PB[j % 2][:, 0:128], fscr2[:, j * 128:(j + 1) * 128], sId[:]))
            T(t76)

            def v77(v, p=p, tv0=tv0):
                for j in range(NT):
                    v.wait_ge(tC.sem, tv0 + j + 1)
                    vC.inc(v.tensor_copy(
                        g1[:, j * DM + p * 128:j * DM + p * 128 + 128],
                        PB[j % 2][:, 0:128]))
            V(v77)
            hq_gate = (vC, vC.val)
        barrier()

        if KLIM < 6:
            def tail(s):
                for j in range(NT):
                    st.inc(s.dma_start(out=y_out[j * 128:(j + 1) * 128, :],
                                       in_=fscr[:, 0:DM]))
                s.wait_ge(st.sem, st.val)
            Y(tail)

            def tailg(g):
                g.wait_ge(st.sem, st.val)
            G(tailg)
            return nc
        NG8 = NT // 4
        LA = 6 * NT + 8
        LB = 6 * NT + 24
        fscr2_free = [None]

        def ln8(src_view, base, writer):
            for g in range(NG8):
                g0 = g * 4

                def vsumA(v, g0=g0, base=base):
                    v.reduce_sum(sMu[:, base:base + 4],
                                 src_view(g0, 4).rearrange("p (t d) -> p t d", t=4),
                                 axis=AX.X)
                    DR_V and v.drain()
                    v.tensor_scalar_mul(sMu[:, base:base + 4], sMu[:, base:base + 4], 1.0 / DM)
                    DR_V and v.drain()
                    vC.inc(v.engine_nop())
                V(vsumA)
                VMU = vC.val
                ffree = fscr2_free[0]

                def ssqA(s, g0=g0, VMU=VMU, ffree=ffree):
                    s.wait_ge(vC.sem, VMU)
                    if ffree is not None:
                        s.wait_ge(ffree[0].sem, ffree[1])
                    s.activation(fscr2[:, 0:4 * DM], src_view(g0, 4), AF.Square)
                    DR_S and s.drain()
                    sC.inc(s.nop())
                S(ssqA)
                SSQ = sC.val

                def vvarA(v, SSQ=SSQ, base=base):
                    v.wait_ge(sC.sem, SSQ)
                    mu4 = sMu[:, base:base + 4]
                    var4 = sMu[:, base + 4:base + 8]
                    nb4 = sMu[:, base + 12:base + 16]
                    v.reduce_sum(var4, fscr2[:, 0:4 * DM].rearrange(
                        "p (t d) -> p t d", t=4), axis=AX.X)
                    DR_V and v.drain()
                    v.tensor_scalar_mul(var4, var4, 1.0 / DM)
                    DR_V and v.drain()
                    v.tensor_tensor(nb4, mu4, mu4, ALU.mult)
                    DR_V and v.drain()
                    v.tensor_sub(var4, var4, nb4)
                    DR_V and v.drain()
                    v.tensor_scalar_add(var4, var4, LN_EPS)
                    DR_V and v.drain()
                    vC.inc(v.engine_nop())
                V(vvarA)
                VV = vC.val

                def srsA(s, VV=VV, base=base):
                    s.wait_ge(vC.sem, VV)
                    var4 = sMu[:, base + 4:base + 8]
                    rs4 = sMu[:, base + 8:base + 12]
                    s.activation(rs4, var4, AF.Ln)
                    DR_S and s.drain()
                    s.activation(rs4, rs4, AF.Exp, scale=-0.5)
                    DR_S and s.drain()
                    sC.inc(s.nop())
                S(srsA)
                SR = sC.val

                def vbiasA(v, SR=SR, base=base):
                    v.wait_ge(sC.sem, SR)
                    mu4 = sMu[:, base:base + 4]
                    rs4 = sMu[:, base + 8:base + 12]
                    nb4 = sMu[:, base + 12:base + 16]
                    v.tensor_tensor(nb4, mu4, rs4, ALU.mult)
                    DR_V and v.drain()
                    v.tensor_scalar_mul(nb4, nb4, -1.0)
                    DR_V and v.drain()
                    vC.inc(v.engine_nop())
                V(vbiasA)
                VB = vC.val

                def sapA(s, g0=g0, VB=VB, base=base):
                    s.wait_ge(vC.sem, VB)
                    for j in range(4):
                        s.activation(fscr2[:, j * DM:(j + 1) * DM],
                                     src_view(g0 + j, 1), AF.Identity,
                                     bias=sMu[:, base + 12 + j:base + 13 + j],
                                     scale=sMu[:, base + 8 + j:base + 9 + j])
                    s.drain()
                    sC.inc(s.nop())
                S(sapA)
                SA = sC.val
                writer(g0, SA)
                fscr2_free[0] = (vC, vC.val)

        def g1view(j0, n):
            return g1[:, j0 * DM:(j0 + n) * DM]

        def lna_write(g0, SA):
            def vw(v, g0=g0, SA=SA):
                v.wait_ge(sC.sem, SA)
                fview = fscr2[:, 0:4 * DM].rearrange("p (t d) -> p t d", t=4)
                gv = sGam[:, 0:DM].rearrange("p (o d) -> p o d", o=1).to_broadcast([128, 4, DM])
                bv = sBet[:, 0:DM].rearrange("p (o d) -> p o d", o=1).to_broadcast([128, 4, DM])
                v.tensor_tensor(fview, fview, gv, ALU.mult)
                DR_V and v.drain()
                v.tensor_tensor(g1[:, g0 * DM:(g0 + 4) * DM].rearrange(
                    "p (t d) -> p t d", t=4), fview, bv, ALU.add)
                DR_V and v.drain()
                vC.inc(v.engine_nop())
            V(vw)
        ln8(g1view, LA, lna_write)

        def lnb_write(g0, SA):
            def vw(v, g0=g0, SA=SA):
                v.wait_ge(sC.sem, SA)
                xv = xres[:, g0 * DM:(g0 + 4) * DM]
                v.tensor_tensor(xv, xv, fscr2[:, 0:4 * DM], ALU.add)
                DR_V and v.drain()
                vC.inc(v.engine_nop())
            V(vw)
        ln8(g1view, LB, lnb_write)
        LNBEND = vC.val

        if KLIM < 7:
            def tail(s):
                for j in range(NT):
                    st.inc(s.dma_start(out=y_out[j * 128:(j + 1) * 128, :],
                                       in_=fscr[:, 0:DM]))
                s.wait_ge(st.sem, st.val)
            Y(tail)

            def tailg(g):
                g.wait_ge(st.sem, st.val)
            G(tailg)
            return nc
        KPTf = KPT[:].bitcast(F32)
        f2bufs = [f2win, Qwin]
        L2 = 6 * NT + 8
        L3 = 6 * NT + 24
        pf_gate = [None, None]
        w1_gate = [None, None]
        f2_gate = [None, None]
        t90_prev = tC.val
        st_hist = [0, 0]
        ln2_sap = {}

        def emit_ln2(fb):
            jt0 = (fb * FBLK) // 128
            TPB = FBLK // 128

            def ln2_src(j, jt0=jt0):
                return xres[:, (jt0 + j) * DM:(jt0 + j + 1) * DM]

            def vsum2(v, jt0=jt0):
                v.reduce_sum(sMu[:, L2:L2 + TPB],
                             xres[:, jt0 * DM:(jt0 + TPB) * DM].rearrange(
                                 "p (t d) -> p t d", t=TPB), axis=AX.X)
                DR_V and v.drain()
                v.tensor_scalar_mul(sMu[:, L2:L2 + TPB], sMu[:, L2:L2 + TPB], 1.0 / DM)
                DR_V and v.drain()
                vC.inc(v.engine_nop())
            V(vsum2)
            VMU = vC.val

            def ssq2(s, jt0=jt0, tp=t90_prev, VMU=VMU, fb=fb):
                if fb == 0:
                    s.wait_ge(vC.sem, LNBEND)
                s.wait_ge(vC.sem, VMU)
                s.wait_ge(tC.sem, tp)
                s.activation(fscr2[:, 0:TPB * DM], xres[:, jt0 * DM:(jt0 + TPB) * DM],
                             AF.Square)
                DR_S and s.drain()
                sC.inc(s.nop())
            S(ssq2)
            SSQ = sC.val

            def vvar2(v, SSQ=SSQ):
                v.wait_ge(sC.sem, SSQ)
                mu4 = sMu[:, L2:L2 + TPB]
                var4 = sMu[:, L2 + 4:L2 + 4 + TPB]
                nb4 = sMu[:, L2 + 12:L2 + 12 + TPB]
                v.reduce_sum(var4, fscr2[:, 0:TPB * DM].rearrange(
                    "p (t d) -> p t d", t=TPB), axis=AX.X)
                DR_V and v.drain()
                v.tensor_scalar_mul(var4, var4, 1.0 / DM)
                DR_V and v.drain()
                v.tensor_tensor(nb4, mu4, mu4, ALU.mult)
                DR_V and v.drain()
                v.tensor_sub(var4, var4, nb4)
                DR_V and v.drain()
                v.tensor_scalar_add(var4, var4, LN_EPS)
                DR_V and v.drain()
                vC.inc(v.engine_nop())
            V(vvar2)
            VV = vC.val

            def srs2(s, VV=VV):
                s.wait_ge(vC.sem, VV)
                var4 = sMu[:, L2 + 4:L2 + 4 + TPB]
                rs4 = sMu[:, L2 + 8:L2 + 8 + TPB]
                s.activation(rs4, var4, AF.Ln)
                DR_S and s.drain()
                s.activation(rs4, rs4, AF.Exp, scale=-0.5)
                DR_S and s.drain()
                sC.inc(s.nop())
            S(srs2)
            SR = sC.val

            def vbias2(v, SR=SR):
                v.wait_ge(sC.sem, SR)
                mu4 = sMu[:, L2:L2 + TPB]
                rs4 = sMu[:, L2 + 8:L2 + 8 + TPB]
                nb4 = sMu[:, L2 + 12:L2 + 12 + TPB]
                v.tensor_tensor(nb4, mu4, rs4, ALU.mult)
                DR_V and v.drain()
                v.tensor_scalar_mul(nb4, nb4, -1.0)
                DR_V and v.drain()
                vC.inc(v.engine_nop())
            V(vbias2)
            VB = vC.val

            def sap2(s, jt0=jt0, VB=VB):
                s.wait_ge(vC.sem, VB)
                for j in range(TPB):
                    s.activation(fscr2[:, j * DM:(j + 1) * DM], ln2_src(j),
                                 AF.Identity,
                                 bias=sMu[:, L2 + 12 + j:L2 + 13 + j],
                                 scale=sMu[:, L2 + 8 + j:L2 + 9 + j])
                s.drain()
                sC.inc(s.nop())
            S(sap2)
            ln2_sap[fb] = sC.val

        emit_ln2(0)
        for fb in range(NFB):
            t0 = fb * FBLK
            jt0 = t0 // 128
            TPB = FBLK // 128
            SAP2 = ln2_sap[fb]

            n = TPB * FC
            tv0, vv0 = tC.val, vC.val

            def t90(t, n=n, tv0=tv0, vv0=vv0, SAP2=SAP2):
                t.wait_ge(sC.sem, SAP2)
                for i in range(n):
                    jj, fc = divmod(i, FC)
                    if i >= 2:
                        t.wait_ge(vC.sem, vv0 + i - 1)
                    else:
                        t.wait_ge(vC.sem, vv0)
                    tC.inc(t.transpose(PB[i % 2][:, 0:128],
                                fscr2[:, jj * DM + fc * 128:jj * DM + fc * 128 + 128], sId[:]))
            T(t90)
            t90_prev = tC.val

            def v90(v, n=n, tv0=tv0):
                for i in range(n):
                    jj, fc = divmod(i, FC)
                    v.wait_ge(tC.sem, tv0 + i + 1)
                    vC.inc(v.tensor_copy(
                        x2T[:, fc * FBLK + jj * 128:fc * FBLK + jj * 128 + 128],
                        PB[i % 2][:, 0:128]))
            V(v90)
            VX2T = vC.val
            if fb + 1 < NFB:
                emit_ln2(fb + 1)

            for hcx in range(HC):
                slot = hcx % 2
                bank = hcx % 2
                sg = w1_gate[slot]

                def yf1(s, hcx=hcx, slot=slot, sg=sg, fb=fb):
                    if fb == 0 and hcx < 2:
                        s.wait_ge(gd.sem, GD_PREP)
                    wait_gate(s, sg)
                    ldw.inc(s.dma_start(
                        out=Wwin[:, slot * FC * 128:(slot + 1) * FC * 128].rearrange(
                            "p (c n) -> p c n", c=FC),
                        in_=s1sc[0:DM, hcx * 128:(hcx + 1) * 128].rearrange(
                            "(c p) n -> p c n", p=128)))
                    s.wait_ge(ldw.sem, ldw.val)
                Y(yf1)
                LV = ldw.val
                bg = pf_gate[bank]

                def tf1(t, hcx=hcx, bank=bank, LV=LV, bg=bg, slot=slot, VX2T=VX2T):
                    t.wait_ge(ldw.sem, LV)
                    t.wait_ge(vC.sem, VX2T)
                    wait_gate(t, bg)
                    for fc in range(FC):
                        t.matmul(PF[bank][:, 0:FBLK],
                                 Wwin[:, slot * FC * 128 + fc * 128:slot * FC * 128 + fc * 128 + 128],
                                 x2T[:, fc * FBLK:fc * FBLK + FBLK],
                                 start=(fc == 0), stop=(fc == FC - 1))
                    t.drain()
                    tC.inc(t.nop())
                T(tf1)
                TV = tC.val
                w1_gate[slot] = (tC, TV)

                def sf1(s, hcx=hcx, bank=bank, TV=TV, fb=fb):
                    if fb == 0 and hcx < 2:
                        s.wait_ge(vC.sem, VPREP)
                    s.wait_ge(tC.sem, TV)
                    s.activation(g1[:, hcx * FBLK:(hcx + 1) * FBLK],
                                 PF[bank][:, 0:FBLK],
                                 AF.Gelu, bias=sB1[:, hcx:hcx + 1], scale=sA1[:, hcx:hcx + 1])
                    s.drain()
                    sC.inc(s.nop())
                S(sf1)
                pf_gate[bank] = (sC, sC.val)
            GF1 = sC.val

            for e in range(FC):
                sl2 = e % 2
                bank = e % 2
                buf = f2bufs[sl2]
                sg = f2_gate[sl2]

                def yf2(s, e=e, buf=buf, sg=sg, fb=fb):
                    if fb == 0 and e < 2:
                        s.wait_ge(gd.sem, GD_PREP)
                    wait_gate(s, sg)
                    ldw.inc(s.dma_start(
                        out=buf[:, 0:HC * 128].rearrange("p (c n) -> p c n", c=HC),
                        in_=s2sc[0:DF, e * 128:(e + 1) * 128].rearrange(
                            "(c p) n -> p c n", p=128)))
                    s.wait_ge(ldw.sem, ldw.val)
                Y(yf2)
                LV = ldw.val
                bg = pf_gate[bank]

                def tf2(t, e=e, buf=buf, bank=bank, LV=LV, bg=bg, GF1=GF1):
                    t.wait_ge(ldw.sem, LV)
                    t.wait_ge(sC.sem, GF1)
                    wait_gate(t, bg)
                    for hcx in range(HC):
                        t.matmul(PF[bank][:, 0:FBLK],
                                 buf[:, hcx * 128:hcx * 128 + 128],
                                 g1[:, hcx * FBLK:hcx * FBLK + FBLK],
                                 start=(hcx == 0), stop=(hcx == HC - 1))
                    t.drain()
                    tC.inc(t.nop())
                T(tf2)
                TV = tC.val
                f2_gate[sl2] = (tC, TV)

                def sf2(s, e=e, bank=bank, TV=TV, fb=fb):
                    if fb == 0 and e < 2:
                        s.wait_ge(vC.sem, VPREP)
                    s.wait_ge(tC.sem, TV)
                    s.activation(ffnO[:, e * FBLK:(e + 1) * FBLK],
                                 PF[bank][:, 0:FBLK],
                                 AF.Identity, bias=sB2[:, e:e + 1], scale=sA2[:, e:e + 1])
                    s.drain()
                    sC.inc(s.nop())
                S(sf2)
                pf_gate[bank] = (sC, sC.val)
            GF2 = sC.val

            n = TPB * FC
            tv0, vv0 = tC.val, vC.val

            def t91(t, n=n, tv0=tv0, vv0=vv0, GF2=GF2):
                t.wait_ge(sC.sem, GF2)
                for i in range(n):
                    jj, fc = divmod(i, FC)
                    if i >= 2:
                        t.wait_ge(vC.sem, vv0 + i - 1)
                    else:
                        t.wait_ge(vC.sem, vv0)
                    tC.inc(t.transpose(PB[i % 2][:, 0:128],
                                ffnO[:, fc * FBLK + jj * 128:fc * FBLK + jj * 128 + 128], sId[:]))
            T(t91)

            def v91(v, n=n, tv0=tv0):
                for i in range(n):
                    jj, fc = divmod(i, FC)
                    v.wait_ge(tC.sem, tv0 + i + 1)
                    vC.inc(v.tensor_copy(
                        g1[:, jj * DM + fc * 128:jj * DM + fc * 128 + 128],
                        PB[i % 2][:, 0:128]))
            V(v91)
            V91 = vC.val

            def vsum3(v, V91=V91):
                v.wait_ge(vC.sem, V91)
                v.reduce_sum(sMu[:, L3:L3 + TPB],
                             g1[:, 0:TPB * DM].rearrange("p (t d) -> p t d", t=TPB),
                             axis=AX.X)
                DR_V and v.drain()
                v.tensor_scalar_mul(sMu[:, L3:L3 + TPB], sMu[:, L3:L3 + TPB], 1.0 / DM)
                DR_V and v.drain()
                vC.inc(v.engine_nop())
            V(vsum3)
            VMU3 = vC.val

            HTP = TPB // 2
            SSQH = []
            for hh in range(2):
                def ssq3(s, VMU3=VMU3, hh=hh):
                    if hh == 0:
                        s.wait_ge(vC.sem, VMU3)
                    else:
                        s.wait_ge(vC.sem, vC.val)
                    s.activation(fscr[:, 0:HTP * DM],
                                 g1[:, hh * HTP * DM:(hh + 1) * HTP * DM], AF.Square)
                    DR_S and s.drain()
                    sC.inc(s.nop())
                S(ssq3)
                SSQ3 = sC.val
                SSQH.append(SSQ3)

                def vred3(v, SSQ3=SSQ3, hh=hh):
                    v.wait_ge(sC.sem, SSQ3)
                    v.reduce_sum(sMu[:, L3 + 4 + hh * HTP:L3 + 4 + (hh + 1) * HTP],
                                 fscr[:, 0:HTP * DM].rearrange(
                                     "p (t d) -> p t d", t=HTP), axis=AX.X)
                    DR_V and v.drain()
                    vC.inc(v.engine_nop())
                V(vred3)

            def vvar3(v):
                mu4 = sMu[:, L3:L3 + TPB]
                var4 = sMu[:, L3 + 4:L3 + 4 + TPB]
                nb4 = sMu[:, L3 + 12:L3 + 12 + TPB]
                v.tensor_scalar_mul(var4, var4, 1.0 / DM)
                DR_V and v.drain()
                v.tensor_tensor(nb4, mu4, mu4, ALU.mult)
                DR_V and v.drain()
                v.tensor_sub(var4, var4, nb4)
                DR_V and v.drain()
                v.tensor_scalar_add(var4, var4, LN_EPS)
                DR_V and v.drain()
                vC.inc(v.engine_nop())
            V(vvar3)
            VV3 = vC.val

            def srs3(s, VV3=VV3):
                s.wait_ge(vC.sem, VV3)
                var4 = sMu[:, L3 + 4:L3 + 4 + TPB]
                rs4 = sMu[:, L3 + 8:L3 + 8 + TPB]
                s.activation(rs4, var4, AF.Ln)
                DR_S and s.drain()
                s.activation(rs4, rs4, AF.Exp, scale=-0.5)
                DR_S and s.drain()
                sC.inc(s.nop())
            S(srs3)
            SR3 = sC.val

            def vbias3(v, SR3=SR3):
                v.wait_ge(sC.sem, SR3)
                mu4 = sMu[:, L3:L3 + TPB]
                rs4 = sMu[:, L3 + 8:L3 + 8 + TPB]
                nb4 = sMu[:, L3 + 12:L3 + 12 + TPB]
                v.tensor_tensor(nb4, mu4, rs4, ALU.mult)
                DR_V and v.drain()
                v.tensor_scalar_mul(nb4, nb4, -1.0)
                DR_V and v.drain()
                vC.inc(v.engine_nop())
            V(vbias3)
            VB3 = vC.val

            for j in range(TPB):
                jglob = jt0 + j
                c = j % 2
                stv2 = st_hist[-2]

                def sap3(s, j=j, c=c, VB3=VB3, stv2=stv2):
                    s.wait_ge(vC.sem, VB3)
                    if stv2 > 0:
                        s.wait_ge(st.sem, stv2)
                    s.activation(KPTf[:, c * DM:(c + 1) * DM], g1[:, j * DM:(j + 1) * DM],
                                 AF.Identity,
                                 bias=sMu[:, L3 + 12 + j:L3 + 13 + j],
                                 scale=sMu[:, L3 + 8 + j:L3 + 9 + j])
                    s.drain()
                    sC.inc(s.nop())
                S(sap3)
                SA = sC.val

                def vadd3(v, jglob=jglob, c=c, SA=SA):
                    v.wait_ge(sC.sem, SA)
                    vC.inc(v.tensor_tensor(KPTf[:, c * DM:(c + 1) * DM],
                                           xres[:, jglob * DM:(jglob + 1) * DM],
                                           KPTf[:, c * DM:(c + 1) * DM], ALU.add))
                V(vadd3)
                VF = vC.val

                def ystore(s, jglob=jglob, c=c, VF=VF):
                    s.wait_ge(vC.sem, VF)
                    st.inc(s.dma_start(out=y_out[jglob * 128:(jglob + 1) * 128, :],
                                       in_=KPTf[:, c * DM:(c + 1) * DM]))
                    s.wait_ge(st.sem, st.val)
                Y(ystore)
                st_hist.append(st.val)

        def yfin(s):
            s.wait_ge(st.sem, st.val)
        Y(yfin)

        def gfin(g):
            g.wait_ge(st.sem, st.val)
        G(gfin)
    return nc


def kernel(**inputs):
    cfg = Cfg()
    x = np.ascontiguousarray(np.asarray(inputs["L0_attn_x"], np.float32))
    B, T, DM = x.shape
    TLOC = T // 2
    assert TLOC == cfg.TLOC and DM == cfg.DM

    Wq = np.ascontiguousarray(np.asarray(inputs["L0_attn_W_q"], np.float32))
    Wk = np.ascontiguousarray(np.asarray(inputs["L0_attn_W_k"], np.float32))
    Wv = np.ascontiguousarray(np.asarray(inputs["L0_attn_W_v"], np.float32))
    Wo = np.ascontiguousarray(np.asarray(inputs["L0_attn_W_o"], np.float32))
    w_i = np.asarray(inputs["L0_attn_w_i"], np.float32)
    w_f = np.asarray(inputs["L0_attn_w_f"], np.float32)
    b_i = np.asarray(inputs["L0_attn_b_i"], np.float32)
    b_f = np.asarray(inputs["L0_attn_b_f"], np.float32)
    b_o = np.asarray(inputs["L0_attn_b_o"], np.float32)
    gamma = np.asarray(inputs["L0_attn_gamma"], np.float32)
    beta = np.asarray(inputs["L0_attn_beta"], np.float32)
    w1 = np.asarray(inputs["L0_ffn_w1"], np.float32)
    b1 = np.asarray(inputs["L0_ffn_b1"], np.float32)
    w2 = np.asarray(inputs["L0_ffn_w2"], np.float32)
    b2 = np.asarray(inputs["L0_ffn_b2"], np.float32)

    FC, HC = cfg.FC, cfg.HC
    Wif = np.ascontiguousarray(np.concatenate([w_i, w_f], axis=1))
    b_if = np.ascontiguousarray(np.stack([b_i, b_f], axis=1))
    b_o_c = np.ascontiguousarray(b_o.reshape(FC, 128).T)
    gamma_r = np.ascontiguousarray(np.tile(gamma[None, :], (128, 1)))
    beta_r = np.ascontiguousarray(np.tile(beta[None, :], (128, 1)))
    w1T = np.ascontiguousarray(w1.T)
    w2T = np.ascontiguousarray(w2.T)
    b1_c = np.ascontiguousarray(b1.reshape(HC, 128).T)
    b2_c = np.ascontiguousarray(b2.reshape(FC, 128).T)
    import ml_dtypes
    identity = np.eye(128, dtype=ml_dtypes.bfloat16)
    maskT = np.triu(np.ones((128, 128), np.float32)).astype(ml_dtypes.bfloat16)

    groups = [[2 * i, 2 * i + 1] for i in range(cfg.NCORE // 2)]
    nc = build_nc(cfg, groups)

    in_maps = []
    for c in range(cfg.NCORE):
        bidx, half = c // 2, c % 2
        xs = np.ascontiguousarray(x[bidx, half * TLOC:(half + 1) * TLOC, :])
        in_maps.append(dict(
            x=xs, Wq=Wq, Wk=Wk, Wv=Wv, Wo=Wo, Wif=Wif, b_if=b_if, b_o_c=b_o_c,
            gamma_r=gamma_r, beta_r=beta_r, w1T=w1T, w1h=w1, w2T=w2T, w2h=w2,
            b1_c=b1_c, b2_c=b2_c, ident=identity, maskT=maskT,
            isr=np.full((128, 1), float(half), np.float32),
        ))

    res = run_bass_kernel_spmd(nc, in_maps, core_ids=list(range(cfg.NCORE)))
    out = np.zeros((B, T, DM), np.float32)
    for c in range(cfg.NCORE):
        bidx, half = c // 2, c % 2
        out[bidx, half * TLOC:(half + 1) * TLOC, :] = res.results[c]["y"]
    return out


# version 34 (final)
# speedup vs baseline: 1.0126x; 1.0059x over previous
import os
import numpy as np
from contextlib import ExitStack

KLIM = float(os.environ.get("KLIM", "99"))
KSUB = float(os.environ.get("KSUB", "99"))
MMSAFE = int(os.environ.get("MMSAFE", "2"))
DR_S = int(os.environ.get("DR_S", "1"))
DR_V = int(os.environ.get("DR_V", "1"))

import concourse.bass as bass
import concourse.mybir as mybir
from concourse.bass_utils import run_bass_kernel_spmd

F32 = mybir.dt.float32
BF16 = mybir.dt.bfloat16
AF = mybir.ActivationFunctionType
ALU = mybir.AluOpType
AX = mybir.AxisListType

LN_EPS = 1e-5
NEG_BIG = -1.0e30


class Cfg:
    def __init__(self, DM=1024, H=16, DF=4096, TLOC=2048, BLK=512, NCORE=8):
        self.DM, self.H, self.HD, self.DF = DM, H, 64, DF
        assert H * 64 == DM
        self.TLOC, self.L, self.BLK = TLOC, 128, BLK
        self.NBLK = TLOC // BLK
        self.NC = TLOC // 128
        self.CPB = BLK // 128
        self.FC = DM // 128
        self.HC = DF // 128
        self.NT = TLOC // 128
        self.HP = H // 2
        self.NCORE = NCORE
        self.FBLK = min(512, TLOC)
        self.NFB = TLOC // self.FBLK


def build_nc(cfg: Cfg, groups):
    nc = bass.Bass("TRN2", target_bir_lowering=False)
    DM, H, DF, TLOC = cfg.DM, cfg.H, cfg.DF, cfg.TLOC
    L, NC, BLK, NBLK, CPB = cfg.L, cfg.NC, cfg.BLK, cfg.NBLK, cfg.CPB
    FC, HC, NT, HP = cfg.FC, cfg.HC, cfg.NT, cfg.HP
    FBLK, NFB = cfg.FBLK, cfg.NFB
    SL = 65
    NSL = CPB + 1
    NBRAW = max(1, HP // 4)
    NBDEN = max(1, H // 8)
    PPB = HP // 2

    dt = nc.dram_tensor
    x_in = dt("x", [TLOC, DM], F32, kind="ExternalInput").ap()
    Wq = dt("Wq", [DM, DM], F32, kind="ExternalInput").ap()
    Wk = dt("Wk", [DM, DM], F32, kind="ExternalInput").ap()
    Wv = dt("Wv", [DM, DM], F32, kind="ExternalInput").ap()
    Wo = dt("Wo", [DM, DM], F32, kind="ExternalInput").ap()
    Wif_in = dt("Wif", [DM, 32], F32, kind="ExternalInput").ap()
    b_if = dt("b_if", [16, 2], F32, kind="ExternalInput").ap()
    b_o_c = dt("b_o_c", [128, FC], F32, kind="ExternalInput").ap()
    gamma_r = dt("gamma_r", [128, DM], F32, kind="ExternalInput").ap()
    beta_r = dt("beta_r", [128, DM], F32, kind="ExternalInput").ap()
    w1T_in = dt("w1T", [DM, DF], F32, kind="ExternalInput").ap()
    w1h_in = dt("w1h", [DF, DM], F32, kind="ExternalInput").ap()
    w2T_in = dt("w2T", [DF, DM], F32, kind="ExternalInput").ap()
    w2h_in = dt("w2h", [DM, DF], F32, kind="ExternalInput").ap()
    b1_c = dt("b1_c", [128, HC], F32, kind="ExternalInput").ap()
    b2_c = dt("b2_c", [128, FC], F32, kind="ExternalInput").ap()
    ident = dt("ident", [128, 128], BF16, kind="ExternalInput").ap()
    maskT_in = dt("maskT", [128, 128], BF16, kind="ExternalInput").ap()
    isr_in = dt("isr", [128, 1], F32, kind="ExternalInput").ap()
    y_out = dt("y", [TLOC, DM], F32, kind="ExternalOutput").ap()

    NROW = 10 * NC + 2 * NBLK * NSL
    s1sc = dt("s1sc", [DM, DF], BF16, kind="Internal")
    s2sc = dt("s2sc", [DF, DM], BF16, kind="Internal")
    rowsc = dt("rowsc", [16, NROW], F32, kind="Internal")
    gsc = dt("gsc", [16, TLOC], BF16, kind="Internal")
    prow = dt("prow", [TLOC, 16], F32, kind="Internal")
    cm_in = dt("cm_in", [16, 1], F32, kind="Internal")
    cm_out = dt("cm_out", [16, 1], F32, kind="Internal")
    cs_in = dt("cs_in", [128, HP * SL], F32, kind="Internal")
    cs_out = dt("cs_out", [128, HP * SL], F32, kind="Internal")
    Qhbm = dt("Qhbm", [128, FC * TLOC], BF16, kind="Internal")
    Ohbm = dt("Ohbm", [128, FC * TLOC], BF16, kind="Internal")
    N0hbm = dt("N0hbm", [128, HP * TLOC], BF16, kind="Internal")

    with ExitStack() as ctx:
        def sbt(name, shape, dtyp):
            return ctx.enter_context(nc.sbuf_tensor(name, shape, dtyp))

        def pst(name, shape, dtyp):
            return ctx.enter_context(nc.psum_tensor(name, shape, dtyp))

        xres = sbt("xres", [128, NT * DM], BF16)
        xT = sbt("xT", [128, FC * BLK], BF16)
        SHR = sbt("SHR", [128, max(FC * BLK, 2 * H * L, CPB * H * 64)], BF16)
        Qwin = sbt("Qwin", [128, FC * BLK], BF16)
        N0stg = sbt("N0stg", [128, HP * 128], BF16)
        Kfm = sbt("Kfm", [128, FC * BLK], BF16)
        Vfm = sbt("Vfm", [128, FC * BLK], BF16)
        KPT = sbt("KPT", [128, max(CPB * DM, 4 * DM)], BF16)
        VTp = SHR
        BvB = SHR[:, 0:H * L]
        BkB = SHR[:, H * L:2 * H * L]
        Wwin = sbt("Wwin", [128, 2 * FC * 128], BF16)
        rawZ = sbt("rawZ", [128, HP * SL * NSL], BF16)
        scanZ = sbt("scanZ", [128, HP * SL * NSL], BF16)
        scanS = sbt("scanS", [128, HP * SL * CPB], BF16)
        sXcC = sbt("sXcC", [128, HP * SL], BF16)
        gI = sbt("gI", [16, TLOC], F32)
        gF = sbt("gF", [16, TLOC], F32)
        gB = sbt("gB", [16, TLOC], F32)
        gMi = sbt("gMi", [16, TLOC], F32)
        gU = gI
        gM = gF
        GrowT = sbt("GrowT", [16, TLOC], BF16)
        gSc = sbt("gSc", [16, NROW], F32)
        gTf = sbt("gTf", [128, NC * 16], F32)
        gTb = sbt("gTb", [128, NC * 16], BF16)
        arep = sbt("arep", [128, SL * NSL], F32)
        areptmp = sbt("areptmp", [128, HP * NSL], F32)
        rrep = sbt("rrep", [128, HP * NC], F32)
        rhorep = sbt("rhorep", [128, HP * NC], F32)
        cpxrep = sbt("cpxrep", [128, HP * NC], F32)
        Grep = sbt("Grep", [128, TLOC], BF16)
        dslab = sbt("dslab", [128, 1024], BF16)
        sId = sbt("sId", [128, 128], BF16)
        sMk = sbt("sMk", [128, 128], BF16)
        sOne = sbt("sOne", [128, 1], BF16)
        sIsr = sbt("sIsr", [128, 1], F32)
        sBif = sbt("sBif", [16, 2], F32)
        sBo = sbt("sBo", [128, FC], F32)
        sGam = sbt("sGam", [128, DM], BF16)
        sBet = sbt("sBet", [128, DM], BF16)
        sA1 = sbt("sA1", [128, HC], F32)
        sA2 = sbt("sA2", [128, FC], F32)
        sB1 = sbt("sB1", [128, HC], F32)
        sB2 = sbt("sB2", [128, FC], F32)
        sMu = sbt("sMu", [128, 6 * NT + 40], F32)
        sIdF = sbt("sIdF", [16, 16], F32)
        sSend = sbt("sSend", [128, HP * SL], F32)
        sXsum = sbt("sXsum", [128, HP * SL], F32)
        fscr = sbt("fscr", [128, 2 * DM], F32)
        fscr2 = sbt("fscr2", [128, max(DF // 2, CPB * DM, TLOC, (FBLK // 128) * DM)], BF16)
        g1 = sbt("g1", [128, max(HC * FBLK, NT * DM, 14 * DM)], BF16)
        x2T = xT if FC * BLK >= FC * FBLK else sbt("x2T", [128, FC * FBLK], BF16)
        f2win = Vfm if FC * BLK >= HC * 128 else sbt("f2win", [128, HC * 128], BF16)
        ffnO = Kfm if FC * BLK >= max(FC * FBLK, (FBLK // 128) * DM) else sbt("ffnO", [128, max(FC * FBLK, (FBLK // 128) * DM)], BF16)

        nc.dbg = dict(gI=gI[:], gF=gF[:], gB=gB[:], gMi=gMi[:], gSc=gSc[:],
                      Wwin=Wwin[:], xT=xT[:], fscr=fscr[:], fscr2=fscr2[:],
                      sBif=sBif[:])

        PFa = pst("PFa", [128, 512], F32)
        PFb = pst("PFb", [128, 512], F32)
        PFc = pst("PFc", [128, 512], F32)
        PFd = pst("PFd", [128, 512], F32)
        PBa = pst("PBa", [128, 1024], BF16)
        PBb = pst("PBb", [128, 1024], BF16)
        PXa = pst("PXa", [128, 512], F32)
        PXb = pst("PXb", [128, 512], F32)
        PF = [PFa, PFb, PFc, PFd]
        PB = [PBa, PBb]
        PX = [PXa, PXb]

        class Ctr:
            def __init__(self, name, step=1):
                self.sem = ctx.enter_context(nc.semaphore(name))
                self.val = 0
                self.step = step

            def inc(self, bi, n=None):
                n = self.step if n is None else n
                self.val += n
                return bi.then_inc(self.sem, n)

        dfin = gMi
        den0 = gB.bitcast(BF16)[:, 0:TLOC]
        denXr = gB.bitcast(BF16)[:, TLOC:2 * TLOC]
        Grow = GrowT
        ld = Ctr("ld", 16)
        ldw = Ctr("ldw", 16)
        gd = Ctr("gd", 16)
        cc = Ctr("cc", 1)
        tC = Ctr("tC", 1)
        vC = Ctr("vC", 1)
        sC = Ctr("sC", 1)
        bT = Ctr("bT", 1)
        bV = Ctr("bV", 1)
        bS = Ctr("bS", 1)
        st = Ctr("st", 16)

        block = ctx.enter_context(nc.Block())

        def T(fn):
            @block.tensor
            def _(t):
                fn(t)

        def V(fn):
            @block.vector
            def _(v):
                fn(v)

        def S(fn):
            @block.scalar
            def _(s):
                fn(s)

        def Y(fn):
            @block.sync
            def _(s):
                fn(s)

        def G(fn):
            @block.gpsimd
            def _(g):
                fn(g)

        mmrb = [None]

        def mmn(t, rb):
            if MMSAFE == 1:
                t.drain()
                mmrb[0] = rb
                return
            if rb is None:
                mmrb[0] = None
                return
            if mmrb[0] is not None and mmrb[0] != rb:
                t.drain() if MMSAFE == 2 else t.nop()
            mmrb[0] = rb

        def barrier():
            tv, vv, sv = bT.val + 1, bV.val + 1, bS.val + 1

            def tf(t):
                t.drain()
                bT.inc(t.nop())
                t.wait_ge(bV.sem, vv)
                t.wait_ge(bS.sem, sv)

            def vf(v):
                DR_V and v.drain()
                bV.inc(v.engine_nop())
                v.wait_ge(bT.sem, tv)
                v.wait_ge(bS.sem, sv)

            def sf(s):
                DR_S and s.drain()
                bS.inc(s.nop())
                s.wait_ge(bT.sem, tv)
                s.wait_ge(bV.sem, vv)
            T(tf)
            V(vf)
            S(sf)

        def loads(s):
            for ap_, src in [(sId[:], ident), (sMk[:], maskT_in), (sIsr[:], isr_in),
                             (sBif[:], b_if), (sBo[:], b_o_c),
                             (sB1[:], b1_c), (sB2[:], b2_c)]:
                ld.inc(s.dma_start(out=ap_, in_=src))
        Y(loads)

        def loadx(g):
            for j in range(NT):
                gd.inc(g.dma_start(out=xres[:, j * DM:(j + 1) * DM],
                                   in_=x_in[j * 128:(j + 1) * 128, :]))
            g.wait_ge(gd.sem, gd.val)
        G(loadx)
        LDA0 = ld.val
        LDX = gd.val

        def gbl(s):
            ld.inc(s.dma_start(out=fscr[:, DM:2 * DM], in_=gamma_r))
        Y(gbl)
        LDG = ld.val

        def gbc(v):
            v.wait_ge(ld.sem, LDG)
            v.tensor_copy(sGam[:, 0:DM], fscr[:, DM:2 * DM])
            DR_V and v.drain()
            vC.inc(v.engine_nop())
        V(gbc)
        VGB = vC.val

        def gbl2(s):
            s.wait_ge(vC.sem, VGB)
            ld.inc(s.dma_start(out=fscr[:, DM:2 * DM], in_=beta_r))
        Y(gbl2)
        LDB = ld.val

        def gbc2(v):
            v.wait_ge(ld.sem, LDB)
            v.tensor_copy(sBet[:, 0:DM], fscr[:, DM:2 * DM])
            DR_V and v.drain()
            vC.inc(v.engine_nop())
        V(gbc2)
        LDA = ld.val
        VGB2 = vC.val

        def init_consts(v):
            v.wait_ge(ld.sem, LDG)
            v.tensor_copy(sIdF[:], sId[0:16, 0:16])
            DR_V and v.drain()
            v.memset(sOne[:], 1.0)
            v.memset(gI[:], 0.0)
            v.memset(gF[:], 0.0)
            v.memset(gB[:], 0.0)
            v.memset(gMi[:], 0.0)
            v.memset(fscr[:, 0:512], 0.0)
            v.memset(fscr2[:, 0:1024], 0.0)
            DR_V and v.drain()
            for pb_ in (PFa, PFb, PFc, PFd, PXa, PXb):
                v.tensor_copy(pb_[:], fscr[:, 0:512])
            v.memset(gSc[:], 0.0)
            v.memset(rawZ[:], 0.0)
            v.memset(scanZ[:], 0.0)
            DR_V and v.drain()
            vC.inc(v.engine_nop())
        V(init_consts)
        VIC = vC.val

        def wifload(s):
            s.wait_ge(vC.sem, VIC)
            for fc in range(FC):
                ld.inc(s.dma_start(out=fscr[:, fc * 32:(fc + 1) * 32],
                                   in_=Wif_in[fc * 128:(fc + 1) * 128, :]))
        Y(wifload)
        LDWIF = ld.val

        def init_pb(t):
            t.wait_ge(vC.sem, VIC)
            t.wait_ge(ld.sem, LDG)
            for k in range(8):
                t.transpose(PBa[:, k * 128:(k + 1) * 128], fscr2[:, 0:128], sId[:])
                t.transpose(PBb[:, k * 128:(k + 1) * 128], fscr2[:, 0:128], sId[:])
            t.drain()
            tC.inc(t.nop())
        T(init_pb)

        def ln1_sums(v):
            v.wait_ge(ld.sem, LDG)
            v.wait_ge(gd.sem, LDX)
            for j in range(NT):
                v.reduce_sum(sMu[:, j:j + 1], xres[:, j * DM:(j + 1) * DM], axis=AX.X)
            DR_V and v.drain()
            v.tensor_scalar_mul(sMu[:, 0:NT], sMu[:, 0:NT], 1.0 / DM)
            DR_V and v.drain()
            vC.inc(v.engine_nop())
        V(ln1_sums)

        def ln1_sq(s):
            s.wait_ge(ld.sem, LDA)
            s.wait_ge(gd.sem, LDX)
            s.wait_ge(vC.sem, VGB2)
            for j in range(NT):
                s.activation(fscr[:, DM:2 * DM], xres[:, j * DM:(j + 1) * DM], AF.Square,
                             accum_out=sMu[:, NT + j:NT + j + 1])
                DR_S and s.drain()
            sC.inc(s.nop())
        S(ln1_sq)
        SQ1 = sC.val

        def ln1_var(v):
            v.wait_ge(sC.sem, SQ1)
            mu, ex2 = sMu[:, 0:NT], sMu[:, NT:2 * NT]
            var = sMu[:, 4 * NT:5 * NT]
            v.tensor_scalar_mul(ex2, ex2, 1.0 / DM)
            DR_V and v.drain()
            v.tensor_tensor(var, mu, mu, ALU.mult)
            DR_V and v.drain()
            v.tensor_sub(var, ex2, var)
            DR_V and v.drain()
            v.tensor_scalar_add(var, var, LN_EPS)
            DR_V and v.drain()
            vC.inc(v.engine_nop())
        V(ln1_var)
        VV1 = vC.val

        def ln1_rs(s):
            s.wait_ge(vC.sem, VV1)
            var, rs = sMu[:, 4 * NT:5 * NT], sMu[:, 3 * NT:4 * NT]
            s.activation(rs, var, AF.Ln)
            DR_S and s.drain()
            s.activation(rs, rs, AF.Exp, scale=-0.5)
            DR_S and s.drain()
            sC.inc(s.nop())
        S(ln1_rs)
        SQ2 = sC.val

        def ln1_bias(v):
            v.wait_ge(sC.sem, SQ2)
            bias = sMu[:, 2 * NT:3 * NT]
            v.tensor_tensor(bias, sMu[:, 0:NT], sMu[:, 3 * NT:4 * NT], ALU.mult)
            DR_V and v.drain()
            v.tensor_scalar_mul(bias, bias, -1.0)
            DR_V and v.drain()
            vC.inc(v.engine_nop())
        V(ln1_bias)

        def ln1_apply_block(s, b):
            for jj in range(CPB):
                j = b * CPB + jj
                s.activation(fscr2[:, jj * DM:(jj + 1) * DM],
                             xres[:, j * DM:(j + 1) * DM], AF.Identity,
                             bias=sMu[:, 2 * NT + j:2 * NT + j + 1],
                             scale=sMu[:, 3 * NT + j:3 * NT + j + 1])

        def xpose_block_emit(pb_free=None):
            n = CPB * FC
            tv0, vv0 = tC.val, vC.val
            if pb_free is None:
                pb_free = vv0

            def tfn(t, pb_free=pb_free):
                t.wait_ge(sC.sem, sC.val)
                for i in range(n):
                    jj, fc = divmod(i, FC)
                    if i >= 2:
                        t.wait_ge(vC.sem, vv0 + i - 1)
                    else:
                        t.wait_ge(vC.sem, pb_free)
                    tC.inc(t.transpose(PB[i % 2][:, 0:128],
                                fscr2[:, jj * DM + fc * 128:jj * DM + fc * 128 + 128],
                                sId[:]))

            def vfn(v):
                for i in range(n):
                    jj, fc = divmod(i, FC)
                    v.wait_ge(tC.sem, tv0 + i + 1)
                    vC.inc(v.tensor_copy(
                        xT[:, fc * BLK + jj * 128:fc * BLK + jj * 128 + 128],
                        PB[i % 2][:, 0:128]))
            T(tfn)
            V(vfn)

        if KLIM < 1:
            def tail1(s):
                for j in range(NT):
                    st.inc(s.dma_start(out=y_out[j * 128:(j + 1) * 128, :],
                                       in_=fscr[:, 0:DM]))
                s.wait_ge(st.sem, st.val)
            Y(tail1)

            def tailg1(g):
                g.wait_ge(st.sem, st.val)
            G(tailg1)
            return nc
        def wif_cast(v):
            v.wait_ge(ld.sem, LDWIF)
            v.tensor_copy(Wwin[:, 0:FC * 32], fscr[:, 0:FC * 32])
            DR_V and v.drain()
            vC.inc(v.engine_nop())
        V(wif_cast)

        for b in range(NBLK):
            tb = b * BLK

            def sA_(s, b=b):
                s.wait_ge(vC.sem, vC.val)
                s.wait_ge(tC.sem, tC.val)
                ln1_apply_block(s, b)
                DR_S and s.drain()
                sC.inc(s.nop())
            S(sA_)
            xpose_block_emit()

            def tif(t, b=b):
                t.wait_ge(vC.sem, vC.val)
                wid = min(BLK, 512)
                for q in range(BLK // wid):
                    for fc in range(FC):
                        t.matmul(PXa[0:16, q * wid:q * wid + wid],
                                 Wwin[0:128, fc * 32:fc * 32 + 16],
                                 xT[:, fc * BLK + q * wid:fc * BLK + (q + 1) * wid],
                                 start=(fc == 0), stop=(fc == FC - 1))
                    for fc in range(FC):
                        t.matmul(PXb[0:16, q * wid:q * wid + wid],
                                 Wwin[0:128, fc * 32 + 16:fc * 32 + 32],
                                 xT[:, fc * BLK + q * wid:fc * BLK + (q + 1) * wid],
                                 start=(fc == 0), stop=(fc == FC - 1))
                t.drain()
                tC.inc(t.nop())
            T(tif)
            TIF = tC.val

            def vif(v, b=b, tb=tb, TIF=TIF):
                v.wait_ge(tC.sem, TIF)
                v.tensor_scalar_add(gI[:, tb:tb + BLK], PXa[0:16, 0:BLK], sBif[0:16, 0:1])
                DR_V and v.drain()
                v.tensor_scalar_add(gF[:, tb:tb + BLK], PXb[0:16, 0:BLK], sBif[0:16, 1:2])
                DR_V and v.drain()
                binit = 0.0 if b == 0 else gB[:, tb - 1:tb]
                v.tensor_tensor_scan(gB[:, tb:tb + BLK], gF[:, tb:tb + BLK],
                                     gSc[:, 0:1].to_broadcast([16, BLK]),
                                     binit, ALU.add, ALU.add)
                DR_V and v.drain()
                minit = NEG_BIG if b == 0 else gMi[:, tb - 1:tb]
                v.tensor_tensor_scan(gMi[:, tb:tb + BLK], gF[:, tb:tb + BLK],
                                     gI[:, tb:tb + BLK], minit, ALU.add, ALU.max)
                DR_V and v.drain()
                vC.inc(v.engine_nop())
            V(vif)

        if KLIM < 2:
            def tail2(s):
                s.wait_ge(vC.sem, vC.val)
                for j in range(NT):
                    st.inc(s.dma_start(out=y_out[j * 128:(j + 1) * 128, :],
                                       in_=fscr[:, 0:DM]))
                s.wait_ge(st.sem, st.val)
            Y(tail2)

            def tailg2(g):
                g.wait_ge(st.sem, st.val)
            G(tailg2)
            return nc
        S2END, V2END, T2END = sC.val, vC.val, tC.val
        def msend_prep(v):
            v.tensor_tensor(gSc[:, 2:3], gMi[:, TLOC - 1:TLOC], gB[:, TLOC - 1:TLOC], ALU.max)
            DR_V and v.drain()
            vC.inc(v.engine_nop())
        V(msend_prep)
        VMS = vC.val

        def msend(g):
            g.wait_ge(vC.sem, VMS)
            gd.inc(g.dma_start(out=cm_in[:, :], in_=gSc[:, 2:3]))
            g.wait_ge(gd.sem, gd.val)
            cc.inc(g.collective_compute(
                "AllReduce", ALU.add, replica_groups=groups,
                ins=[cm_in.ap().opt()], outs=[cm_out.ap().opt()]))
            g.wait_ge(cc.sem, cc.val)
            gd.inc(g.dma_start(out=gSc[:, 3:4], in_=cm_out[:, :]))
            g.wait_ge(gd.sem, gd.val)
        G(msend)
        GD_M = gd.val

        if KLIM < 2.5:
            def tail25(s):
                for j in range(NT):
                    st.inc(s.dma_start(out=y_out[j * 128:(j + 1) * 128, :],
                                       in_=fscr[:, 0:DM]))
                s.wait_ge(st.sem, st.val)
            Y(tail25)

            def tailg25(g):
                g.wait_ge(st.sem, st.val)
            G(tailg25)
            return nc
        BP, MI, MB = 8, 8 + NC, 8 + 2 * NC
        EB = 8 + 3 * NC
        r_, a_, rho_, cpx_ = EB, EB + NC, EB + 2 * NC, EB + 3 * NC
        S1_ = EB + 4 * NC
        AB = S1_ + 2 * NC

        def cview(rowap):
            return rowap.rearrange("p (c o) -> p c o", o=1).to_broadcast([16, NC, L])

        def lview(rowap):
            return rowap.rearrange("p (c l) -> p c l", l=L)

        def gatefin(v):
            v.wait_ge(gd.sem, GD_M)
            v.tensor_sub(gSc[:, 4:5], gSc[:, 3:4], gSc[:, 2:3])
            DR_V and v.drain()
            v.tensor_tensor(gSc[:, 4:5], gSc[:, 4:5], sIsr[0:16, 0:1], ALU.mult)
            DR_V and v.drain()
            v.tensor_scalar_add(gM[:, 0:TLOC], gB[:, 0:TLOC], gSc[:, 4:5])
            DR_V and v.drain()
            v.tensor_tensor(gMi[:, 0:TLOC], gMi[:, 0:TLOC], gM[:, 0:TLOC], ALU.max)
            DR_V and v.drain()
            v.memset(gSc[:, BP:BP + 1], 0.0)
            DR_V and v.drain()
            if NC > 1:
                v.tensor_copy(gSc[:, BP + 1:BP + NC], gB[:, L - 1:TLOC - 1:L])
                DR_V and v.drain()
                v.tensor_copy(gSc[:, MI + 1:MI + NC], gMi[:, L - 1:TLOC - 1:L])
                DR_V and v.drain()
            v.tensor_copy(gSc[:, MI:MI + 1], gSc[:, 4:5])
            DR_V and v.drain()
            bl = gM
            v.tensor_tensor(lview(bl[:, 0:TLOC]), lview(gB[:, 0:TLOC]),
                            cview(gSc[:, BP:BP + NC]), ALU.subtract)
            DR_V and v.drain()
            v.tensor_sub(gU[:, 0:TLOC], gI[:, 0:TLOC], bl[:, 0:TLOC])
            DR_V and v.drain()
            v.tensor_scalar_mul(gM[:, 0:TLOC], gM[:, 0:TLOC], -1.0)
            DR_V and v.drain()
            v.tensor_add(gM[:, 0:TLOC], gM[:, 0:TLOC], gMi[:, 0:TLOC])
            DR_V and v.drain()
            v.reduce_max(gSc[:, MB:MB + NC], lview(gM[:, 0:TLOC]), axis=AX.X)
            DR_V and v.drain()
            v.tensor_tensor(lview(gU[:, 0:TLOC]), lview(gU[:, 0:TLOC]),
                            cview(gSc[:, MB:MB + NC]), ALU.subtract)
            DR_V and v.drain()
            v.tensor_scalar_mul(gM[:, 0:TLOC], gM[:, 0:TLOC], -1.0)
            DR_V and v.drain()
            v.tensor_tensor(lview(gM[:, 0:TLOC]), lview(gM[:, 0:TLOC]),
                            cview(gSc[:, MB:MB + NC]), ALU.add)
            DR_V and v.drain()
            v.tensor_copy(gSc[:, S1_:S1_ + NC], gB[:, L - 1:TLOC:L])
            DR_V and v.drain()
            v.tensor_copy(gSc[:, S1_ + NC:S1_ + 2 * NC], gMi[:, L - 1:TLOC:L])
            DR_V and v.drain()
            v.tensor_sub(gSc[:, r_:r_ + NC], gSc[:, MB:MB + NC], gSc[:, S1_ + NC:S1_ + 2 * NC])
            DR_V and v.drain()
            v.tensor_add(gSc[:, r_:r_ + NC], gSc[:, r_:r_ + NC], gSc[:, S1_:S1_ + NC])
            DR_V and v.drain()
            v.tensor_sub(gSc[:, r_:r_ + NC], gSc[:, r_:r_ + NC], gSc[:, BP:BP + NC])
            DR_V and v.drain()
            v.tensor_sub(gSc[:, a_:a_ + NC], gSc[:, MI:MI + NC], gSc[:, S1_ + NC:S1_ + 2 * NC])
            DR_V and v.drain()
            v.tensor_add(gSc[:, a_:a_ + NC], gSc[:, a_:a_ + NC], gSc[:, S1_:S1_ + NC])
            DR_V and v.drain()
            v.tensor_sub(gSc[:, a_:a_ + NC], gSc[:, a_:a_ + NC], gSc[:, BP:BP + NC])
            DR_V and v.drain()
            v.tensor_sub(gSc[:, rho_:rho_ + NC], gSc[:, MI:MI + NC], gSc[:, MB:MB + NC])
            DR_V and v.drain()
            v.tensor_scalar_add(gSc[:, cpx_:cpx_ + NC], gSc[:, BP:BP + NC], gSc[:, 4:5])
            DR_V and v.drain()
            v.tensor_sub(gSc[:, cpx_:cpx_ + NC], gSc[:, cpx_:cpx_ + NC], gSc[:, MB:MB + NC])
            DR_V and v.drain()
            vC.inc(v.engine_nop())
        V(gatefin)
        VGF = vC.val

        if KLIM < 3.2:
            def tailx32(s):
                for j in range(NT):
                    st.inc(s.dma_start(out=y_out[j * 128:(j + 1) * 128, :],
                                       in_=fscr[:, 0:DM]))
                s.wait_ge(st.sem, st.val)
            Y(tailx32)

            def tailgx32(g):
                g.wait_ge(st.sem, st.val)
            G(tailgx32)
            return nc
        def gateexp(s):
            s.wait_ge(vC.sem, VGF)
            s.activation(gU[:, 0:TLOC], gU[:, 0:TLOC], AF.Exp)
            DR_S and s.drain()
            s.activation(gM[:, 0:TLOC], gM[:, 0:TLOC], AF.Exp)
            DR_S and s.drain()
            s.activation(gSc[:, EB:EB + 4 * NC], gSc[:, EB:EB + 4 * NC], AF.Exp)
            DR_S and s.drain()
            sC.inc(s.nop())
        S(gateexp)
        SGE = sC.val

        if KLIM < 3.4:
            def tailx34(s):
                for j in range(NT):
                    st.inc(s.dma_start(out=y_out[j * 128:(j + 1) * 128, :],
                                       in_=fscr[:, 0:DM]))
                s.wait_ge(st.sem, st.val)
            Y(tailx34)

            def tailgx34(g):
                g.wait_ge(st.sem, st.val)
            G(tailgx34)
            return nc
        def ahat_rows(v):
            v.wait_ge(sC.sem, SGE)
            v.memset(gB[:], 0.0)
            DR_V and v.drain()
            for b in range(NBLK):
                v.memset(gSc[:, AB + b * NSL:AB + b * NSL + 1], 0.0)
                DR_V and v.drain()
                v.tensor_copy(gSc[:, AB + b * NSL + 1:AB + (b + 1) * NSL],
                              gSc[:, a_ + b * CPB:a_ + (b + 1) * CPB])
                DR_V and v.drain()
            vC.inc(v.engine_nop())
        V(ahat_rows)
        VAH = vC.val

        if KLIM < 3.5:
            def tailx35(s):
                for j in range(NT):
                    st.inc(s.dma_start(out=y_out[j * 128:(j + 1) * 128, :],
                                       in_=fscr[:, 0:DM]))
                s.wait_ge(st.sem, st.val)
            Y(tailx35)

            def tailgx35(g):
                g.wait_ge(st.sem, st.val)
            G(tailgx35)
            return nc
        def rows_ship(g):
            g.wait_ge(vC.sem, VAH)
            gd.inc(g.dma_start(out=rowsc[:, :], in_=gSc[:, 0:NROW]))
            g.wait_ge(gd.sem, gd.val)

            def rep2(dst, base, X):
                for p in range(HP):
                    src = rowsc[2 * p:2 * p + 2, base:base + X]
                    gd.inc(g.dma_start(
                        out=dst[:, p * X:(p + 1) * X],
                        in_=src.rearrange("h (o w) -> h o w", o=1).to_broadcast([2, 64, X])))
            rep2(rrep, r_, NC)
            rep2(rhorep, rho_, NC)
            rep2(cpxrep, cpx_, NC)
            g.wait_ge(gd.sem, gd.val)
        G(rows_ship)
        GD_REP = gd.val

        def arep_build(g, b):
            for p in range(HP):
                src = rowsc[2 * p:2 * p + 2, AB + b * NSL:AB + (b + 1) * NSL]
                gd.inc(g.dma_start(
                    out=areptmp[:, p * NSL:(p + 1) * NSL],
                    in_=src.rearrange("h (o w) -> h o w", o=1).to_broadcast([2, 64, NSL])))

        if KLIM < 3.7:
            def tailx37(s):
                for j in range(NT):
                    st.inc(s.dma_start(out=y_out[j * 128:(j + 1) * 128, :],
                                       in_=fscr[:, 0:DM]))
                s.wait_ge(st.sem, st.val)
            Y(tailx37)

            def tailgx37(g):
                g.wait_ge(st.sem, st.val)
            G(tailgx37)
            return nc
        def pcols():
            sv0 = sC.val
            tv0, vv0 = tC.val, vC.val
            PXp = [PXa, PXb]

            def tfn(t):
                t.wait_ge(sC.sem, sv0)
                for c in range(NC):
                    if c >= 2:
                        t.wait_ge(vC.sem, vv0 + 2 * (c - 1))
                    tC.inc(t.transpose(PXp[c % 2][0:128, 0:16],
                                gU[0:16, c * L:(c + 1) * L], sIdF[:]))

            def vfn(v):
                for c in range(NC):
                    v.wait_ge(tC.sem, tv0 + c + 1)
                    v.tensor_copy(gTb[:, c * 16:(c + 1) * 16], PXp[c % 2][0:128, 0:16])
                    DR_V and v.drain()
                    vC.inc(v.tensor_copy(gTf[:, c * 16:(c + 1) * 16], PXp[c % 2][0:128, 0:16]))
                    vC.inc(v.engine_nop())
            T(tfn)
            V(vfn)
        pcols()
        VPCOLS = vC.val

        def ffnprep_build():
            g1f = g1[:].bitcast(F32)
            SGB = 8 * DM
            DH = DM
            ND = 4
            thunks = []
            sc_hist = [0] * ND
            gd_hist = [gd.val] * ND

            def sign_iter(i, src_ap, dst_ap, first):
                half = i % ND
                sc2 = sc_hist[-ND]

                def yl(s, i=i, half=half, sc2=sc2, src_ap=src_ap, first=first):
                    if first:
                        s.wait_ge(sC.sem, sC.val)
                        s.wait_ge(vC.sem, vC.val)
                    if sc2 > 0:
                        s.wait_ge(sC.sem, sc2)
                    ldw.inc(s.dma_start(out=g1f[:, half * DH:(half + 1) * DH],
                                        in_=src_ap(i)))
                    s.wait_ge(ldw.sem, ldw.val)
                Y(yl)
                lv = ldw.val
                gd2 = gd_hist[-ND]

                def sg(s, lv=lv, half=half, gd2=gd2):
                    s.wait_ge(ldw.sem, lv)
                    if gd2 > 0:
                        s.wait_ge(gd.sem, gd2)
                    s.activation(g1[:, SGB + half * DH:SGB + (half + 1) * DH],
                                 g1f[:, half * DH:(half + 1) * DH], AF.Sign)
                    s.drain()
                    sC.inc(s.nop())
                S(sg)
                sv = sC.val
                sc_hist.append(sv)

                def yst(g, i=i, half=half, sv=sv, dst_ap=dst_ap):
                    g.wait_ge(sC.sem, sv)
                    g.wait_ge(gd.sem, gd.val)
                    gd.inc(g.dma_start(out=dst_ap(i),
                                       in_=g1[:, SGB + half * DH:SGB + (half + 1) * DH]))
                    g.wait_ge(gd.sem, gd.val)
                G(yst)
                gd_hist.append(gd.val)

            vh = [0] * ND

            def alpha_iter(i, src_ap, red_fn, first):
                half = i % ND
                v2 = vh[-ND]

                def yl(s, i=i, half=half, v2=v2, src_ap=src_ap, first=first):
                    if first:
                        s.wait_ge(sC.sem, sC.val)
                        s.wait_ge(vC.sem, vC.val)
                    if v2 > 0:
                        s.wait_ge(vC.sem, v2)
                    ldw.inc(s.dma_start(out=g1f[:, half * DH:(half + 1) * DH],
                                        in_=src_ap(i)))
                    s.wait_ge(ldw.sem, ldw.val)
                Y(yl)
                lv = ldw.val

                def va(v, i=i, half=half, lv=lv, red_fn=red_fn):
                    v.wait_ge(ldw.sem, lv)
                    red_fn(v, i, g1f[:, half * DH:(half + 1) * DH])
                    vC.inc(v.engine_nop())
                V(va)
                vh.append(vC.val)

            NS1 = DF // DH
            for i in range(FC * NS1):
                thunks.append(lambda i=i: sign_iter(
                    i,
                    lambda i: w1T_in[(i // NS1) * 128:(i // NS1) * 128 + 128,
                                     (i % NS1) * DH:(i % NS1 + 1) * DH],
                    lambda i: s1sc[(i // NS1) * 128:(i // NS1) * 128 + 128,
                                   (i % NS1) * DH:(i % NS1 + 1) * DH],
                    i < ND))

            def red1(v, i, ap_):
                v.reduce_sum(sA1[:, i:i + 1], ap_[:, 0:DM], axis=AX.X,
                             apply_absolute_value=True)
                DR_V and v.drain()
            for i in range(HC):
                thunks.append(lambda i=i: alpha_iter(
                    i, lambda i: w1h_in[i * 128:(i + 1) * 128, :], red1, i < ND))

            def vsc1(v):
                v.tensor_scalar_mul(sA1[:, 0:HC], sA1[:, 0:HC], 1.0 / DM)
                DR_V and v.drain()
                vC.inc(v.engine_nop())
            thunks.append(lambda: V(vsc1))

            for i in range(HC):
                thunks.append(lambda i=i: sign_iter(
                    i, lambda i: w2T_in[i * 128:(i + 1) * 128, :],
                    lambda i: s2sc[i * 128:(i + 1) * 128, :], i < ND))

            NS2 = DF // DH
            A2 = 5 * NT + 5

            def red2(v, i, ap_):
                d, hseg = divmod(i, NS2)
                v.reduce_sum(sMu[:, A2 + hseg:A2 + hseg + 1], ap_[:, 0:DH],
                             axis=AX.X, apply_absolute_value=True)
                DR_V and v.drain()
                if hseg == NS2 - 1:
                    for h2 in range(1, NS2):
                        v.tensor_add(sMu[:, A2:A2 + 1], sMu[:, A2:A2 + 1],
                                     sMu[:, A2 + h2:A2 + h2 + 1])
                        DR_V and v.drain()
                    v.tensor_copy(sA2[:, d:d + 1], sMu[:, A2:A2 + 1])
                    DR_V and v.drain()
            for i in range(FC * NS2):
                thunks.append(lambda i=i: alpha_iter(
                    i,
                    lambda i: w2h_in[(i // NS2) * 128:(i // NS2) * 128 + 128,
                                     (i % NS2) * DH:(i % NS2 + 1) * DH],
                    red2, i < ND))

            def vsc2(v):
                v.tensor_scalar_mul(sA2[:, 0:FC], sA2[:, 0:FC], 1.0 / DF)
                DR_V and v.drain()
                vC.inc(v.engine_nop())
            thunks.append(lambda: V(vsc2))
            return thunks

        prep_thunks = ffnprep_build() if KLIM >= 3.9 else []

        slot_gate = [None, None]


        def wait_gate(eng, gate):
            if gate is None:
                eng.wait_ge(bT.sem, bT.val)
                return
            ctr, val = gate
            eng.wait_ge(ctr.sem, val)

        t60_end = [T2END]
        blkT = [T2END]
        scanV = [V2END]
        carry_gate = [None]
        for b in (range(NBLK) if KLIM >= 4 else []):
            tb = b * BLK
            c0 = b * CPB
            t60p = t60_end[-1]

            def s60(s, b=b, t60p=t60p):
                if b == 0:
                    s.wait_ge(vC.sem, V2END)
                s.wait_ge(tC.sem, t60p)
                ln1_apply_block(s, b)
                DR_S and s.drain()
                sC.inc(s.nop())
            S(s60)
            xpose_block_emit(pb_free=(V2END if b == 0 else None))
            t60_end.append(tC.val)
            XTE = vC.val

            if KSUB < 1:
                continue
            bank_gate = {0: None, 1: None}
            spill_gd = []
            pairs = [(w, e) for w in ("k", "v", "q", "o") for e in range(FC)]
            for i, (which, e) in enumerate(pairs):
                slot = i % 2
                bank = i % 2
                Wsrc = {"k": Wk, "v": Wv, "q": Wq, "o": Wo}[which]
                sg = slot_gate[1] if i >= 1 else None

                def yload(g, Wsrc=Wsrc, e=e, slot=slot, sg=sg, b=b, i=i):
                    if b == 0 and i == 0:
                        g.wait_ge(tC.sem, T2END)
                    wait_gate(g, sg)
                    g.wait_ge(gd.sem, gd.val)
                    gd.inc(g.dma_start(
                        out=Wwin[:, slot * FC * 128:(slot + 1) * FC * 128].rearrange(
                            "p (c n) -> p c n", c=FC),
                        in_=Wsrc[0:DM, e * 128:(e + 1) * 128].rearrange(
                            "(c p) n -> p c n", p=128)))
                    g.wait_ge(gd.sem, gd.val)
                G(yload)
                LV = gd.val
                bg = bank_gate[bank]

                def tmm(t, slot=slot, bank=bank, VV=LV, bg=bg, XTE=XTE):
                    t.wait_ge(vC.sem, XTE)
                    t.wait_ge(gd.sem, VV)
                    wait_gate(t, bg)
                    for fc in range(FC):
                        t.matmul(PF[bank][:, 0:BLK],
                                 Wwin[:, slot * FC * 128 + fc * 128:slot * FC * 128 + fc * 128 + 128],
                                 xT[:, fc * BLK:fc * BLK + BLK],
                                 start=(fc == 0), stop=(fc == FC - 1))
                    t.drain()
                    tC.inc(t.nop())
                T(tmm)
                TV = tC.val
                slot_gate[slot] = (tC, TV)
                slot_gate[1] = (tC, TV)
                if which in ("k", "v"):
                    dstT = Kfm if which == "k" else Vfm

                    def ev(v, e=e, bank=bank, TV=TV, dstT=dstT, i=i, bTp=blkT[-1]):
                        if i == 0:
                            v.wait_ge(tC.sem, bTp)
                        v.wait_ge(tC.sem, TV)
                        vC.inc(v.tensor_copy(dstT[:, e * BLK:(e + 1) * BLK], PF[bank][:, 0:BLK]))
                    V(ev)
                    bank_gate[bank] = (vC, vC.val)
                elif which == "q":
                    def ev(s, e=e, bank=bank, TV=TV, tb=tb, bTp=blkT[-1]):
                        if e == 0:
                            s.wait_ge(tC.sem, bTp)
                        s.wait_ge(tC.sem, TV)
                        s.activation(Qwin[:, e * BLK:(e + 1) * BLK],
                                     PF[bank][:, 0:BLK], AF.Identity, scale=0.125)
                        DR_S and s.drain()
                        sC.inc(s.nop())
                    S(ev)
                    bank_gate[bank] = (sC, sC.val)

                    def qsp(g, e=e, sv=sC.val, tb=tb):
                        g.wait_ge(sC.sem, sv)
                        g.wait_ge(gd.sem, gd.val)
                        gd.inc(g.dma_start(out=Qhbm[:, e * TLOC + tb:e * TLOC + tb + BLK],
                                           in_=Qwin[:, e * BLK:(e + 1) * BLK]))
                        g.wait_ge(gd.sem, gd.val)
                    G(qsp)
                    spill_gd.append(gd.val)
                else:
                    def ev(s, e=e, bank=bank, TV=TV, tb=tb, b=b, bTp=blkT[-1]):
                        if e == 0:
                            s.wait_ge(tC.sem, bTp)
                        s.wait_ge(tC.sem, TV)
                        s.activation(SHR[:, e * BLK:(e + 1) * BLK],
                                     PF[bank][:, 0:BLK], AF.Sigmoid, bias=sBo[:, e:e + 1])
                        DR_S and s.drain()
                        sC.inc(s.nop())
                    S(ev)
                    bank_gate[bank] = (sC, sC.val)

                    def osp(g, e=e, sv=sC.val, tb=tb):
                        g.wait_ge(sC.sem, sv)
                        g.wait_ge(gd.sem, gd.val)
                        gd.inc(g.dma_start(out=Ohbm[:, e * TLOC + tb:e * TLOC + tb + BLK],
                                           in_=SHR[:, e * BLK:(e + 1) * BLK]))
                        g.wait_ge(gd.sem, gd.val)
                    G(osp)
                    spill_gd.append(gd.val)
            GD_SPILL = gd.val

            if KSUB < 2:
                continue
            items = [(src, cb, fc) for src in ("k", "v") for cb in range(CPB) for fc in range(FC)]
            tv0, vv0 = tC.val, vC.val

            def t62(t, items=items, tv0=tv0, vv0=vv0):
                for i, (src, cb, fc) in enumerate(items):
                    if i >= 2:
                        t.wait_ge(vC.sem, vv0 + 2 * (i - 1))
                    else:
                        t.wait_ge(vC.sem, vv0)
                    buf = Kfm if src == "k" else Vfm
                    tC.inc(t.transpose(PB[i % 2][:, 0:128],
                                buf[:, fc * BLK + cb * 128:fc * BLK + cb * 128 + 128],
                                sId[:]))
            T(t62)

            def v62(v, items=items, tv0=tv0, c0=c0, bTp=blkT[-1]):
                v.wait_ge(tC.sem, bTp)
                v.wait_ge(gd.sem, GD_SPILL)
                for i, (src, cb, fc) in enumerate(items):
                    v.wait_ge(tC.sem, tv0 + i + 1)
                    cg = c0 + cb
                    if src == "k":
                        for hh in range(2):
                            h = 2 * fc + hh
                            v.tensor_scalar(
                                KPT[:, cb * DM + h * 64:cb * DM + (h + 1) * 64],
                                PB[i % 2][:, hh * 64:(hh + 1) * 64],
                                gTf[:, cg * 16 + h:cg * 16 + h + 1], None, ALU.mult)
                        DR_V and v.drain()
                        vC.inc(v.engine_nop())
                        vC.inc(v.engine_nop())
                    else:
                        vC.inc(v.tensor_copy(
                            VTp[:, cb * H * 64 + fc * 128:cb * H * 64 + (fc + 1) * 128],
                            PB[i % 2][:, 0:128]))
                        vC.inc(v.engine_nop())
            V(v62)
            V62END = vC.val

            if KSUB < 3:
                continue
            def g63(g, b=b, sV=scanV[-1]):
                g.wait_ge(vC.sem, sV)
                g.wait_ge(gd.sem, gd.val)
                arep_build(g, b)
                g.wait_ge(gd.sem, gd.val)
            G(g63)
            GAR = gd.val

            raw_gate = [carry_gate[0]]
            for cb in range(CPB):
                cg = c0 + cb
                rg = raw_gate[0]

                def t63(t, cb=cb, rg=rg, b=b, V62END=V62END):
                    if cb == 0:
                        if b == 0:
                            t.wait_ge(vC.sem, VPCOLS)
                        t.wait_ge(vC.sem, V62END)
                        t.wait_ge(gd.sem, GD_REP)
                    wait_gate(t, rg)
                    for h in range(H):
                        p = h // 2
                        bankp = PX[p // 4]
                        rowb = (h % 2) * 64
                        colb = (p % 4) * SL
                        t.matmul(bankp[rowb:rowb + 64, colb:colb + 64],
                                 VTp[:, cb * H * 64 + h * 64:cb * H * 64 + (h + 1) * 64],
                                 KPT[:, cb * DM + h * 64:cb * DM + (h + 1) * 64],
                                 start=True, stop=True)
                        t.matmul(bankp[rowb:rowb + 64, colb + 64:colb + 65],
                                 KPT[:, cb * DM + h * 64:cb * DM + (h + 1) * 64],
                                 sOne[:, 0:1], start=True, stop=True)
                    t.drain()
                    tC.inc(t.nop())
                T(t63)
                TV63 = tC.val

                def v63(v, cb=cb, cg=cg, TV63=TV63):
                    v.wait_ge(tC.sem, TV63)
                    for bk in range(NBRAW):
                        npair = min(4, HP - bk * 4)
                        pv = PX[bk][:, 0:npair * SL].rearrange("p (q e) -> p q e", e=SL)
                        v.tensor_tensor(
                            rawZ[:, 0:HP * SL * NSL].rearrange(
                                "p (q e n) -> p q e n", e=SL, n=NSL)[:, bk * 4:bk * 4 + npair, :, cb + 1:cb + 2],
                            pv.rearrange("p q (e o) -> p q e o", o=1),
                            rrep[:, 0:HP * NC].rearrange("p (q c) -> p q c", c=NC)[
                                :, bk * 4:bk * 4 + npair, cg:cg + 1]
                            .rearrange("p q (c o) -> p q c o", o=1)
                            .to_broadcast([128, npair, SL, 1]),
                            ALU.mult)
                        DR_V and v.drain()
                    vC.inc(v.engine_nop())
                V(v63)
                raw_gate[0] = (vC, vC.val)

            if KSUB < 4:
                continue
            def v64(v, b=b, c0=c0, bTp=blkT[-1]):
                v.wait_ge(tC.sem, bTp)
                v.wait_ge(gd.sem, GAR)
                if b > 0:
                    v.tensor_copy(
                        rawZ[:, 0:HP * SL * NSL].rearrange("p (q e n) -> p q e n", e=SL, n=NSL)[:, :, :, 0:1],
                        scanZ[:, 0:HP * SL * NSL].rearrange("p (q e n) -> p q e n", e=SL, n=NSL)[:, :, :, CPB:CPB + 1])
                    DR_V and v.drain()
                for p in range(HP):
                    v.tensor_copy(
                        arep[:, 0:SL * NSL].rearrange("p (e n) -> p e n", n=NSL),
                        areptmp[:, p * NSL:(p + 1) * NSL].rearrange("p (o n) -> p o n", o=1)
                        .to_broadcast([128, SL, NSL]))
                    DR_V and v.drain()
                    v.tensor_tensor_scan(
                        scanZ[:, p * SL * NSL:(p + 1) * SL * NSL],
                        arep[:, 0:SL * NSL],
                        rawZ[:, p * SL * NSL:(p + 1) * SL * NSL],
                        0.0, ALU.mult, ALU.add)
                    DR_V and v.drain()
                v.tensor_tensor(
                    scanS[:, 0:HP * SL * CPB].rearrange("p (q n e) -> p q n e", n=CPB, e=SL),
                    scanZ[:, 0:HP * SL * NSL].rearrange("p (q e n) -> p q n e", e=SL, n=NSL)[:, :, 0:CPB, :],
                    rhorep[:, 0:HP * NC].rearrange("p (q c) -> p q c", c=NC)[:, :, c0:c0 + CPB]
                    .rearrange("p q (c o) -> p q c o", o=1).to_broadcast([128, HP, CPB, SL]),
                    ALU.mult)
                DR_V and v.drain()
                vC.inc(v.engine_nop())
            V(v64)
            scanV.append(vC.val)

            if KSUB < 5:
                continue
            chunk_gate = carry_gate[0]
            for cb in range(CPB):
                cg = c0 + cb
                grp_gate = chunk_gate
                for gq in range(H // 4):
                    def t65(t, cb=cb, gq=gq, gg=grp_gate, tb=tb):
                        wait_gate(t, gg)
                        for rb, hhs in ((0, (0, 2)), (64, (1, 3))):
                            for dst, srcb in ((PFa, Vfm), (PFb, Kfm)):
                                for hh in hhs:
                                    h = gq * 4 + hh
                                    fc = h // 2
                                    mmn(t, rb)
                                    t.matmul(dst[:, hh * 128:(hh + 1) * 128],
                                             srcb[rb:rb + 64, fc * BLK + cb * 128:fc * BLK + cb * 128 + 128],
                                             Qwin[rb:rb + 64, fc * BLK + cb * 128:fc * BLK + cb * 128 + 128],
                                             start=True, stop=True)
                        t.drain()
                        tC.inc(t.nop())
                    T(t65)
                    TV65 = tC.val

                    def v65(v, gq=gq, TV65=TV65):
                        v.wait_ge(tC.sem, TV65)
                        mview = sMk[:, 0:128].rearrange("p (o l) -> p o l", o=1).to_broadcast([128, 4, 128])
                        v.tensor_tensor(
                            BvB[:, gq * 512:(gq + 1) * 512].rearrange("p (q l) -> p q l", q=4),
                            PFa[:, 0:512].rearrange("p (q l) -> p q l", q=4),
                            mview, ALU.mult)
                        DR_V and v.drain()
                        v.tensor_tensor(
                            BkB[:, gq * 512:(gq + 1) * 512].rearrange("p (q l) -> p q l", q=4),
                            PFb[:, 0:512].rearrange("p (q l) -> p q l", q=4),
                            mview, ALU.mult)
                        DR_V and v.drain()
                        vC.inc(v.engine_nop())
                    V(v65)
                    grp_gate = (vC, vC.val)
                    VB65 = vC.val
                    if KSUB < 5.5:
                        continue

                    def t66(t, cb=cb, cg=cg, gq=gq, VB65=VB65, tb=tb):
                        t.wait_ge(vC.sem, VB65)
                        for hh in range(4):
                            h = gq * 4 + hh
                            p = h // 2
                            rb = (h % 2) * 64
                            fc = h // 2
                            comb = PF[2 + (p % 2)]
                            ccol = (p // 2) * 128
                            qap = Qwin[rb:rb + 64, fc * BLK + cb * 128:fc * BLK + cb * 128 + 128]
                            mmn(t, None)
                            t.matmul(comb[rb:rb + 64, ccol:ccol + 128],
                                     KPT[:, cb * DM + h * 64:cb * DM + (h + 1) * 64],
                                     BvB[:, h * 128:(h + 1) * 128],
                                     start=True, stop=False)
                            mmn(t, rb)
                            t.matmul(comb[rb:rb + 64, ccol:ccol + 128],
                                     scanS[rb:rb + 64, p * SL * CPB + cb * SL:p * SL * CPB + cb * SL + 64],
                                     qap, start=False, stop=True)
                            dbk = PX[h // 8]
                            drow = 32 * ((h % 8) // 4)
                            dcol = (h % 4) * 128
                            t.matmul(dbk[drow:drow + 1, dcol:dcol + 128],
                                     scanS[rb:rb + 64, p * SL * CPB + cb * SL + 64:p * SL * CPB + cb * SL + 65],
                                     qap, start=True, stop=False)
                            mmn(t, None)
                            t.matmul(dbk[drow:drow + 1, dcol:dcol + 128],
                                     gTb[:, cg * 16 + h:cg * 16 + h + 1],
                                     BkB[:, h * 128:(h + 1) * 128],
                                     start=False, stop=True)
                        t.drain()
                        tC.inc(t.nop())
                    T(t66)
                TV66 = tC.val
                if KSUB < 5.7:
                    chunk_gate = grp_gate
                    continue

                def v66(v, cb=cb, TV66=TV66, tb=tb, nsg=None):
                    v.wait_ge(tC.sem, TV66)
                    if nsg is not None:
                        v.wait_ge(gd.sem, nsg)
                    for bk in range(2):
                        v.tensor_copy(
                            N0stg[:, 0:HP * 128].rearrange("p (f t) -> p f t", f=HP)[
                                :, bk::2, :],
                            PF[2 + bk][:, 0:PPB * 128].rearrange("p (q l) -> p q l", q=PPB))
                        DR_V and v.drain()
                    for bk in range(NBDEN):
                        v.tensor_copy(dslab[:, bk * 512:bk * 512 + 512], PX[bk][:, 0:512])
                        DR_V and v.drain()
                    vC.inc(v.engine_nop())
                V(v66)
                VD = vC.val
                chunk_gate = (vC, VD)
                gprev = gd.val

                def g66(g, cg=cg, VD=VD, gprev=gprev):
                    g.wait_ge(vC.sem, VD)
                    g.wait_ge(gd.sem, gd.val)
                    gd.inc(g.dma_start(
                        out=N0hbm[:, 0:HP * TLOC].rearrange("p (q t) -> p q t", q=HP)[
                            :, :, cg * 128:(cg + 1) * 128],
                        in_=N0stg[:, 0:HP * 128].rearrange("p (q t) -> p q t", q=HP)))
                    for half in range(NBDEN):
                        nh = min(8, H - half * 8)
                        gd.inc(g.dma_start(
                            out=den0[8 * half:8 * half + nh, cg * 128:(cg + 1) * 128],
                            in_=dslab[:, half * 512:(half + 1) * 512]
                            .rearrange("p (q l) -> p q l", q=4)[0:64:32, :, :][0:(nh + 3) // 4, 0:min(4, nh), :]))
                G(g66)
                chunk_gate_gd = gd.val

                def vwait_gd(v, gdv=chunk_gate_gd):
                    v.wait_ge(gd.sem, gdv)
                    vC.inc(v.engine_nop())
                V(vwait_gd)
                chunk_gate = (vC, vC.val)
                npt = (len(prep_thunks) + NBLK * CPB - 1) // (NBLK * CPB)
                i0t = (b * CPB + cb) * npt
                for th in prep_thunks[i0t:i0t + npt]:
                    th()
            blkT.append(tC.val)
            carry_gate[0] = chunk_gate
        _npt = (len(prep_thunks) + NBLK * CPB - 1) // (NBLK * CPB)
        for th in prep_thunks[((NBLK * CPB if KLIM >= 4 else 0)) * _npt:]:
            th()
        GD_PREP = gd.val
        VPREP = vC.val

        if KLIM < 5:
            def tail(s):
                for j in range(NT):
                    st.inc(s.dma_start(out=y_out[j * 128:(j + 1) * 128, :],
                                       in_=fscr[:, 0:DM]))
            Y(tail)

            def tailw(s):
                s.wait_ge(st.sem, st.val)
            Y(tailw)

            def tailg(g):
                g.wait_ge(st.sem, st.val)
            G(tailg)
            return nc

        def v70(v):
            v.tensor_copy(
                sSend[:, 0:HP * SL].rearrange("p (q e o) -> p q e o", e=SL, o=1),
                scanZ[:, 0:HP * SL * NSL].rearrange("p (q e n) -> p q e n", e=SL, n=NSL)[:, :, :, CPB:CPB + 1])
            DR_V and v.drain()
            vC.inc(v.engine_nop())
        V(v70)
        V70 = vC.val

        def g70(g):
            g.wait_ge(vC.sem, V70)
            gd.inc(g.dma_start(out=cs_in[:, :], in_=sSend[:]))
            g.wait_ge(gd.sem, gd.val)
            cc.inc(g.collective_compute(
                "AllReduce", ALU.add, replica_groups=groups,
                ins=[cs_in.ap().opt()], outs=[cs_out.ap().opt()]))
            g.wait_ge(cc.sem, cc.val)
            gd.inc(g.dma_start(out=sXsum[:, :], in_=cs_out[:, :]))
            g.wait_ge(gd.sem, gd.val)
        G(g70)
        GD_X = gd.val

        def v71(v):
            v.wait_ge(gd.sem, GD_X)
            v.tensor_sub(sXsum[:], sXsum[:], sSend[:])
            DR_V and v.drain()
            v.tensor_scalar(sXsum[:], sXsum[:], sIsr[:, 0:1], None, ALU.mult)
            DR_V and v.drain()
            vC.inc(v.engine_nop())
        V(v71)
        barrier()

        chunk_gate = None
        gate_hist = [None, None]
        vd2_hist = [0]
        tv72_hist = [0]
        g72_hist = [0]
        for cg in range(NC):
            b = cg // CPB
            tb = b * BLK
            slot = cg % 2
            VO = slot * FC * 128
            N0b = N0stg if slot == 0 else Kfm
            NO = 0
            XO = 0
            gg2 = gate_hist[-2]

            def gql(g, cg=cg, gg=gg2, VO=VO, NO=NO, N0b=N0b):
                wait_gate(g, gg)
                g.wait_ge(gd.sem, gd.val)
                gd.inc(g.dma_start(
                    out=Vfm[:, VO:VO + FC * 128],
                    in_=Qhbm[:, 0:FC * TLOC].rearrange("p (f t) -> p f t", f=FC)[
                        :, :, cg * 128:(cg + 1) * 128]))
                gd.inc(g.dma_start(
                    out=N0b[:, NO:NO + HP * 128],
                    in_=N0hbm[:, 0:HP * TLOC].rearrange("p (q t) -> p q t", q=HP)[
                        :, :, cg * 128:(cg + 1) * 128]))
                g.wait_ge(gd.sem, gd.val)
            G(gql)
            GQL = gd.val

            tv72p = tv72_hist[-1]

            def v72(v, cg=cg, GQL=GQL, XO=XO, tv72p=tv72p):
                if tv72p > 0:
                    v.wait_ge(tC.sem, tv72p)
                v.wait_ge(gd.sem, GQL)
                v.tensor_tensor(
                    sXcC[:, XO:XO + HP * SL].rearrange("p (q e o) -> p q e o", e=SL, o=1),
                    sXsum[:, 0:HP * SL].rearrange("p (q e o) -> p q e o", e=SL, o=1),
                    cpxrep[:, 0:HP * NC].rearrange("p (q c) -> p q c", c=NC)[:, :, cg:cg + 1]
                    .rearrange("p q (c o) -> p q c o", o=1).to_broadcast([128, HP, SL, 1]),
                    ALU.mult)
                DR_V and v.drain()
                vC.inc(v.engine_nop())
            V(v72)
            VXC = vC.val
            vd2p = vd2_hist[-1]

            def t72(t, cg=cg, VXC=VXC, VO=VO, XO=XO, vd2p=vd2p):
                t.wait_ge(vC.sem, VXC)
                if vd2p > 0:
                    t.wait_ge(vC.sem, vd2p)
                for h in list(range(0, H, 2)) + list(range(1, H, 2)):
                    p = h // 2
                    rb = (h % 2) * 64
                    fc = h // 2
                    comb = PF[2 + (p % 2)]
                    ccol = (p // 2) * 128
                    qap = Vfm[rb:rb + 64, VO + fc * 128:VO + (fc + 1) * 128]
                    mmn(t, rb)
                    t.matmul(comb[rb:rb + 64, ccol:ccol + 128],
                             sXcC[rb:rb + 64, XO + p * SL:XO + p * SL + 64],
                             qap, start=True, stop=True)
                    dbk = PX[h // 8]
                    drow = 32 * ((h % 8) // 4)
                    dcol = (h % 4) * 128
                    t.matmul(dbk[drow:drow + 1, dcol:dcol + 128],
                             sXcC[rb:rb + 64, XO + p * SL + 64:XO + p * SL + 65],
                             qap, start=True, stop=True)
                t.drain()
                tC.inc(t.nop())
            T(t72)
            TV72 = tC.val
            tv72_hist.append(TV72)
            g72p = g72_hist[-1]

            def v73(v, cg=cg, TV72=TV72, NO=NO, g72p=g72p, N0b=N0b):
                v.wait_ge(tC.sem, TV72)
                if g72p > 0:
                    v.wait_ge(gd.sem, g72p)
                for bk in range(2):
                    nview = N0b[:, NO:NO + HP * 128].rearrange("p (f t) -> p f t", f=HP)[
                        :, bk::2, :]
                    v.tensor_tensor(nview, PF[2 + bk][:, 0:PPB * 128].rearrange("p (q l) -> p q l", q=PPB),
                                    nview, ALU.add)
                    DR_V and v.drain()
                for bk in range(NBDEN):
                    v.tensor_copy(dslab[:, bk * 512:bk * 512 + 512], PX[bk][:, 0:512])
                    DR_V and v.drain()
                vC.inc(v.engine_nop())
            V(v73)
            VD2 = vC.val
            vd2_hist.append(VD2)

            def g72(g, cg=cg, VD2=VD2, NO=NO, N0b=N0b):
                g.wait_ge(vC.sem, VD2)
                g.wait_ge(gd.sem, gd.val)
                gd.inc(g.dma_start(
                    out=N0hbm[:, 0:HP * TLOC].rearrange("p (q t) -> p q t", q=HP)[
                        :, :, cg * 128:(cg + 1) * 128],
                    in_=N0b[:, NO:NO + HP * 128].rearrange("p (q t) -> p q t", q=HP)))
                for half in range(NBDEN):
                    nh = min(8, H - half * 8)
                    gd.inc(g.dma_start(
                        out=denXr[8 * half:8 * half + nh, cg * 128:(cg + 1) * 128],
                        in_=dslab[:, half * 512:(half + 1) * 512]
                        .rearrange("p (q l) -> p q l", q=4)[0:64:32, :, :][0:(nh + 3) // 4, 0:min(4, nh), :]))
                g.wait_ge(gd.sem, gd.val)
            G(g72)
            gdv = gd.val
            g72_hist.append(gdv)

            def vw(v, gdv=gdv):
                v.wait_ge(gd.sem, gdv)
                vC.inc(v.engine_nop())
            V(vw)
            chunk_gate = (vC, vC.val)
            gate_hist.append(chunk_gate)

        def v74(v):
            v.tensor_add(dfin[:, 0:TLOC], den0[:, 0:TLOC], denXr[:, 0:TLOC])
            DR_V and v.drain()
            v.tensor_tensor(dfin[:, 0:TLOC], dfin[:, 0:TLOC], gM[:, 0:TLOC], ALU.mult)
            DR_V and v.drain()
            vC.inc(v.engine_nop())
        V(v74)
        V74 = vC.val

        def s74(s):
            s.wait_ge(vC.sem, V74)
            s.activation(dfin[:, 0:TLOC], dfin[:, 0:TLOC], AF.Abs)
            DR_S and s.drain()
            sC.inc(s.nop())
        S(s74)
        S74 = sC.val

        def v75(v):
            v.wait_ge(sC.sem, S74)
            v.tensor_scalar_max(dfin[:, 0:TLOC], dfin[:, 0:TLOC], 1.0)
            DR_V and v.drain()
            v.reciprocal(dfin[:, 0:TLOC], dfin[:, 0:TLOC])
            DR_V and v.drain()
            v.tensor_tensor(Grow[:, 0:TLOC], gM[:, 0:TLOC], dfin[:, 0:TLOC], ALU.mult)
            DR_V and v.drain()
            vC.inc(v.engine_nop())
        V(v75)
        V75 = vC.val

        def g75(g):
            g.wait_ge(vC.sem, V75)
            gd.inc(g.dma_start(out=gsc[:, :], in_=Grow[:, 0:TLOC]))
            g.wait_ge(gd.sem, gd.val)
        G(g75)
        GD_G = gd.val

        hq_gate = None
        for p in range(FC):
            gprev = gd.val

            def g76(g, p=p):
                if p == 0:
                    g.wait_ge(gd.sem, GD_G)
                g.wait_ge(gd.sem, gd.val)
                gd.inc(g.dma_start(
                    out=Grep[:, 0:TLOC],
                    in_=gsc[2 * p:2 * p + 2, :].rearrange("h (o w) -> h o w", o=1)
                    .to_broadcast([2, 64, TLOC])))
                gd.inc(g.dma_start(out=Kfm[:, 0:TLOC],
                                   in_=Ohbm[:, p * TLOC:(p + 1) * TLOC]))
                gd.inc(g.dma_start(out=Kfm[:, TLOC:2 * TLOC],
                                   in_=N0hbm[:, p * TLOC:(p + 1) * TLOC]))
            def g76w(g, hq=hq_gate):
                wait_gate(g, hq)
            G(g76w)
            G(g76)
            GDS = gd.val

            def v76(v, p=p, GDS=GDS):
                v.wait_ge(gd.sem, GDS)
                v.tensor_tensor(fscr2[:, 0:TLOC], Kfm[:, 0:TLOC],
                                Grep[:, 0:TLOC], ALU.mult)
                DR_V and v.drain()
                v.tensor_tensor(fscr2[:, 0:TLOC], fscr2[:, 0:TLOC],
                                Kfm[:, TLOC:2 * TLOC], ALU.mult)
                DR_V and v.drain()
                vC.inc(v.engine_nop())
            V(v76)
            hq_gate = (vC, vC.val)
            VH = vC.val
            tv0, vv0 = tC.val, vC.val

            def t76(t, p=p, VH=VH, tv0=tv0, vv0=vv0):
                t.wait_ge(vC.sem, VH)
                for j in range(NT):
                    if j >= 2:
                        t.wait_ge(vC.sem, vv0 + j - 1)
                    else:
                        t.wait_ge(vC.sem, vv0)
                    tC.inc(t.transpose(PB[j % 2][:, 0:128], fscr2[:, j * 128:(j + 1) * 128], sId[:]))
            T(t76)

            def v77(v, p=p, tv0=tv0):
                for j in range(NT):
                    v.wait_ge(tC.sem, tv0 + j + 1)
                    vC.inc(v.tensor_copy(
                        g1[:, j * DM + p * 128:j * DM + p * 128 + 128],
                        PB[j % 2][:, 0:128]))
            V(v77)
            hq_gate = (vC, vC.val)
        barrier()

        if KLIM < 6:
            def tail(s):
                for j in range(NT):
                    st.inc(s.dma_start(out=y_out[j * 128:(j + 1) * 128, :],
                                       in_=fscr[:, 0:DM]))
                s.wait_ge(st.sem, st.val)
            Y(tail)

            def tailg(g):
                g.wait_ge(st.sem, st.val)
            G(tailg)
            return nc
        NG8 = NT // 4
        LA = 6 * NT + 8
        LB = 6 * NT + 24
        fscr2_free = [None]

        def ln8(src_view, base, writer):
            for g in range(NG8):
                g0 = g * 4

                def vsumA(v, g0=g0, base=base):
                    v.reduce_sum(sMu[:, base:base + 4],
                                 src_view(g0, 4).rearrange("p (t d) -> p t d", t=4),
                                 axis=AX.X)
                    DR_V and v.drain()
                    v.tensor_scalar_mul(sMu[:, base:base + 4], sMu[:, base:base + 4], 1.0 / DM)
                    DR_V and v.drain()
                    vC.inc(v.engine_nop())
                V(vsumA)
                VMU = vC.val
                ffree = fscr2_free[0]

                def ssqA(s, g0=g0, VMU=VMU, ffree=ffree):
                    s.wait_ge(vC.sem, VMU)
                    if ffree is not None:
                        s.wait_ge(ffree[0].sem, ffree[1])
                    s.activation(fscr2[:, 0:4 * DM], src_view(g0, 4), AF.Square)
                    DR_S and s.drain()
                    sC.inc(s.nop())
                S(ssqA)
                SSQ = sC.val

                def vvarA(v, SSQ=SSQ, base=base):
                    v.wait_ge(sC.sem, SSQ)
                    mu4 = sMu[:, base:base + 4]
                    var4 = sMu[:, base + 4:base + 8]
                    nb4 = sMu[:, base + 12:base + 16]
                    v.reduce_sum(var4, fscr2[:, 0:4 * DM].rearrange(
                        "p (t d) -> p t d", t=4), axis=AX.X)
                    DR_V and v.drain()
                    v.tensor_scalar_mul(var4, var4, 1.0 / DM)
                    DR_V and v.drain()
                    v.tensor_tensor(nb4, mu4, mu4, ALU.mult)
                    DR_V and v.drain()
                    v.tensor_sub(var4, var4, nb4)
                    DR_V and v.drain()
                    v.tensor_scalar_add(var4, var4, LN_EPS)
                    DR_V and v.drain()
                    vC.inc(v.engine_nop())
                V(vvarA)
                VV = vC.val

                def srsA(s, VV=VV, base=base):
                    s.wait_ge(vC.sem, VV)
                    var4 = sMu[:, base + 4:base + 8]
                    rs4 = sMu[:, base + 8:base + 12]
                    s.activation(rs4, var4, AF.Ln)
                    DR_S and s.drain()
                    s.activation(rs4, rs4, AF.Exp, scale=-0.5)
                    DR_S and s.drain()
                    sC.inc(s.nop())
                S(srsA)
                SR = sC.val

                def vbiasA(v, SR=SR, base=base):
                    v.wait_ge(sC.sem, SR)
                    mu4 = sMu[:, base:base + 4]
                    rs4 = sMu[:, base + 8:base + 12]
                    nb4 = sMu[:, base + 12:base + 16]
                    v.tensor_tensor(nb4, mu4, rs4, ALU.mult)
                    DR_V and v.drain()
                    v.tensor_scalar_mul(nb4, nb4, -1.0)
                    DR_V and v.drain()
                    vC.inc(v.engine_nop())
                V(vbiasA)
                VB = vC.val

                def sapA(s, g0=g0, VB=VB, base=base):
                    s.wait_ge(vC.sem, VB)
                    for j in range(4):
                        s.activation(fscr2[:, j * DM:(j + 1) * DM],
                                     src_view(g0 + j, 1), AF.Identity,
                                     bias=sMu[:, base + 12 + j:base + 13 + j],
                                     scale=sMu[:, base + 8 + j:base + 9 + j])
                    s.drain()
                    sC.inc(s.nop())
                S(sapA)
                SA = sC.val
                writer(g0, SA)
                fscr2_free[0] = (vC, vC.val)

        def g1view(j0, n):
            return g1[:, j0 * DM:(j0 + n) * DM]

        def lna_write(g0, SA):
            def vw(v, g0=g0, SA=SA):
                v.wait_ge(sC.sem, SA)
                fview = fscr2[:, 0:4 * DM].rearrange("p (t d) -> p t d", t=4)
                gv = sGam[:, 0:DM].rearrange("p (o d) -> p o d", o=1).to_broadcast([128, 4, DM])
                bv = sBet[:, 0:DM].rearrange("p (o d) -> p o d", o=1).to_broadcast([128, 4, DM])
                v.tensor_tensor(fview, fview, gv, ALU.mult)
                DR_V and v.drain()
                v.tensor_tensor(g1[:, g0 * DM:(g0 + 4) * DM].rearrange(
                    "p (t d) -> p t d", t=4), fview, bv, ALU.add)
                DR_V and v.drain()
                vC.inc(v.engine_nop())
            V(vw)
        ln8(g1view, LA, lna_write)

        def lnb_write(g0, SA):
            def vw(v, g0=g0, SA=SA):
                v.wait_ge(sC.sem, SA)
                xv = xres[:, g0 * DM:(g0 + 4) * DM]
                v.tensor_tensor(xv, xv, fscr2[:, 0:4 * DM], ALU.add)
                DR_V and v.drain()
                vC.inc(v.engine_nop())
            V(vw)
        ln8(g1view, LB, lnb_write)
        LNBEND = vC.val

        if KLIM < 7:
            def tail(s):
                for j in range(NT):
                    st.inc(s.dma_start(out=y_out[j * 128:(j + 1) * 128, :],
                                       in_=fscr[:, 0:DM]))
                s.wait_ge(st.sem, st.val)
            Y(tail)

            def tailg(g):
                g.wait_ge(st.sem, st.val)
            G(tailg)
            return nc
        KPTf = KPT[:].bitcast(F32)
        f2bufs = [f2win, Qwin]
        L2 = 6 * NT + 8
        L3 = 6 * NT + 24
        pf_gate = [None, None, None, None]
        w1_gate = [None, None]
        f2_gate = [None, None]
        t90_prev = tC.val
        st_hist = [0, 0]
        ln2_sap = {}

        def emit_ln2(fb):
            jt0 = (fb * FBLK) // 128
            TPB = FBLK // 128

            def ln2_src(j, jt0=jt0):
                return xres[:, (jt0 + j) * DM:(jt0 + j + 1) * DM]

            def vsum2(v, jt0=jt0):
                v.reduce_sum(sMu[:, L2:L2 + TPB],
                             xres[:, jt0 * DM:(jt0 + TPB) * DM].rearrange(
                                 "p (t d) -> p t d", t=TPB), axis=AX.X)
                DR_V and v.drain()
                v.tensor_scalar_mul(sMu[:, L2:L2 + TPB], sMu[:, L2:L2 + TPB], 1.0 / DM)
                DR_V and v.drain()
                vC.inc(v.engine_nop())
            V(vsum2)
            VMU = vC.val

            def ssq2(s, jt0=jt0, tp=t90_prev, VMU=VMU, fb=fb):
                if fb == 0:
                    s.wait_ge(vC.sem, LNBEND)
                s.wait_ge(vC.sem, VMU)
                s.wait_ge(tC.sem, tp)
                s.activation(fscr2[:, 0:TPB * DM], xres[:, jt0 * DM:(jt0 + TPB) * DM],
                             AF.Square)
                DR_S and s.drain()
                sC.inc(s.nop())
            S(ssq2)
            SSQ = sC.val

            def vvar2(v, SSQ=SSQ):
                v.wait_ge(sC.sem, SSQ)
                mu4 = sMu[:, L2:L2 + TPB]
                var4 = sMu[:, L2 + 4:L2 + 4 + TPB]
                nb4 = sMu[:, L2 + 12:L2 + 12 + TPB]
                v.reduce_sum(var4, fscr2[:, 0:TPB * DM].rearrange(
                    "p (t d) -> p t d", t=TPB), axis=AX.X)
                DR_V and v.drain()
                v.tensor_scalar_mul(var4, var4, 1.0 / DM)
                DR_V and v.drain()
                v.tensor_tensor(nb4, mu4, mu4, ALU.mult)
                DR_V and v.drain()
                v.tensor_sub(var4, var4, nb4)
                DR_V and v.drain()
                v.tensor_scalar_add(var4, var4, LN_EPS)
                DR_V and v.drain()
                vC.inc(v.engine_nop())
            V(vvar2)
            VV = vC.val

            def srs2(s, VV=VV):
                s.wait_ge(vC.sem, VV)
                var4 = sMu[:, L2 + 4:L2 + 4 + TPB]
                rs4 = sMu[:, L2 + 8:L2 + 8 + TPB]
                s.activation(rs4, var4, AF.Ln)
                DR_S and s.drain()
                s.activation(rs4, rs4, AF.Exp, scale=-0.5)
                DR_S and s.drain()
                sC.inc(s.nop())
            S(srs2)
            SR = sC.val

            def vbias2(v, SR=SR):
                v.wait_ge(sC.sem, SR)
                mu4 = sMu[:, L2:L2 + TPB]
                rs4 = sMu[:, L2 + 8:L2 + 8 + TPB]
                nb4 = sMu[:, L2 + 12:L2 + 12 + TPB]
                v.tensor_tensor(nb4, mu4, rs4, ALU.mult)
                DR_V and v.drain()
                v.tensor_scalar_mul(nb4, nb4, -1.0)
                DR_V and v.drain()
                vC.inc(v.engine_nop())
            V(vbias2)
            VB = vC.val

            def sap2(s, jt0=jt0, VB=VB):
                s.wait_ge(vC.sem, VB)
                for j in range(TPB):
                    s.activation(fscr2[:, j * DM:(j + 1) * DM], ln2_src(j),
                                 AF.Identity,
                                 bias=sMu[:, L2 + 12 + j:L2 + 13 + j],
                                 scale=sMu[:, L2 + 8 + j:L2 + 9 + j])
                s.drain()
                sC.inc(s.nop())
            S(sap2)
            ln2_sap[fb] = sC.val

        emit_ln2(0)
        for fb in range(NFB):
            t0 = fb * FBLK
            jt0 = t0 // 128
            TPB = FBLK // 128
            SAP2 = ln2_sap[fb]

            n = TPB * FC
            tv0, vv0 = tC.val, vC.val

            def t90(t, n=n, tv0=tv0, vv0=vv0, SAP2=SAP2):
                t.wait_ge(sC.sem, SAP2)
                for i in range(n):
                    jj, fc = divmod(i, FC)
                    if i >= 2:
                        t.wait_ge(vC.sem, vv0 + i - 1)
                    else:
                        t.wait_ge(vC.sem, vv0)
                    tC.inc(t.transpose(PB[i % 2][:, 0:128],
                                fscr2[:, jj * DM + fc * 128:jj * DM + fc * 128 + 128], sId[:]))
            T(t90)
            t90_prev = tC.val

            def v90(v, n=n, tv0=tv0):
                for i in range(n):
                    jj, fc = divmod(i, FC)
                    v.wait_ge(tC.sem, tv0 + i + 1)
                    vC.inc(v.tensor_copy(
                        x2T[:, fc * FBLK + jj * 128:fc * FBLK + jj * 128 + 128],
                        PB[i % 2][:, 0:128]))
            V(v90)
            VX2T = vC.val
            if fb + 1 < NFB:
                emit_ln2(fb + 1)

            for hcx in range(HC):
                slot = hcx % 2
                bank = hcx % 4
                sg = w1_gate[slot]

                def yf1(s, hcx=hcx, slot=slot, sg=sg, fb=fb):
                    if fb == 0 and hcx < 2:
                        s.wait_ge(gd.sem, GD_PREP)
                    wait_gate(s, sg)
                    ldw.inc(s.dma_start(
                        out=Wwin[:, slot * FC * 128:(slot + 1) * FC * 128].rearrange(
                            "p (c n) -> p c n", c=FC),
                        in_=s1sc[0:DM, hcx * 128:(hcx + 1) * 128].rearrange(
                            "(c p) n -> p c n", p=128)))
                    s.wait_ge(ldw.sem, ldw.val)
                Y(yf1)
                LV = ldw.val
                bg = pf_gate[bank]

                def tf1(t, hcx=hcx, bank=bank, LV=LV, bg=bg, slot=slot, VX2T=VX2T):
                    t.wait_ge(ldw.sem, LV)
                    t.wait_ge(vC.sem, VX2T)
                    wait_gate(t, bg)
                    for fc in range(FC):
                        t.matmul(PF[bank][:, 0:FBLK],
                                 Wwin[:, slot * FC * 128 + fc * 128:slot * FC * 128 + fc * 128 + 128],
                                 x2T[:, fc * FBLK:fc * FBLK + FBLK],
                                 start=(fc == 0), stop=(fc == FC - 1))
                    t.drain()
                    tC.inc(t.nop())
                T(tf1)
                TV = tC.val
                w1_gate[slot] = (tC, TV)

                def sf1(s, hcx=hcx, bank=bank, TV=TV, fb=fb):
                    if fb == 0 and hcx < 2:
                        s.wait_ge(vC.sem, VPREP)
                    s.wait_ge(tC.sem, TV)
                    s.activation(g1[:, hcx * FBLK:(hcx + 1) * FBLK],
                                 PF[bank][:, 0:FBLK],
                                 AF.Gelu, bias=sB1[:, hcx:hcx + 1], scale=sA1[:, hcx:hcx + 1])
                    s.drain()
                    sC.inc(s.nop())
                S(sf1)
                pf_gate[bank] = (sC, sC.val)
            GF1 = sC.val

            for e in range(FC):
                sl2 = e % 2
                bank = e % 4
                buf = f2bufs[sl2]
                sg = f2_gate[sl2]

                def yf2(s, e=e, buf=buf, sg=sg, fb=fb):
                    if fb == 0 and e < 2:
                        s.wait_ge(gd.sem, GD_PREP)
                    wait_gate(s, sg)
                    ldw.inc(s.dma_start(
                        out=buf[:, 0:HC * 128].rearrange("p (c n) -> p c n", c=HC),
                        in_=s2sc[0:DF, e * 128:(e + 1) * 128].rearrange(
                            "(c p) n -> p c n", p=128)))
                    s.wait_ge(ldw.sem, ldw.val)
                Y(yf2)
                LV = ldw.val
                bg = pf_gate[bank]

                def tf2(t, e=e, buf=buf, bank=bank, LV=LV, bg=bg, GF1=GF1):
                    t.wait_ge(ldw.sem, LV)
                    t.wait_ge(sC.sem, GF1)
                    wait_gate(t, bg)
                    for hcx in range(HC):
                        t.matmul(PF[bank][:, 0:FBLK],
                                 buf[:, hcx * 128:hcx * 128 + 128],
                                 g1[:, hcx * FBLK:hcx * FBLK + FBLK],
                                 start=(hcx == 0), stop=(hcx == HC - 1))
                    t.drain()
                    tC.inc(t.nop())
                T(tf2)
                TV = tC.val
                f2_gate[sl2] = (tC, TV)

                def sf2(s, e=e, bank=bank, TV=TV, fb=fb):
                    if fb == 0 and e < 2:
                        s.wait_ge(vC.sem, VPREP)
                    s.wait_ge(tC.sem, TV)
                    s.activation(ffnO[:, e * FBLK:(e + 1) * FBLK],
                                 PF[bank][:, 0:FBLK],
                                 AF.Identity, bias=sB2[:, e:e + 1], scale=sA2[:, e:e + 1])
                    s.drain()
                    sC.inc(s.nop())
                S(sf2)
                pf_gate[bank] = (sC, sC.val)
            GF2 = sC.val

            n = TPB * FC
            tv0, vv0 = tC.val, vC.val

            def t91(t, n=n, tv0=tv0, vv0=vv0, GF2=GF2):
                t.wait_ge(sC.sem, GF2)
                for i in range(n):
                    jj, fc = divmod(i, FC)
                    if i >= 2:
                        t.wait_ge(vC.sem, vv0 + i - 1)
                    else:
                        t.wait_ge(vC.sem, vv0)
                    tC.inc(t.transpose(PB[i % 2][:, 0:128],
                                ffnO[:, fc * FBLK + jj * 128:fc * FBLK + jj * 128 + 128], sId[:]))
            T(t91)

            def v91(v, n=n, tv0=tv0):
                for i in range(n):
                    jj, fc = divmod(i, FC)
                    v.wait_ge(tC.sem, tv0 + i + 1)
                    vC.inc(v.tensor_copy(
                        g1[:, jj * DM + fc * 128:jj * DM + fc * 128 + 128],
                        PB[i % 2][:, 0:128]))
            V(v91)
            V91 = vC.val

            def vsum3(v, V91=V91):
                v.wait_ge(vC.sem, V91)
                v.reduce_sum(sMu[:, L3:L3 + TPB],
                             g1[:, 0:TPB * DM].rearrange("p (t d) -> p t d", t=TPB),
                             axis=AX.X)
                DR_V and v.drain()
                v.tensor_scalar_mul(sMu[:, L3:L3 + TPB], sMu[:, L3:L3 + TPB], 1.0 / DM)
                DR_V and v.drain()
                vC.inc(v.engine_nop())
            V(vsum3)
            VMU3 = vC.val

            HTP = TPB // 2
            SSQH = []
            for hh in range(2):
                def ssq3(s, VMU3=VMU3, hh=hh):
                    if hh == 0:
                        s.wait_ge(vC.sem, VMU3)
                    else:
                        s.wait_ge(vC.sem, vC.val)
                    s.activation(fscr[:, 0:HTP * DM],
                                 g1[:, hh * HTP * DM:(hh + 1) * HTP * DM], AF.Square)
                    DR_S and s.drain()
                    sC.inc(s.nop())
                S(ssq3)
                SSQ3 = sC.val
                SSQH.append(SSQ3)

                def vred3(v, SSQ3=SSQ3, hh=hh):
                    v.wait_ge(sC.sem, SSQ3)
                    v.reduce_sum(sMu[:, L3 + 4 + hh * HTP:L3 + 4 + (hh + 1) * HTP],
                                 fscr[:, 0:HTP * DM].rearrange(
                                     "p (t d) -> p t d", t=HTP), axis=AX.X)
                    DR_V and v.drain()
                    vC.inc(v.engine_nop())
                V(vred3)

            def vvar3(v):
                mu4 = sMu[:, L3:L3 + TPB]
                var4 = sMu[:, L3 + 4:L3 + 4 + TPB]
                nb4 = sMu[:, L3 + 12:L3 + 12 + TPB]
                v.tensor_scalar_mul(var4, var4, 1.0 / DM)
                DR_V and v.drain()
                v.tensor_tensor(nb4, mu4, mu4, ALU.mult)
                DR_V and v.drain()
                v.tensor_sub(var4, var4, nb4)
                DR_V and v.drain()
                v.tensor_scalar_add(var4, var4, LN_EPS)
                DR_V and v.drain()
                vC.inc(v.engine_nop())
            V(vvar3)
            VV3 = vC.val

            def srs3(s, VV3=VV3):
                s.wait_ge(vC.sem, VV3)
                var4 = sMu[:, L3 + 4:L3 + 4 + TPB]
                rs4 = sMu[:, L3 + 8:L3 + 8 + TPB]
                s.activation(rs4, var4, AF.Ln)
                DR_S and s.drain()
                s.activation(rs4, rs4, AF.Exp, scale=-0.5)
                DR_S and s.drain()
                sC.inc(s.nop())
            S(srs3)
            SR3 = sC.val

            def vbias3(v, SR3=SR3):
                v.wait_ge(sC.sem, SR3)
                mu4 = sMu[:, L3:L3 + TPB]
                rs4 = sMu[:, L3 + 8:L3 + 8 + TPB]
                nb4 = sMu[:, L3 + 12:L3 + 12 + TPB]
                v.tensor_tensor(nb4, mu4, rs4, ALU.mult)
                DR_V and v.drain()
                v.tensor_scalar_mul(nb4, nb4, -1.0)
                DR_V and v.drain()
                vC.inc(v.engine_nop())
            V(vbias3)
            VB3 = vC.val

            for j in range(TPB):
                jglob = jt0 + j
                c = j % 2
                stv2 = st_hist[-2]

                def sap3(s, j=j, c=c, VB3=VB3, stv2=stv2):
                    s.wait_ge(vC.sem, VB3)
                    if stv2 > 0:
                        s.wait_ge(st.sem, stv2)
                    s.activation(KPTf[:, c * DM:(c + 1) * DM], g1[:, j * DM:(j + 1) * DM],
                                 AF.Identity,
                                 bias=sMu[:, L3 + 12 + j:L3 + 13 + j],
                                 scale=sMu[:, L3 + 8 + j:L3 + 9 + j])
                    s.drain()
                    sC.inc(s.nop())
                S(sap3)
                SA = sC.val

                def vadd3(v, jglob=jglob, c=c, SA=SA):
                    v.wait_ge(sC.sem, SA)
                    vC.inc(v.tensor_tensor(KPTf[:, c * DM:(c + 1) * DM],
                                           xres[:, jglob * DM:(jglob + 1) * DM],
                                           KPTf[:, c * DM:(c + 1) * DM], ALU.add))
                V(vadd3)
                VF = vC.val

                def ystore(s, jglob=jglob, c=c, VF=VF):
                    s.wait_ge(vC.sem, VF)
                    st.inc(s.dma_start(out=y_out[jglob * 128:(jglob + 1) * 128, :],
                                       in_=KPTf[:, c * DM:(c + 1) * DM]))
                    s.wait_ge(st.sem, st.val)
                Y(ystore)
                st_hist.append(st.val)

        def yfin(s):
            s.wait_ge(st.sem, st.val)
        Y(yfin)

        def gfin(g):
            g.wait_ge(st.sem, st.val)
        G(gfin)
    return nc


def kernel(**inputs):
    cfg = Cfg()
    x = np.ascontiguousarray(np.asarray(inputs["L0_attn_x"], np.float32))
    B, T, DM = x.shape
    TLOC = T // 2
    assert TLOC == cfg.TLOC and DM == cfg.DM

    Wq = np.ascontiguousarray(np.asarray(inputs["L0_attn_W_q"], np.float32))
    Wk = np.ascontiguousarray(np.asarray(inputs["L0_attn_W_k"], np.float32))
    Wv = np.ascontiguousarray(np.asarray(inputs["L0_attn_W_v"], np.float32))
    Wo = np.ascontiguousarray(np.asarray(inputs["L0_attn_W_o"], np.float32))
    w_i = np.asarray(inputs["L0_attn_w_i"], np.float32)
    w_f = np.asarray(inputs["L0_attn_w_f"], np.float32)
    b_i = np.asarray(inputs["L0_attn_b_i"], np.float32)
    b_f = np.asarray(inputs["L0_attn_b_f"], np.float32)
    b_o = np.asarray(inputs["L0_attn_b_o"], np.float32)
    gamma = np.asarray(inputs["L0_attn_gamma"], np.float32)
    beta = np.asarray(inputs["L0_attn_beta"], np.float32)
    w1 = np.asarray(inputs["L0_ffn_w1"], np.float32)
    b1 = np.asarray(inputs["L0_ffn_b1"], np.float32)
    w2 = np.asarray(inputs["L0_ffn_w2"], np.float32)
    b2 = np.asarray(inputs["L0_ffn_b2"], np.float32)

    FC, HC = cfg.FC, cfg.HC
    Wif = np.ascontiguousarray(np.concatenate([w_i, w_f], axis=1))
    b_if = np.ascontiguousarray(np.stack([b_i, b_f], axis=1))
    b_o_c = np.ascontiguousarray(b_o.reshape(FC, 128).T)
    gamma_r = np.ascontiguousarray(np.tile(gamma[None, :], (128, 1)))
    beta_r = np.ascontiguousarray(np.tile(beta[None, :], (128, 1)))
    w1T = np.ascontiguousarray(w1.T)
    w2T = np.ascontiguousarray(w2.T)
    b1_c = np.ascontiguousarray(b1.reshape(HC, 128).T)
    b2_c = np.ascontiguousarray(b2.reshape(FC, 128).T)
    import ml_dtypes
    identity = np.eye(128, dtype=ml_dtypes.bfloat16)
    maskT = np.triu(np.ones((128, 128), np.float32)).astype(ml_dtypes.bfloat16)

    groups = [[2 * i, 2 * i + 1] for i in range(cfg.NCORE // 2)]
    nc = build_nc(cfg, groups)

    in_maps = []
    for c in range(cfg.NCORE):
        bidx, half = c // 2, c % 2
        xs = np.ascontiguousarray(x[bidx, half * TLOC:(half + 1) * TLOC, :])
        in_maps.append(dict(
            x=xs, Wq=Wq, Wk=Wk, Wv=Wv, Wo=Wo, Wif=Wif, b_if=b_if, b_o_c=b_o_c,
            gamma_r=gamma_r, beta_r=beta_r, w1T=w1T, w1h=w1, w2T=w2T, w2h=w2,
            b1_c=b1_c, b2_c=b2_c, ident=identity, maskT=maskT,
            isr=np.full((128, 1), float(half), np.float32),
        ))

    res = run_bass_kernel_spmd(nc, in_maps, core_ids=list(range(cfg.NCORE)))
    out = np.zeros((B, T, DM), np.float32)
    for c in range(cfg.NCORE):
        bidx, half = c // 2, c % 2
        out[bidx, half * TLOC:(half + 1) * TLOC, :] = res.results[c]["y"]
    return out
